# version 01
# baseline (speedup 1.0000x reference)
import numpy as np
from contextlib import ExitStack

import concourse.bass as bass
import concourse.mybir as mybir
from concourse.bass_utils import run_bass_kernel_spmd

F32 = mybir.dt.float32
BF16 = mybir.dt.bfloat16
AF = mybir.ActivationFunctionType
ALU = mybir.AluOpType
AX = mybir.AxisListType

NCORE = 8
B = 2
SPAN = 1024
TOK = B * SPAN
NT = TOK // 128
D = 1024
DC = D // 128
H = 8
HD = 128
CH = 64
NCH = SPAN // CH
E = 8
FF = 2048
DI = 2048
DS = 16
DTR = 64
I1 = 2816
LN_EPS = 1e-5
RMS_EPS = 1e-5


def _f(x):
    return np.ascontiguousarray(np.asarray(x), dtype=np.float32)


def host_stage(inp):
    x = _f(inp["L0_attn_hidden_states"])
    shared = {
        "wqT": _f(inp["L0_attn_Wq"].T), "wkT": _f(inp["L0_attn_Wk"].T),
        "wvT": _f(inp["L0_attn_Wv"].T), "woT": _f(inp["L0_attn_Wo"].T),
        "normw_b": np.ascontiguousarray(np.broadcast_to(_f(inp["L0_attn_norm_w"])[None, :], (128, 128))),
        "routerT": _f(inp["L0_ffn_router_w"].T),
        "router_b_b": np.ascontiguousarray(np.broadcast_to(_f(inp["L0_ffn_router_b"])[None, :], (128, 8))),
        "temp_col": np.ascontiguousarray(np.broadcast_to(_f(inp["L0_ffn_temperature"]).reshape(1, 1), (128, 1))),
        "wg": _f(inp["L0_ffn_wg"]), "wp": _f(inp["L0_ffn_wp"]), "wo_e": _f(inp["L0_ffn_wo"]),
        "c0T": _f(inp["L0_c"].T), "wmod0": _f(inp["L0_wmod"]),
        "c1T": _f(inp["L1_c"].T), "wmod1": _f(inp["L1_wmod"]),
        "inpT": _f(inp["L1_attn_in_proj_w"].T),
        "convw": _f(np.ascontiguousarray(
            inp["L1_attn_conv1d_w"].reshape(16, 128, 4).transpose(1, 2, 0).reshape(128, 64))),
        "convb": _f(np.ascontiguousarray(inp["L1_attn_conv1d_b"].reshape(16, 128).T)),
        "xprojT": _f(inp["L1_attn_x_proj_w"].T),
        "dtprojT": _f(inp["L1_attn_dt_proj_w"].T),
        "dtb": _f(np.ascontiguousarray(inp["L1_attn_dt_proj_b"].reshape(16, 128).T)),
        "alog": _f(inp["L1_attn_A_log"]),
        "dparam": _f(np.ascontiguousarray(inp["L1_attn_D_param"].reshape(16, 128).T)),
        "outpT": _f(inp["L1_attn_out_proj_w"].T),
        "wgate_n": _f(inp["L1_ffn_w_gate"]), "wup_n": _f(inp["L1_ffn_w_up"]),
        "wdown_n": _f(inp["L1_ffn_w_down"]),
    }
    maps = []
    for c in range(NCORE):
        m = dict(shared)
        xs = np.concatenate([x[0, c * SPAN:(c + 1) * SPAN], x[1, c * SPAN:(c + 1) * SPAN]], axis=0)
        m["x"] = np.ascontiguousarray(xs)
        mlt = np.zeros((128, 8), np.float32); mlt[:, :c] = 1.0
        meq = np.zeros((128, 8), np.float32)
        if c > 0:
            meq[:, c - 1] = 1.0
        m["mask_lt"] = mlt
        m["mask_eq"] = meq
        m8 = np.zeros((8, 1), np.float32); m8[:c, 0] = 1.0
        m["mask8"] = m8
        s48 = np.zeros((256, 32), np.float32)
        if c > 0:
            s48[(c - 1) * 32:(c - 1) * 32 + 32, :] = np.eye(32, dtype=np.float32)
        m["sel48"] = s48
        maps.append(m)
    return maps


def host_gather(results):
    out = np.empty((B, NCORE * SPAN, D), np.float32)
    for c, r in enumerate(results):
        y = r["y"]
        out[0, c * SPAN:(c + 1) * SPAN] = y[:SPAN]
        out[1, c * SPAN:(c + 1) * SPAN] = y[SPAN:]
    return out


class Arena:

    def __init__(self, t, nf32):
        self.t = t
        self.n = nf32
        self.off = 0
        self.marks = []
        self.peak = 0

    def push(self):
        self.marks.append(self.off)

    def pop(self):
        self.off = self.marks.pop()

    def f32(self, n, name=""):
        self.off = (self.off + 63) & ~63
        assert self.off + n <= self.n, f"arena overflow: {name} need {n} at {self.off}/{self.n}"
        ap = self.t[:, self.off:self.off + n]
        self.off += n
        self.peak = max(self.peak, self.off)
        return ap

    def bf16(self, n, name=""):
        nw = (n + 1) // 2
        return self.f32(nw, name).bitcast(BF16)


_SER_WAIT_ONLY = {"dma_start", "dma_start_transpose", "collective_compute", "indirect_dma_start"}
_SER_SKIP = {"wait_ge", "wait_op", "nop", "sem_inc", "halt", "drain", "br",
             "then_inc", "snap", "reg_mov", "register"}


class Ser:

    def __init__(self, eng, sem, cnt, key):
        self._e = eng
        self._sem = sem
        self._cnt = cnt
        self._key = key

    def __getattr__(self, name):
        attr = getattr(self._e, name)
        if not callable(attr) or name.startswith("_") or name in _SER_SKIP:
            return attr

        def wrapper(*a, **k):
            n = self._cnt[self._key]
            if n > 0:
                self._e.wait_ge(self._sem, n)
            r = attr(*a, **k)
            if name not in _SER_WAIT_ONLY and hasattr(r, "then_inc"):
                r.then_inc(self._sem, 1)
                self._cnt[self._key] = n + 1
            return r
        return wrapper


class MK:
    def __init__(self, debug_taps=()):
        self.nc = bass.Bass("TRN2", target_bir_lowering=False, debug=False)
        self.es = ExitStack()
        self.debug_taps = dict(debug_taps)
        self._build()

    def blk(self):
        return self.nc.Block()

    def ser(self, eng):
        cls = type(eng).__name__
        if "Vector" in cls:
            key = "selfv"
        elif "Scalar" in cls:
            key = "selfs"
        elif "GpSimd" in cls:
            key = "selfg"
        else:
            return eng
        return Ser(eng, self.sem[key], self.cnt, key)

    def chain(self, proxy, name, k=1):
        ins = proxy._e.wait_ge(proxy._sem, self.cnt[proxy._key])
        ins.then_inc(self.sem[name], k)
        self.cnt[name] += k

    def sbuf(self, name, shape, dtype):
        return self.es.enter_context(self.nc.sbuf_tensor(name, shape, dtype))

    _uid = 0

    def tmp_sbuf(self, stack, shape, dtype, name="tmp", side="left"):
        MK._uid += 1
        return stack.enter_context(self.nc.sbuf_tensor(f"tsb_{name}_{MK._uid}", shape, dtype,
                                                       side=side, align_bytes=256))[:]

    def bump(self, name, k=1):
        self.cnt[name] += k
        return self.cnt[name]

    def tap_bf16(self, tap, sb_flat, n):
        sem = self.sem
        nt = n // D
        with self.blk() as b:
            @b.gpsimd
            def _(g):
                g = self.ser(g)
                g.dma_start(out=tap.rearrange("(t p) d -> p t d", p=128),
                            in_=sb_flat.rearrange("p (t d) -> p t d", d=D)).then_inc(sem["dmag"], 16)
                self.bump("dmag", 16)
                g.wait_ge(sem["dmag"], self.cnt["dmag"])

    def tap_f32(self, tap, sb_ap):
        sem = self.sem
        with self.blk() as b:
            @b.sync
            def _(s):
                s = self.ser(s)
                s.dma_start(out=tap, in_=sb_ap).then_inc(sem["dmas"], 16)
                self.bump("dmas", 16)
                s.wait_ge(sem["dmas"], self.cnt["dmas"])

    def mm_pipeline(self, groups, banks, evac="vector", extra_tensor=None):
        nb = len(banks)
        t0 = self.cnt["td"]
        v0 = self.cnt["vd"]
        sem = self.sem
        with self.blk() as b:
            @b.tensor
            def _(t):
                t = self.ser(t)
                for i, (mme, _) in enumerate(groups):
                    if i >= nb:
                        t.wait_ge(sem["vd"], v0 + i - nb + 1)
                    last = mme(t, banks[i % nb])
                    last.then_inc(sem["td"], 1)
                if extra_tensor is not None:
                    extra_tensor(t)

            emitter = getattr(b, evac)

            @emitter
            def _(v):
                v = self.ser(v)
                for i, (_, eve) in enumerate(groups):
                    v.wait_ge(sem["td"], t0 + i + 1)
                    eve(v, banks[i % nb])
                    self.chain(v, "vd", 1)
        self.cnt["td"] = t0 + len(groups)

    def _build(self):
        nc, es = self.nc, self.es
        self._in_shapes = {}
        for name, shape in [
            ("x", [TOK, D]), ("wqT", [D, D]), ("wkT", [D, D]), ("wvT", [D, D]), ("woT", [D, D]),
            ("normw_b", [128, 128]), ("routerT", [D, 8]), ("router_b_b", [128, 8]), ("temp_col", [128, 1]),
            ("wg", [E, D, FF]), ("wp", [E, D, FF]), ("wo_e", [E, FF, D]),
            ("c0T", [D, 2]), ("wmod0", [D, 6 * D]), ("c1T", [D, 2]), ("wmod1", [D, 6 * D]),
            ("inpT", [D, 2 * DI]), ("convw", [128, 64]), ("convb", [128, 16]),
            ("xprojT", [DI, DTR + 2 * DS]), ("dtprojT", [DTR, DI]), ("dtb", [128, 16]),
            ("alog", [DI, DS]), ("dparam", [128, 16]), ("outpT", [DI, D]),
            ("wgate_n", [I1, D]), ("wup_n", [I1, D]), ("wdown_n", [D, I1]),
            ("mask_lt", [128, 8]), ("mask_eq", [128, 8]), ("mask8", [8, 1]), ("sel48", [256, 32]),
        ]:
            self._in_shapes[name] = shape

        class LazyIn:
            def __init__(s2):
                s2.created = {}

            def __getitem__(s2, name):
                if name not in s2.created:
                    s2.created[name] = nc.dram_tensor(name, self._in_shapes[name], F32,
                                                      kind="ExternalInput").ap()
                return s2.created[name]

        self.din = LazyIn()
        din = self.din

        self.y_out = nc.dram_tensor("y", [TOK, D], F32, kind="ExternalOutput").ap()
        self.taps = {}
        for tname, tshape in self.debug_taps.items():
            self.taps[tname] = nc.dram_tensor(tname, list(tshape), F32, kind="ExternalOutput").ap()

        self.spill = nc.dram_tensor("spill", [TOK, D], F32).ap()
        self.mod_d = nc.dram_tensor("mod_d", [2, 2 * 6 * D], F32).ap()
        self.acc_d = nc.dram_tensor("acc_d", [TOK, D], F32).ap()
        self.zspill = nc.dram_tensor("zspill", [B * (DI // 128), 128, SPAN], BF16).ap()
        self.bc_d = nc.dram_tensor("bc_d", [2 * DS, B * SPAN], F32).ap()
        self.wq_d = {w: nc.dram_tensor(f"w{w}q_d", [I1, D] if w != "down" else [D, I1], BF16).ap()
                     for w in ("gate", "up", "down")}
        self.cc_at_in = [nc.dram_tensor(f"cc_at_in{b}", [H * 128 + H, 128], F32).ap() for b in range(B)]
        self.cc_at_out = [nc.dram_tensor(f"cc_at_out{b}", [NCORE * (H * 128 + H), 128], F32).ap() for b in range(B)]
        self.ar_in = nc.dram_tensor("ar_in", [B, 8], F32).ap()
        self.ar_out = nc.dram_tensor("ar_out", [B, 8], F32).ap()
        self.cc_pq_in = nc.dram_tensor("cc_pq_in", [256, B * 16 * DS], F32).ap()
        self.cc_pq_out = nc.dram_tensor("cc_pq_out", [NCORE * 256, B * 16 * DS], F32).ap()
        self.halo_d = nc.dram_tensor("halo_d", [B * 16, 384], F32).ap()
        self.cc_halo_in = nc.dram_tensor("cc_halo_in", [B * 16, 384], F32).ap()
        self.cc_halo_out = nc.dram_tensor("cc_halo_out", [NCORE * B * 16, 384], F32).ap()
        self.y0_d = nc.dram_tensor("y0_d", [B * 16 * 128, 1024], BF16).ap()
        self.xq_d = nc.dram_tensor("xq_d", [128, NT * D], BF16).ap()

        self.x_sb = self.sbuf("x_sb", [128, NT * D], F32)
        arena_t = self.sbuf("arena", [128, 8192], F32)
        self.A = Arena(arena_t[:], 8192)
        self.ones1 = self.sbuf("ones1", [1, 128], BF16)
        self.ones1f = self.sbuf("ones1f", [1, 128], F32)
        self.ident = self.sbuf("ident", [128, 128], BF16)
        self.triu = self.sbuf("triu", [128, 128], BF16)
        self.tri64 = self.sbuf("tri64", [64, 64], F32)
        self.normw = self.sbuf("normw", [128, 128], F32)
        self.rb_b = self.sbuf("rb_b", [128, 8], F32)
        self.tcol = self.sbuf("tcol", [128, 1], F32)
        self.mlt = self.sbuf("mlt", [128, 8], F32)
        self.meq = self.sbuf("meq", [128, 8], F32)
        self.csil = self.sbuf("csil", [128, 2 * DC * 2], BF16)
        self.bc = self.sbuf("bcmod", [128, 4 * D], F32)
        self.eps_col = self.sbuf("eps_col", [128, 1], F32)
        self.epsr_col = self.sbuf("epsr_col", [128, 1], F32)
        self.onescol_bf = self.sbuf("onescol_bf", [128, 1], BF16)
        self.onescol_f = self.sbuf("onescol_f", [128, 1], F32)
        self.neg1_col = self.sbuf("neg1_col", [128, 1], F32)
        self.mask8_bf = self.sbuf("mask8_bf", [8, 1], BF16)

        self.sem = {n: es.enter_context(nc.semaphore(n))
                    for n in ["dmas", "dmag", "dmax", "td", "vd", "sd", "gd", "cc", "dma0", "dma1", "dmar", "selfv", "selfs", "selfg"]}
        self.cnt = {n: 0 for n in self.sem}

        self.psum = [es.enter_context(nc.psum_tensor(f"pb{i}", [128, 512], F32)) for i in range(8)]

        self.ph0_consts()
        self.ph0_mod()
        self.ph0_quant()
        self.layer0()
        self.layer1()
        self.finish()

        nc._es = es

    def ph0_consts(self):
        nc, A, sem, din = self.nc, self.A, self.sem, self.din
        with self.blk() as b:
            @b.sync
            def _(s):
                s = self.ser(s)
                for dst, src in [(self.x_sb[:].rearrange("p (t d) -> p t d", d=D),
                                  din["x"].rearrange("(t p) d -> p t d", p=128)),
                                 (self.normw[:], din["normw_b"]),
                                 (self.rb_b[:], din["router_b_b"]),
                                 (self.tcol[:], din["temp_col"]),
                                 (self.mlt[:], din["mask_lt"]),
                                 (self.meq[:], din["mask_eq"])]:
                    s.dma_start(out=dst, in_=src).then_inc(sem["dmas"], 16)
                    self.bump("dmas", 16)
                s.wait_ge(sem["dmas"], self.cnt["dmas"])

            @b.gpsimd
            def _(g):
                g = self.ser(g)
                g.memset(self.ones1[:], 1.0)
                g.memset(self.ones1f[:], 1.0)
                g.memset(self.eps_col[:], LN_EPS)
                g.memset(self.epsr_col[:], RMS_EPS)
                g.memset(self.onescol_bf[:], 1.0)
                g.memset(self.onescol_f[:], 1.0)
                g.memset(self.neg1_col[:], -1.0)
                g.dma_start(out=self.mask8_bf[:], in_=din["mask8"]).then_inc(sem["dmag"], 16)
                self.bump("dmag", 16)
                g.wait_ge(sem["dmag"], self.cnt["dmag"])

        A.push()
        it_r = A.f32(128, "it_r")
        it_c = A.f32(128, "it_c")
        with self.blk() as b:
            @b.gpsimd
            def _(g):
                g = self.ser(g)
                g.iota(it_r, [[1, 128]], channel_multiplier=0, allow_small_or_imprecise_dtypes=True)
                g.iota(it_c, [[0, 128]], channel_multiplier=1, allow_small_or_imprecise_dtypes=True)
        with self.blk() as b:
            @b.vector
            def _(v):
                v = self.ser(v)
                v.tensor_tensor(self.ident[:], it_c, it_r, ALU.is_equal)
                v.tensor_tensor(self.triu[:], it_c, it_r, ALU.is_le)
                v.tensor_tensor(self.tri64[:], it_c[0:64, 0:64], it_r[0:64, 0:64], ALU.is_le)
        A.pop()

    def ph0_mod(self):
        nc, A, sem, din = self.nc, self.A, self.sem, self.din
        A.push()
        c_f32 = A.f32(2 * DC * 2, "c_f32")
        with self.blk() as b:
            @b.sync
            def _(s):
                s = self.ser(s)
                s.dma_start(out=c_f32[:, 0:16].rearrange("p (c t) -> p c t", t=2),
                            in_=din["c0T"].rearrange("(c p) t -> p c t", p=128)).then_inc(sem["dmas"], 16)
                s.dma_start(out=c_f32[:, 16:32].rearrange("p (c t) -> p c t", t=2),
                            in_=din["c1T"].rearrange("(c p) t -> p c t", p=128)).then_inc(sem["dmas"], 16)
                self.bump("dmas", 32)
                s.wait_ge(sem["dmas"], self.cnt["dmas"])
        with self.blk() as b:
            @b.scalar
            def _(s):
                s = self.ser(s)
                s.activation(c_f32, c_f32, AF.Silu)
        with self.blk() as b:
            @b.vector
            def _(v):
                v = self.ser(v)
                v.tensor_copy(self.csil[:], c_f32)
        A.pop()

        from contextlib import ExitStack as _ES
        phms = _ES()
        self.mod_stage = self.tmp_sbuf(phms, [2, 1024], F32, "mod_stage")
        for l in range(2):
            wname = "wmod0" if l == 0 else "wmod1"
            for eh in range(2):
                A.push()
                phm = _ES()
                wmod_sb = self.tmp_sbuf(phm, [128, DC * 3 * D], BF16, "wmod")
                with self.blk() as b:
                    @b.gpsimd
                    def _(g):
                        g = self.ser(g)
                        g.dma_start(out=wmod_sb.rearrange("p (c e) -> p c e", c=DC),
                                    in_=din[wname][:, eh * 3072:(eh + 1) * 3072]
                                    .rearrange("(c p) e -> p c e", p=128)).then_inc(sem["dmag"], 16)
                        self.bump("dmag", 16)
                        g.wait_ge(sem["dmag"], self.cnt["dmag"])
                t0 = self.cnt["td"]
                v0 = self.cnt["vd"]
                d0 = self.cnt["dmas"]
                with self.blk() as b:
                    @b.tensor
                    def _(t):
                        for et in range(6):
                            if et >= 2:
                                t.wait_ge(sem["vd"], v0 + et - 1)
                            pb = self.psum[et % 2]
                            for dc in range(DC):
                                ins = t.matmul(pb[0:2, :],
                                               self.csil[:, (l * DC + dc) * 2:(l * DC + dc) * 2 + 2],
                                               wmod_sb[:, dc * 3072 + et * 512: dc * 3072 + (et + 1) * 512],
                                               start=(dc == 0), stop=(dc == DC - 1))
                            ins.then_inc(sem["td"], 1)

                    p0 = [self.cnt["dma0"], self.cnt["dma1"]]

                    @b.vector
                    def _(v):
                        v = self.ser(v)
                        for et in range(6):
                            v.wait_ge(sem["td"], t0 + et + 1)
                            if et >= 2:
                                v.wait_ge(sem["dma%d" % (et % 2)], p0[et % 2] + 16 * ((et - 2) // 2 + 1))
                            v.tensor_copy(self.mod_stage[0:2, (et % 2) * 512:(et % 2) * 512 + 512],
                                          self.psum[et % 2][0:2, :])
                            self.chain(v, "vd", 1)

                    @b.sync
                    def _(s):
                        for et in range(6):
                            s.wait_ge(sem["vd"], v0 + et + 1)
                            off = l * 6144 + eh * 3072 + et * 512
                            s.dma_start(out=self.mod_d[0:2, off: off + 512],
                                        in_=self.mod_stage[0:2, (et % 2) * 512:(et % 2) * 512 + 512]
                                        ).then_inc(sem["dma%d" % (et % 2)], 16)
                        s.wait_ge(sem["dma0"], p0[0] + 3 * 16)
                        s.wait_ge(sem["dma1"], p0[1] + 3 * 16)
                self.cnt["td"] = t0 + 6
                self.cnt["dma0"] += 3 * 16
                self.cnt["dma1"] += 3 * 16
                phm.close()
                A.pop()
        phms.close()

        if "t_mod" in self.taps:
            with self.blk() as b:
                @b.sync
                def _(s):
                    s = self.ser(s)
                    s.dma_start(out=self.taps["t_mod"], in_=self.mod_d).then_inc(sem["dmas"], 16)
                    self.bump("dmas", 16)
                    s.wait_ge(sem["dmas"], self.cnt["dmas"])

    def emit_bcast(self, l, specs):
        sem, A = self.sem, self.A
        A.push()
        sg = A.f32(512, "sg")
        sg2 = A.f32(512, "sg2")
        mod_row = A.f32(1024, "mrow")
        d0 = self.cnt["dmar"]
        t0 = self.cnt["td"]
        v0 = self.cnt["vd"]
        pieces = [(vec, slot, tf, bi) for vec, slot, tf in specs for bi in range(B)]
        with self.blk() as b:
            @b.sync
            def _(s):
                s = self.ser(s)
                for i, (vec, slot, tf, bi) in enumerate(pieces):
                    if i >= 1:
                        s.wait_ge(sem["td"], t0 + i)
                    s.dma_start(out=mod_row[0:1, :],
                                in_=self.mod_d[bi:bi + 1, l * 6144 + vec * D: l * 6144 + (vec + 1) * D]
                                ).then_inc(sem["dmar"], 16)

            @b.tensor
            def _(t):
                t = self.ser(t)
                for i, (vec, slot, tf, bi) in enumerate(pieces):
                    t.wait_ge(sem["dmar"], d0 + 16 * (i + 1))
                    if i >= 2:
                        t.wait_ge(sem["vd"], v0 + i - 1)
                    for half in range(2):
                        pb = self.psum[(2 * i + half) % 4]
                        ins = t.matmul(pb[:, :], self.ones1f[:],
                                 mod_row[0:1, half * 512:(half + 1) * 512], start=True, stop=True)
                    ins.then_inc(sem["td"], 1)

            @b.vector
            def _(v):
                v = self.ser(v)
                for i, (vec, slot, tf, bi) in enumerate(pieces):
                    v.wait_ge(sem["td"], t0 + i + 1)
                    dst = self.bc[:, (slot + bi) * D:(slot + bi + 1) * D]
                    for half in range(2):
                        pb = self.psum[(2 * i + half) % 4]
                        dh = dst[:, half * 512:(half + 1) * 512]
                        if tf == "plus1":
                            v.tensor_scalar_add(dh, pb[:, :], 1.0)
                        elif tf == "signmax":
                            v.tensor_scalar(sg2, pb[:, :], -1.0, None, ALU.mult)
                            v.tensor_tensor(dh, pb[:, :], sg2, ALU.max)
                            v.tensor_scalar_max(dh, dh, 0.25)
                            v.tensor_scalar(sg, pb[:, :], 0.0, 2.0, ALU.is_ge, ALU.mult)
                            v.tensor_scalar_add(sg, sg, -1.0)
                            v.tensor_tensor(dh, dh, sg, ALU.mult)
                        else:
                            v.tensor_copy(dh, pb[:, :])
                    self.chain(v, "vd", 1)
        self.cnt["dmar"] = d0 + 16 * len(pieces)
        self.cnt["td"] = t0 + len(pieces)
        A.pop()

    def emit_ln_mod(self, h_bf_flat, slot_scale, slot_shift, tiles=None, tap=None):
        nc, A, sem = self.nc, self.A, self.sem
        if tiles is None:
            tiles = list(range(NT))
        ntl = len(tiles)
        A.push()
        sum_c = A.f32(ntl, "sum_c")
        sq_c = A.f32(ntl, "sq_c")
        rstd_c = A.f32(ntl, "rstd_c")
        sqs = [A.f32(D, "sq_scratch0"), A.f32(D, "sq_scratch1")]
        xn = [A.f32(D, "xn0"), A.f32(D, "xn1")]
        with self.blk() as b:
            @b.vector
            def _(v):
                v = self.ser(v)
                for i, t in enumerate(tiles):
                    v.reduce_sum(sum_c[:, i:i + 1], self.x_sb[:, t * D:(t + 1) * D], AX.X)

            @b.scalar
            def _(s):
                s = self.ser(s)
                for i, t in enumerate(tiles):
                    s.activation(sqs[i % 2], self.x_sb[:, t * D:(t + 1) * D], AF.Square,
                                 accum_out=sq_c[:, i:i + 1])
        with self.blk() as b:
            @b.vector
            def _(v):
                v = self.ser(v)
                v.tensor_scalar_mul(sum_c, sum_c, 1.0 / D)
                v.tensor_scalar_mul(sq_c, sq_c, 1.0 / D)
                v.tensor_tensor(rstd_c, sum_c, sum_c, ALU.mult)
                v.tensor_tensor(sq_c, sq_c, rstd_c, ALU.subtract)
        with self.blk() as b:
            @b.scalar
            def _(s):
                s = self.ser(s)
                s.activation(sq_c, sq_c, AF.Sqrt, bias=self.eps_col[:])
        with self.blk() as b:
            @b.vector
            def _(v):
                v = self.ser(v)
                v.reciprocal(rstd_c, sq_c)
                for i, t in enumerate(tiles):
                    bi = 0 if t < NT // 2 else 1
                    sc = self.bc[:, (slot_scale + bi) * D:(slot_scale + bi + 1) * D]
                    sh = self.bc[:, (slot_shift + bi) * D:(slot_shift + bi + 1) * D]
                    v.tensor_scalar(xn[i % 2], self.x_sb[:, t * D:(t + 1) * D],
                                    sum_c[:, i:i + 1], rstd_c[:, i:i + 1], ALU.subtract, ALU.mult)
                    v.tensor_tensor(xn[i % 2], xn[i % 2], sc, ALU.mult)
                    v.tensor_tensor(h_bf_flat[:, i * D:(i + 1) * D], xn[i % 2], sh, ALU.add)
        if tap is not None:
            self.tap_bf16(tap, h_bf_flat, ntl * D)
        A.pop()

    def xbar_blocks(self, dst_bf, src_bf):
        sem = self.sem
        with self.blk() as b:
            @b.scalar
            def _(s):
                s = self.ser(s)
                s.dma_start(out=dst_bf.rearrange("p (n f) -> p n f", f=128),
                            in_=src_bf.rearrange("p (n f) -> p n f", f=128),
                            transpose=True).then_inc(sem["dmax"], 16)
                self.bump("dmax", 16)
                s.wait_ge(sem["dmax"], self.cnt["dmax"])

    def load_w_bf16(self, dst_bf, src_ap, nchunk, ncol):
        sem = self.sem
        with self.blk() as b:
            @b.gpsimd
            def _(g):
                g = self.ser(g)
                g.dma_start(out=dst_bf.rearrange("p (c e) -> p c e", c=nchunk),
                            in_=src_ap.rearrange("(c p) e -> p c e", p=128)
                            ).then_inc(sem["dmag"], 16)
                self.bump("dmag", 16)
                g.wait_ge(sem["dmag"], self.cnt["dmag"])

    def proj_tm(self, h_bt, w_bf, dst_tm, btiles, nc_blk=DC):
        groups = []
        for ti, doff in btiles:
            for eh in range(2):
                def mme(t, pb, ti=ti, eh=eh):
                    r = None
                    for dc in range(DC):
                        r = t.matmul(pb[:, :],
                                     h_bt[:, (ti * nc_blk + dc) * 128:(ti * nc_blk + dc + 1) * 128],
                                     w_bf[:, dc * 1024 + eh * 512: dc * 1024 + (eh + 1) * 512],
                                     start=(dc == 0), stop=(dc == DC - 1))
                    return r

                def eve(v, pb, doff=doff, eh=eh):
                    v.tensor_copy(dst_tm[:, doff + eh * 512: doff + (eh + 1) * 512], pb[:, :])
                groups.append((mme, eve))
        self.mm_pipeline(groups, self.psum[:4])

    def layer0(self):
        A, sem, nc = self.A, self.sem, self.nc
        if "t_h1" in self.taps:
            from contextlib import ExitStack as _ES
            self.emit_bcast(0, [(1, 0, "plus1"), (0, 2, None)])
            with _ES() as pht:
                h1 = self.tmp_sbuf(pht, [128, NT * D], BF16, "h1tap")
                self.emit_ln_mod(h1, 0, 2, tap=self.taps.get("t_h1"))
        for bi in range(B):
            self.l0_attn_batch(bi)
        if "t_xa" in self.taps:
            self.tap_f32(self.taps["t_xa"].rearrange("(t p) d -> p t d", p=128),
                         self.x_sb[:].rearrange("p (t d) -> p t d", d=D))
        self.l0_ffn()
        if "t_x0" in self.taps:
            self.tap_f32(self.taps["t_x0"].rearrange("(t p) d -> p t d", p=128),
                         self.x_sb[:].rearrange("p (t d) -> p t d", d=D))

    def l0_attn_batch(self, bi):
        A, sem, nc = self.A, self.sem, self.nc
        din, psum = self.din, self.psum
        from contextlib import ExitStack as _ES
        gtiles = [bi * 8 + tt for tt in range(8)]
        ph = _ES()
        A.push()
        self.emit_bcast(0, [(1, 0, "plus1"), (0, 2, None)])
        ktm = self.tmp_sbuf(ph, [128, 8 * D], BF16, "ktm")
        vtm = self.tmp_sbuf(ph, [128, 8 * D], BF16, "vtm")

        with _ES() as ph2:
            h1fb = self.tmp_sbuf(ph2, [128, 8 * D], BF16, "h1fb", side="right")
            with _ES() as ph3:
                h1b = self.tmp_sbuf(ph3, [128, 8 * D], BF16, "h1b", side="right")
                self.emit_ln_mod(h1b, 0, 2, tiles=gtiles)
                if bi == 0 and "t_hb" in self.taps:
                    self.tap_bf16(self.taps["t_hb"], h1b, 8 * D)
                self.xbar_blocks(h1fb, h1b)
            if bi == 0 and "t_hf" in self.taps:
                self.tap_bf16(self.taps["t_hf"], h1fb, 8 * D)
            qtm = self.tmp_sbuf(ph, [128, 8 * D], BF16, "qtm")
            A.push()
            wbuf = A.bf16(DC * D, "wbuf")
            btiles = [(tt, tt * 1024) for tt in range(8)]
            self.load_w_bf16(wbuf, din["wkT"], DC, D)
            self.proj_tm(h1fb, wbuf, ktm, btiles)
            self.load_w_bf16(wbuf, din["wvT"], DC, D)
            self.proj_tm(h1fb, wbuf, vtm, btiles)
            self.load_w_bf16(wbuf, din["wqT"], DC, D)
            self.proj_tm(h1fb, wbuf, qtm, btiles)
            A.pop()
        if bi == 0:
            if "t_k0" in self.taps:
                self.tap_bf16(self.taps["t_k0"], ktm, 8 * D)
            if "t_q0" in self.taps:
                self.tap_bf16(self.taps["t_q0"], qtm, 8 * D)

        kbt = self.tmp_sbuf(ph, [128, 8 * D], BF16, "kbt")
        qbt = self.tmp_sbuf(ph, [128, 8 * D], BF16, "qbt")
        A.push()
        updb = A.bf16(8 * 128, "updb")
        preb = A.bf16(8 * 128, "preb")
        sgath = A.f32(8 * 128, "sgath")
        accp = [A.f32(128, "acc0"), A.f32(128, "acc1")]
        zg = A.f32(128, "zg")
        zg_bf = A.bf16(128, "zg_bf")
        colsum = A.f32(8 * 128, "colsum")
        zpre = A.f32(8 * 128, "zpre")
        zpre_bf = A.bf16(8 * 128, "zpre_bf")
        o_f = A.f32(8 * 128, "o_f")
        qzp = [A.f32(128, "qzp0"), A.f32(128, "qzp1")]
        qz = A.f32(8, "qz")
        rden = A.f32(8, "rden")
        ss = A.f32(8, "ss")
        rr = A.f32(8, "rr")
        stg = A.f32(128, "stg")
        zst = A.f32(128, "zst")
        sct = [A.bf16(128, "sct0"), A.bf16(128, "sct1")]
        self.xbar_blocks(kbt, ktm)
        self.xbar_blocks(qbt, qtm)

        cc_in = self.cc_at_in[bi]
        cc_out = self.cc_at_out[bi]

        for h in range(H):
            hc = h * 128
            groups = []
            for c in range(8):
                def mme(t, pb, c=c, hc=hc):
                    return t.matmul(pb[:, 0:128],
                                    ktm[:, c * 1024 + hc: c * 1024 + hc + 128],
                                    vtm[:, c * 1024 + hc: c * 1024 + hc + 128],
                                    start=True, stop=True)

                def eve(v, pb, c=c):
                    v.tensor_copy(updb[:, c * 128:(c + 1) * 128], pb[:, 0:128])
                groups.append((mme, eve))
            self.mm_pipeline(groups, psum[:4])
            t0 = self.cnt["td"]
            v0 = self.cnt["vd"]
            with self.blk() as b:
                @b.tensor
                def _(t):
                    r = None
                    for c in range(8):
                        r = t.matmul(psum[4][:, 0:128], self.ident[:], updb[:, c * 128:(c + 1) * 128],
                                     start=(c == 0), stop=(c == 7))
                    r.then_inc(sem["td"], 1)
                    for tt in range(8):
                        r = t.matmul(psum[5][0:1, 0:128], self.onescol_bf[:],
                                     ktm[:, tt * 1024 + hc: tt * 1024 + hc + 128],
                                     start=(tt == 0), stop=(tt == 7))
                    r.then_inc(sem["td"], 1)
                self.bump("td", 2)

                @b.vector
                def _(v):
                    v = self.ser(v)
                    v.wait_ge(sem["td"], t0 + 1)
                    if h > 0:
                        v.wait_ge(sem["dmas"], self.cnt["dmas"])
                    v.tensor_copy(stg, psum[4][:, 0:128])
                    v.wait_ge(sem["td"], t0 + 2)
                    v.tensor_copy(zst[0:1, :], psum[5][0:1, 0:128])
                    self.chain(v, "vd", 1)

                @b.sync
                def _(s):
                    s.wait_ge(sem["vd"], self.cnt["vd"])
                    s.dma_start(out=cc_in[h * 128:(h + 1) * 128, :], in_=stg).then_inc(sem["dmas"], 16)
                    s.dma_start(out=cc_in[H * 128 + h:H * 128 + h + 1, :], in_=zst[0:1, :]).then_inc(sem["dmas"], 16)
                    self.bump("dmas", 32)
        with self.blk() as b:
            @b.gpsimd
            def _(g):
                g.wait_ge(sem["dmas"], self.cnt["dmas"])
                g.collective_compute(
                    "AllGather", ALU.bypass, replica_groups=[list(range(NCORE))],
                    ins=[cc_in.opt()], outs=[cc_out.opt()],
                ).then_inc(sem["cc"], 1)
                self.bump("cc", 1)
        with self.blk() as b:
            @b.gpsimd
            def _(g):
                g.wait_ge(sem["cc"], self.cnt["cc"])

        for h in range(H):
            self._l0_attn_head(bi, h, ktm, vtm, qtm, kbt, qbt, updb, preb, sgath,
                               accp, zg, zg_bf, colsum, zpre, zpre_bf, o_f, qzp, qz,
                               rden, ss, rr, sct, cc_out)
        if bi == 0 and "t_o0" in self.taps:
            self.tap_bf16(self.taps["t_o0"], qtm, 8 * D)
        A.pop()

        self.emit_bcast(0, [(2, 0, "signmax")])
        obt = kbt
        A.push()
        wbuf = A.bf16(DC * D, "wbuf2")
        epi = A.f32(512, "epi")
        self.xbar_blocks(obt, qtm)
        self.load_w_bf16(wbuf, din["woT"], DC, D)
        groups2 = []
        for tt in range(8):
            for dh in range(2):
                def mme(t, pb, tt=tt, dh=dh):
                    r = None
                    for ec in range(8):
                        r = t.matmul(pb[:, :],
                                     obt[:, (tt * 8 + ec) * 128:(tt * 8 + ec + 1) * 128],
                                     wbuf[:, ec * 1024 + dh * 512: ec * 1024 + (dh + 1) * 512],
                                     start=(ec == 0), stop=(ec == 7))
                    return r

                def eve(v, pb, tt=tt, dh=dh, epi=epi):
                    gt = bi * 8 + tt
                    ga = self.bc[:, bi * D + dh * 512: bi * D + (dh + 1) * 512]
                    xt = self.x_sb[:, gt * D + dh * 512: gt * D + (dh + 1) * 512]
                    v.tensor_tensor(epi, pb[:, :], ga, ALU.mult)
                    v.tensor_tensor(xt, xt, epi, ALU.add)
                groups2.append((mme, eve))
        self.mm_pipeline(groups2, psum[:4])
        A.pop()
        A.pop()
        ph.close()

    def _l0_attn_head(self, bi, h, ktm, vtm, qtm, kbt, qbt, updb, preb, sgath,
                      accp, zg, zg_bf, colsum, zpre, zpre_bf, o_f, qzp, qz,
                      rden, ss, rr, sct, cc_out):
        A, sem, nc, psum = self.A, self.sem, self.nc, self.psum
        hc = h * 128
        ROWS = H * 128 + H

        def kblk(c):
            return kbt[:, (c * H + h) * 128:(c * H + h + 1) * 128]

        def qblk(c):
            return qbt[:, (c * H + h) * 128:(c * H + h + 1) * 128]

        groups = []
        for c in range(8):
            def mme(t, pb, c=c, hc=hc):
                return t.matmul(pb[:, 0:128],
                                ktm[:, c * 1024 + hc: c * 1024 + hc + 128],
                                vtm[:, c * 1024 + hc: c * 1024 + hc + 128],
                                start=True, stop=True)

            def eve(v, pb, c=c):
                v.tensor_copy(updb[:, c * 128:(c + 1) * 128], pb[:, 0:128])
            groups.append((mme, eve))
        self.mm_pipeline(groups, psum[:4])

        gsrc = cc_out.rearrange("(j r) e -> r j e", j=NCORE)
        with self.blk() as b:
            @b.sync
            def _(s):
                s.dma_start(out=sgath.rearrange("p (j e) -> p j e", j=NCORE),
                            in_=gsrc[h * 128:(h + 1) * 128]).then_inc(sem["dmas"], 16)
                s.dma_start(out=zg[0:8, :],
                            in_=cc_out.rearrange("(j r) e -> j r e", j=NCORE)
                            [:, H * 128 + h:H * 128 + h + 1, :].squeeze(1)).then_inc(sem["dmas"], 16)
                self.bump("dmas", 32)
                s.wait_ge(sem["dmas"], self.cnt["dmas"])

        with self.blk() as b:
            @b.vector
            def _(v):
                v = self.ser(v)
                v.memset(accp[0], 0.0)
                for j in range(NCORE):
                    v.scalar_tensor_tensor(accp[(j + 1) % 2], sgath[:, j * 128:(j + 1) * 128],
                                           self.mlt[:, j:j + 1], accp[j % 2], ALU.mult, ALU.add)
                v.tensor_copy(preb[:, 0:128], accp[0])
                v.tensor_copy(zg_bf[0:8, :], zg[0:8, :])
                self.chain(v, "vd", 1)
            t0c = self.cnt["td"]
            v0c = self.cnt["vd"]

            @b.tensor
            def _(t):
                t.wait_ge(sem["vd"], v0c)
                t.matmul(psum[6][0:1, 0:128], self.mask8_bf[:], zg_bf[0:8, :],
                         start=True, stop=True).then_inc(sem["td"], 1)
                r = None
                for tt in range(4):
                    r = t.matmul(psum[7][0:1, tt * 128:(tt + 1) * 128],
                                 self.onescol_bf[:],
                                 ktm[:, tt * 1024 + hc: tt * 1024 + hc + 128],
                                 start=True, stop=True)
                r.then_inc(sem["td"], 1)
                t.wait_ge(sem["vd"], v0c + 1)
                for tt in range(4, 8):
                    r = t.matmul(psum[6][0:1, (tt - 4) * 128:(tt - 3) * 128],
                                 self.onescol_bf[:],
                                 ktm[:, tt * 1024 + hc: tt * 1024 + hc + 128],
                                 start=True, stop=True)
                r.then_inc(sem["td"], 1)
                self.bump("td", 3)

            @b.vector
            def _(v):
                v = self.ser(v)
                v.wait_ge(sem["td"], t0c + 2)
                v.tensor_copy(zpre[0:1, 0:128], psum[6][0:1, 0:128])
                v.tensor_copy(colsum[0:1, 0:512], psum[7][0:1, 0:512])
                self.chain(v, "vd", 1)
                v.wait_ge(sem["td"], t0c + 3)
                v.tensor_copy(colsum[0:1, 512:1024], psum[6][0:1, 0:512])
                self.chain(v, "vd", 1)
        with self.blk() as b:
            @b.vector
            def _(v):
                v = self.ser(v)
                for tt in range(1, 8):
                    v.tensor_tensor(zpre[0:1, tt * 128:(tt + 1) * 128],
                                    zpre[0:1, (tt - 1) * 128: tt * 128],
                                    colsum[0:1, (tt - 1) * 128: tt * 128], ALU.add)
                v.tensor_copy(zpre_bf[0:1, :], zpre[0:1, :])
                for c in range(1, 8):
                    v.tensor_tensor(accp[0], accp[0], updb[:, (c - 1) * 128: c * 128], ALU.add)
                    v.tensor_copy(preb[:, c * 128:(c + 1) * 128], accp[0])
        t0 = self.cnt["td"]
        v0 = self.cnt["vd"]
        with self.blk() as b:
            @b.tensor
            def _(t):
                for tt in range(8):
                    if tt >= 2:
                        t.wait_ge(sem["vd"], v0 + tt - 1)
                    pb = psum[tt % 2]
                    t.matmul(pb[:, 0:128], self.triu[:],
                             ktm[:, tt * 1024 + hc: tt * 1024 + hc + 128], start=True, stop=False)
                    t.matmul(pb[:, 0:128], self.ones1[:],
                             zpre_bf[0:1, tt * 128:(tt + 1) * 128],
                             start=False, stop=True).then_inc(sem["td"], 1)

            @b.vector
            def _(v):
                v = self.ser(v)
                for tt in range(8):
                    v.wait_ge(sem["td"], t0 + tt + 1)
                    pb = psum[tt % 2]
                    v.tensor_tensor(qzp[tt % 2], pb[:, 0:128],
                                    qtm[:, tt * 1024 + hc: tt * 1024 + hc + 128], ALU.mult)
                    v.reduce_sum(qz[:, tt:tt + 1], qzp[tt % 2], AX.X)
                    self.chain(v, "vd", 1)
        self.cnt["td"] = t0 + 8
        with self.blk() as b:
            @b.vector
            def _(v):
                v = self.ser(v)
                v.tensor_scalar(rr, qz, -1.0, None, ALU.mult)
                v.tensor_tensor(rden, qz, rr, ALU.max)
                v.tensor_scalar_max(rden, rden, 1.0)
                v.reciprocal(rden, rden)
        t0 = self.cnt["td"]
        v0 = self.cnt["vd"]
        with self.blk() as b:
            @b.tensor
            def _(t):
                for c in range(8):
                    if c >= 2:
                        t.wait_ge(sem["vd"], v0 + 2 * (c - 2) + 1)
                    t.matmul(psum[c % 2][:, 0:128], kblk(c), qblk(c),
                             start=True, stop=True).then_inc(sem["td"], 1)
                    t.wait_ge(sem["vd"], v0 + 2 * c + 1)
                    if c >= 2:
                        t.wait_ge(sem["vd"], v0 + 2 * (c - 2) + 2)
                    t.matmul(psum[2 + c % 2][:, 0:128], sct[c % 2][:],
                             vtm[:, c * 1024 + hc: c * 1024 + hc + 128],
                             start=True, stop=False)
                    t.matmul(psum[2 + c % 2][:, 0:128], qblk(c), preb[:, c * 128:(c + 1) * 128],
                             start=False, stop=True).then_inc(sem["td"], 1)

            @b.vector
            def _(v):
                v = self.ser(v)
                for c in range(8):
                    v.wait_ge(sem["td"], t0 + 2 * c + 1)
                    v.tensor_tensor(sct[c % 2][:], psum[c % 2][:, 0:128],
                                    self.triu[:], ALU.mult)
                    self.chain(v, "vd", 1)
                    v.wait_ge(sem["td"], t0 + 2 * c + 2)
                    v.tensor_scalar(o_f[:, c * 128:(c + 1) * 128],
                                    psum[2 + c % 2][:, 0:128],
                                    rden[:, c:c + 1], None, ALU.mult)
                    self.chain(v, "vd", 1)
        self.cnt["td"] = t0 + 16
        with self.blk() as b:
            @b.scalar
            def _(s):
                s = self.ser(s)
                for tt in range(8):
                    s.activation(qzp[tt % 2], o_f[:, tt * 128:(tt + 1) * 128], AF.Square,
                                 accum_out=ss[:, tt:tt + 1])
        with self.blk() as b:
            @b.vector
            def _(v):
                v = self.ser(v)
                v.tensor_scalar_mul(ss, ss, 1.0 / 128.0)
                self.chain(v, "vd", 1)

            @b.scalar
            def _(s):
                s = self.ser(s)
                s.wait_ge(sem["vd"], self.cnt["vd"])
                s.activation(ss, ss, AF.Sqrt, bias=self.epsr_col[:])
                self.chain(s, "sd", 1)
        with self.blk() as b:
            @b.vector
            def _(v):
                v = self.ser(v)
                v.reciprocal(rr, ss)
                for tt in range(8):
                    v.scalar_tensor_tensor(qtm[:, tt * 1024 + hc: tt * 1024 + hc + 128],
                                           o_f[:, tt * 128:(tt + 1) * 128],
                                           rr[:, tt:tt + 1], self.normw[:],
                                           ALU.mult, ALU.mult)

    def l0_ffn(self):
        from contextlib import ExitStack as _ES
        A, sem, nc = self.A, self.sem, self.nc
        din, psum = self.din, self.psum
        ph = _ES()
        A.push()

        self.emit_bcast(0, [(4, 0, "plus1"), (3, 2, None)])
        h2bt = self.tmp_sbuf(ph, [128, NT * D], BF16, "h2bt")
        with _ES() as ph2:
            h2tm = self.tmp_sbuf(ph2, [128, NT * D], BF16, "h2tm", side="right")
            self.emit_ln_mod(h2tm, 0, 2, tap=self.taps.get("t_h2"))
            self.xbar_blocks(h2bt, h2tm)
            self.emit_bcast(0, [(5, 0, "signmax")])
            with self.blk() as b:
                @b.vector
                def _(v):
                    v = self.ser(v)
                    tmp = A.f32(D, "basetmp")
                    for t in range(NT):
                        bi = 0 if t < 8 else 1
                        gb = self.bc[:, bi * D:(bi + 1) * D]
                        v.tensor_tensor(tmp, h2tm[:, t * D:(t + 1) * D], gb, ALU.mult)
                        v.tensor_tensor(self.x_sb[:, t * D:(t + 1) * D],
                                        self.x_sb[:, t * D:(t + 1) * D], tmp, ALU.add)
        with self.blk() as b:
            @b.sync
            def _(s):
                s.dma_start(out=self.spill.rearrange("(t p) d -> p t d", p=128),
                            in_=self.x_sb[:].rearrange("p (t d) -> p t d", d=D)).then_inc(sem["dmas"], 16)
                self.bump("dmas", 16)
                s.wait_ge(sem["dmas"], self.cnt["dmas"])

        rt_bf = A.bf16(DC * 8, "rt_bf")
        rtemp = A.f32(1, "rtemp")
        rbt = A.f32(8, "rbt")
        scores = A.f32(NT * 8, "scores")
        with self.blk() as b:
            @b.gpsimd
            def _(g):
                g = self.ser(g)
                g.dma_start(out=rt_bf.rearrange("p (c e) -> p c e", c=DC),
                            in_=din["routerT"].rearrange("(c p) e -> p c e", p=128)
                            ).then_inc(sem["dmag"], 16)
                self.bump("dmag", 16)
                g.wait_ge(sem["dmag"], self.cnt["dmag"])

            @b.vector
            def _(v):
                v = self.ser(v)
                v.tensor_scalar_max(rtemp, self.tcol[:], 0.1)
                v.reciprocal(rtemp, rtemp)
                v.tensor_scalar(rbt, self.rb_b[:], rtemp, None, ALU.mult)
        t0 = self.cnt["td"]
        v0 = self.cnt["vd"]
        with self.blk() as b:
            @b.tensor
            def _(t):
                for tt in range(NT):
                    if tt >= 4:
                        t.wait_ge(sem["vd"], v0 + tt - 3)
                    pb = psum[tt % 4]
                    r = None
                    for dc in range(DC):
                        r = t.matmul(pb[:, 0:8],
                                     h2bt[:, (tt * DC + dc) * 128:(tt * DC + dc + 1) * 128],
                                     rt_bf[:, dc * 8:(dc + 1) * 8],
                                     start=(dc == 0), stop=(dc == DC - 1))
                    r.then_inc(sem["td"], 1)

            @b.vector
            def _(v):
                v = self.ser(v)
                for tt in range(NT):
                    v.wait_ge(sem["td"], t0 + tt + 1)
                    v.scalar_tensor_tensor(scores[:, tt * 8:(tt + 1) * 8], psum[tt % 4][:, 0:8],
                                           rtemp, rbt, ALU.mult, ALU.add)
                    self.chain(v, "vd", 1)
        self.cnt["td"] = t0 + NT

        if "t_sc" in self.taps:
            self.tap_f32(self.taps["t_sc"], scores)
        self.l0_sinkhorn(scores)
        ws = scores
        if "t_ws" in self.taps:
            self.tap_f32(self.taps["t_ws"], ws)

        part = self.x_sb[:, 0:8192].bitcast(BF16)
        wg_sb = self.tmp_sbuf(ph, [128, DC * 1024], BF16, "wgh")
        wp_sb = self.tmp_sbuf(ph, [128, DC * 1024], BF16, "wph")
        wo_sb = self.tmp_sbuf(ph, [128, 8 * 1024], BF16, "woh")
        hloc = A.bf16(8 * 512, "hloc")
        pcop = [A.bf16(512, "pcop0"), A.bf16(512, "pcop1")]
        e1 = A.f32(512, "e1")
        esl = A.f32(4 * 512, "esl")
        for e in range(E):
            for fh in range(2):
                with self.blk() as b:
                    @b.gpsimd
                    def _(g):
                        g = self.ser(g)
                        g.dma_start(out=wg_sb.rearrange("p (c f) -> p c f", c=DC),
                                    in_=din["wg"][e, :, fh * 1024:(fh + 1) * 1024]
                                    .rearrange("(c p) f -> p c f", p=128)).then_inc(sem["dmag"], 16)
                        g.dma_start(out=wp_sb.rearrange("p (c f) -> p c f", c=DC),
                                    in_=din["wp"][e, :, fh * 1024:(fh + 1) * 1024]
                                    .rearrange("(c p) f -> p c f", p=128)).then_inc(sem["dmag"], 16)
                        g.dma_start(out=wo_sb.rearrange("p (c f) -> p c f", c=8),
                                    in_=din["wo_e"][e, fh * 1024:(fh + 1) * 1024, :]
                                    .rearrange("(c p) f -> p c f", p=128)).then_inc(sem["dmag"], 16)
                        self.bump("dmag", 48)
                        g.wait_ge(sem["dmag"], self.cnt["dmag"])
                for g4 in range(4):
                    self.l0_moe_group(e, fh, g4, h2bt, wg_sb, wp_sb, wo_sb, hloc,
                                      pcop, part, e1, esl, ws)
        ph.close()
        A.pop()
        A.push()
        accb = A.f32(D, "accb")
        tmp2 = A.f32(D, "tmp2")
        d0 = self.cnt["dmas"]
        with self.blk() as b:
            @b.sync
            def _(s):
                s.dma_start(out=self.x_sb[:].rearrange("p (t d) -> p t d", d=D),
                            in_=self.spill.rearrange("(t p) d -> p t d", p=128)).then_inc(sem["dmas"], 16)
                self.bump("dmas", 16)
                s.wait_ge(sem["dmas"], self.cnt["dmas"])
        p0 = [self.cnt["dma0"], self.cnt["dma1"]]
        v0 = self.cnt["vd"]
        with self.blk() as b:
            @b.sync
            def _(s):
                for t in range(NT):
                    if t >= 2:
                        s.wait_ge(sem["vd"], v0 + t - 1)
                    s.dma_start(out=accb if t % 2 == 0 else tmp2,
                                in_=self.acc_d[t * 128:(t + 1) * 128, :]
                                ).then_inc(sem["dma%d" % (t % 2)], 16)

            @b.vector
            def _(v):
                v = self.ser(v)
                gbscr = A.f32(D, "gbscr")
                for t in range(NT):
                    bi = 0 if t < 8 else 1
                    gb = self.bc[:, bi * D:(bi + 1) * D]
                    v.wait_ge(sem["dma%d" % (t % 2)], p0[t % 2] + 16 * (t // 2 + 1))
                    v.tensor_tensor(gbscr, accb if t % 2 == 0 else tmp2, gb, ALU.mult)
                    v.tensor_tensor(self.x_sb[:, t * D:(t + 1) * D],
                                    self.x_sb[:, t * D:(t + 1) * D], gbscr, ALU.add)
                    self.chain(v, "vd", 1)
        self.cnt["dma0"] = p0[0] + 16 * 8
        self.cnt["dma1"] = p0[1] + 16 * 8
        A.pop()

    def l0_moe_group(self, e, fh, g4, h2bt, wg_sb, wp_sb, wo_sb, hloc, pcop,
                     part, e1, esl, ws):
        sem, psum = self.sem, self.psum
        t0 = self.cnt["td"]
        v0 = self.cnt["vd"]
        s0 = self.cnt["sd"]
        g0 = self.cnt["dmag"]
        with self.blk() as b:
            @b.tensor
            def _(t):
                for fc in range(8):
                    if fc >= 1:
                        t.wait_ge(sem["sd"], s0 + fc)
                        t.wait_ge(sem["vd"], v0 + fc)
                    r = None
                    for dc in range(DC):
                        rhs = h2bt.rearrange("p (t c f) -> p t c f", c=DC, f=128)[
                            :, g4 * 4:(g4 + 1) * 4, dc:dc + 1, :].squeeze(2)
                        r = t.matmul(psum[4][:, :], wg_sb[:, dc * 1024 + fc * 128: dc * 1024 + fc * 128 + 128],
                                     rhs, start=(dc == 0), stop=(dc == DC - 1))
                    for dc in range(DC):
                        rhs = h2bt.rearrange("p (t c f) -> p t c f", c=DC, f=128)[
                            :, g4 * 4:(g4 + 1) * 4, dc:dc + 1, :].squeeze(2)
                        r = t.matmul(psum[5][:, :], wp_sb[:, dc * 1024 + fc * 128: dc * 1024 + fc * 128 + 128],
                                     rhs, start=(dc == 0), stop=(dc == DC - 1))
                    r.then_inc(sem["td"], 1)
                t.wait_ge(sem["vd"], v0 + 8)
                for dh in range(2):
                    if dh == 1:
                        t.wait_ge(sem["vd"], v0 + 8 + 4)
                    r = None
                    for tt in range(4):
                        for fc in range(8):
                            r = t.matmul(psum[tt][:, :],
                                         hloc[:, fc * 512 + tt * 128: fc * 512 + tt * 128 + 128],
                                         wo_sb[:, fc * 1024 + dh * 512: fc * 1024 + (dh + 1) * 512],
                                         start=(fc == 0), stop=(fc == 7))
                        r.then_inc(sem["td"], 1)

            @b.scalar
            def _(s):
                s = self.ser(s)
                for fc in range(8):
                    s.wait_ge(sem["td"], t0 + fc + 1)
                    s.copy(pcop[fc % 2], psum[5][:, :])
                    self.chain(s, "sd", 1)

            @b.vector
            def _(v):
                v = self.ser(v)
                for fc in range(8):
                    v.wait_ge(sem["sd"], s0 + fc + 1)
                    v.scalar_tensor_tensor(hloc[:, fc * 512:(fc + 1) * 512], psum[4][:, :],
                                           0.0, pcop[fc % 2], ALU.max, ALU.mult)
                    self.chain(v, "vd", 1)
                for dh in range(2):
                    if fh == 1 and dh == 1:
                        v.wait_ge(sem["dmag"], g0 + 16)
                    for tt in range(4):
                        v.wait_ge(sem["td"], t0 + 8 + dh * 4 + tt + 1)
                        gt = g4 * 4 + tt
                        if fh == 0:
                            v.tensor_copy(part[:, gt * 1024 + dh * 512: gt * 1024 + (dh + 1) * 512],
                                          psum[tt][:, :])
                        else:
                            v.tensor_tensor(e1, psum[tt][:, :],
                                            part[:, gt * 1024 + dh * 512: gt * 1024 + (dh + 1) * 512],
                                            ALU.add)
                            v.tensor_scalar(esl[:, tt * 512:(tt + 1) * 512], e1,
                                            ws[:, gt * 8 + e: gt * 8 + e + 1], None, ALU.mult)
                        self.chain(v, "vd", 1)

            @b.gpsimd
            def _(g):
                g = self.ser(g)
                if fh == 1:
                    for dh in range(2):
                        g.wait_ge(sem["vd"], v0 + 8 + dh * 4 + 4)
                        g.dma_start(out=self.acc_d[g4 * 512:(g4 + 1) * 512, dh * 512:(dh + 1) * 512]
                                    .rearrange("(t p) d -> p t d", p=128),
                                    in_=esl.rearrange("p (t d) -> p t d", d=512),
                                    accum_op=(ALU.bypass if e == 0 else ALU.add)
                                    ).then_inc(sem["dmag"], 16)
                        self.bump("dmag", 16)
                    g.wait_ge(sem["dmag"], self.cnt["dmag"])
        self.cnt["td"] = t0 + 16

    def l0_sinkhorn(self, scores):
        A, sem, nc, psum = self.A, self.sem, self.nc, self.psum
        A.push()
        m16 = A.f32(16, "m16")
        n16 = A.f32(16, "n16")
        se16 = A.f32(16, "se16")
        esc = A.f32(8, "esc")
        es_all = A.f32(128, "es_all")
        lsb = A.f32(8, "lsb")
        gs_sb = A.f32(16, "gs_sb")
        bc8 = A.f32(16, "bc8")
        sc3 = scores.rearrange("p (t e) -> p t e", e=8)

        for it in range(3):
            v0 = self.cnt["vd"]
            s0 = self.cnt["sd"]
            with self.blk() as b:
                @b.vector
                def _(v):
                    v = self.ser(v)
                    v.tensor_reduce(m16.unsqueeze(2), sc3, AX.X, ALU.max)
                    v.tensor_scalar_mul(n16, m16, -1.0)
                    self.chain(v, "vd", 1)

                @b.scalar
                def _(s):
                    s = self.ser(s)
                    s.wait_ge(sem["vd"], v0 + 1)
                    for tt in range(NT):
                        s.activation(esc, scores[:, tt * 8:(tt + 1) * 8], AF.Exp,
                                     bias=n16[:, tt:tt + 1], accum_out=se16[:, tt:tt + 1])
                    s.activation(se16, se16, AF.Ln)
                    self.chain(s, "sd", 1)

                @b.vector
                def _(v):
                    v = self.ser(v)
                    v.wait_ge(sem["sd"], s0 + 1)
                    v.tensor_tensor(m16, m16, se16, ALU.add)
                    v.tensor_tensor(sc3, sc3, m16.unsqueeze(2).broadcast_to([128, 16, 8]),
                                    ALU.subtract)
            v0 = self.cnt["vd"]
            s0 = self.cnt["sd"]
            t0 = self.cnt["td"]
            d0 = self.cnt["dmas"]
            c0 = self.cnt["cc"]
            with self.blk() as b:
                @b.scalar
                def _(s):
                    s = self.ser(s)
                    s.activation(es_all, scores, AF.Exp)
                    self.chain(s, "sd", 1)

                @b.tensor
                def _(t):
                    t.wait_ge(sem["sd"], s0 + 1)
                    t.matmul(psum[6][0:1, 0:128], self.onescol_f[:], es_all,
                             start=True, stop=True).then_inc(sem["td"], 1)
                self.bump("td", 1)

                @b.vector
                def _(v):
                    v = self.ser(v)
                    v.wait_ge(sem["td"], t0 + 1)
                    for bb in range(B):
                        if bb > 0:
                            v.wait_ge(sem["dmas"], d0 + 16 * bb)
                        v.tensor_reduce(lsb[0:1, :].unsqueeze(2),
                                        psum[6][0:1, bb * 64:(bb + 1) * 64]
                                        .rearrange("p (t e) -> p e t", e=8), AX.X, ALU.add)
                        self.chain(v, "vd", 1)

                @b.sync
                def _(s):
                    for bb in range(B):
                        s.wait_ge(sem["vd"], v0 + bb + 1)
                        s.dma_start(out=self.ar_in[bb:bb + 1, :], in_=lsb[0:1, :]).then_inc(sem["dmas"], 16)
                self.bump("dmas", 32)

                @b.gpsimd
                def _(g):
                    g.wait_ge(sem["dmas"], d0 + 32)
                    g.collective_compute(
                        "AllReduce", ALU.add, replica_groups=[list(range(NCORE))],
                        ins=[self.ar_in.opt()], outs=[self.ar_out.opt()],
                    ).then_inc(sem["cc"], 1)
                self.bump("cc", 1)
            v0 = self.cnt["vd"]
            s0 = self.cnt["sd"]
            t0 = self.cnt["td"]
            d0 = self.cnt["dmas"]
            with self.blk() as b:
                @b.sync
                def _(s):
                    s.wait_ge(sem["cc"], self.cnt["cc"])
                    s.dma_start(out=gs_sb[0:1, 0:8], in_=self.ar_out[0:1, :]).then_inc(sem["dmas"], 16)
                    s.dma_start(out=gs_sb[0:1, 8:16], in_=self.ar_out[1:2, :]).then_inc(sem["dmas"], 16)
                    self.bump("dmas", 32)
                    s.wait_ge(sem["dmas"], self.cnt["dmas"])
            with self.blk() as b:
                @b.scalar
                def _(s):
                    s = self.ser(s)
                    s.activation(gs_sb[0:1, :], gs_sb[0:1, :], AF.Ln)
                    self.chain(s, "sd", 1)

                @b.tensor
                def _(t):
                    t.wait_ge(sem["sd"], s0 + 1)
                    r = None
                    for bb in range(B):
                        r = t.matmul(psum[7][:, bb * 8:(bb + 1) * 8], self.ones1f[:],
                                     gs_sb[0:1, bb * 8:(bb + 1) * 8], start=True, stop=True)
                    r.then_inc(sem["td"], 1)
                self.bump("td", 1)

                @b.vector
                def _(v):
                    v = self.ser(v)
                    v.wait_ge(sem["td"], t0 + 1)
                    v.tensor_copy(bc8, psum[7][:, 0:16])
                    for bb in range(B):
                        h = sc3[:, bb * 8:(bb + 1) * 8]
                        v.tensor_tensor(h, h,
                                        bc8[:, bb * 8:(bb + 1) * 8].unsqueeze(1).broadcast_to([128, 8, 8]), ALU.subtract)
        v0 = self.cnt["vd"]
        s0 = self.cnt["sd"]
        with self.blk() as b:
            @b.scalar
            def _(s):
                s = self.ser(s)
                s.activation(scores, scores, AF.Exp)
                self.chain(s, "sd", 1)

            @b.vector
            def _(v):
                v = self.ser(v)
                v.wait_ge(sem["sd"], s0 + 1)
                v.tensor_reduce(se16.unsqueeze(2), sc3, AX.X, ALU.add)
                v.tensor_scalar_add(se16, se16, 1e-8)
                v.reciprocal(se16, se16)
                v.tensor_tensor(sc3, sc3, se16.unsqueeze(2).broadcast_to([128, 16, 8]),
                                ALU.mult)
                v.tensor_reduce(m16.unsqueeze(2), sc3, AX.X, ALU.max)
                es3 = es_all.rearrange("p (t e) -> p t e", e=8)
                v.tensor_tensor(es3, sc3,
                                m16.unsqueeze(2).broadcast_to([128, 16, 8]),
                                ALU.is_equal)
                v.scalar_tensor_tensor(es_all, es_all, -1e30, scores, ALU.mult, ALU.add)
                v.tensor_reduce(n16.unsqueeze(2), es3, AX.X, ALU.max)
                v.tensor_tensor(se16, m16, n16, ALU.add)
                v.tensor_scalar_add(se16, se16, 1e-8)
                v.reciprocal(se16, se16)
                v.tensor_tensor(es3, sc3,
                                n16.unsqueeze(2).broadcast_to([128, 16, 8]),
                                ALU.is_ge)
                v.tensor_tensor(scores, scores, es_all, ALU.mult)
                v.tensor_tensor(sc3, sc3, se16.unsqueeze(2).broadcast_to([128, 16, 8]),
                                ALU.mult)
        A.pop()

    def emit_quant(self, w_f32, out_bf, K, scr):
        sem = self.sem
        nb = K // 16
        a, bsc, csc = scr["a"][:, :K], scr["b"][:, :K], scr["c"][:, :K]
        bm, rs = scr["bm"][:, :nb], scr["rs"][:, :nb]
        w3 = w_f32.rearrange("p (g k) -> p g k", k=16)
        with self.blk() as b:
            @b.vector
            def _(v):
                v = self.ser(v)
                v.tensor_reduce(bm.unsqueeze(2), w3, AX.X, ALU.max, apply_absolute_value=True)
                v.tensor_scalar(bm, bm, 1.0 / 6.0, 1e-12, ALU.mult, ALU.max)
                v.reciprocal(rs, bm)
                rsb = rs.unsqueeze(2).broadcast_to([128, nb, 16])
                a3 = a.rearrange("p (g k) -> p g k", k=16)
                v.tensor_scalar(a, w_f32, -1.0, None, ALU.mult)
                v.tensor_tensor(a, w_f32, a, ALU.max)
                v.tensor_tensor(a3, a3, rsb, ALU.mult)
                v.tensor_single_scalar(bsc, a, 0.25, ALU.is_gt)
                v.scalar_tensor_tensor(bsc, a, 0.75, bsc, ALU.is_gt, ALU.add)
                v.scalar_tensor_tensor(bsc, a, 1.25, bsc, ALU.is_gt, ALU.add)
                v.scalar_tensor_tensor(bsc, a, 1.75, bsc, ALU.is_gt, ALU.add)
                v.tensor_scalar_mul(bsc, bsc, 0.5)
                v.scalar_tensor_tensor(bsc, a, 2.5, bsc, ALU.is_gt, ALU.add)
                v.scalar_tensor_tensor(bsc, a, 3.5, bsc, ALU.is_gt, ALU.add)
                v.tensor_single_scalar(csc, a, 5.0, ALU.is_gt)
                v.scalar_tensor_tensor(bsc, csc, 2.0, bsc, ALU.mult, ALU.add)
                v.tensor_single_scalar(csc, w_f32, 0.0, ALU.is_ge)
                v.tensor_scalar(csc, csc, 2.0, -1.0, ALU.mult, ALU.add)
                c3 = csc.rearrange("p (g k) -> p g k", k=16)
                v.tensor_tensor(c3, c3, bm.unsqueeze(2).broadcast_to([128, nb, 16]), ALU.mult)
                v.tensor_tensor(out_bf, bsc, csc, ALU.mult)

    def ph0_quant(self):
        from contextlib import ExitStack as _ES
        A, sem, nc, din = self.A, self.sem, self.nc, self.din
        phq = _ES()
        scr = {"a": self.tmp_sbuf(phq, [128, I1], F32, "qa"),
               "b": self.tmp_sbuf(phq, [128, I1], F32, "qb"),
               "c": self.tmp_sbuf(phq, [128, I1], F32, "qc"),
               "bm": self.tmp_sbuf(phq, [128, I1 // 16], F32, "qbm"),
               "rs": self.tmp_sbuf(phq, [128, I1 // 16], F32, "qrs")}
        wrow = self.tmp_sbuf(phq, [128, I1], F32, "wrow")
        wq = self.tmp_sbuf(phq, [128, I1], BF16, "wqrow")
        wqt = self.tmp_sbuf(phq, [128, I1], BF16, "wqtrow")
        A.push()
        for (wname, dq, nrows, K) in [("wgate_n", self.wq_d["gate"], 22, 1024),
                                      ("wup_n", self.wq_d["up"], 22, 1024),
                                      ("wdown_n", self.wq_d["down"], 8, I1)]:
            for r in range(nrows):
                with self.blk() as b:
                    @b.sync
                    def _(s):
                        s.dma_start(out=wrow[:, :K],
                                    in_=din[wname][r * 128:(r + 1) * 128, :]).then_inc(sem["dmas"], 16)
                        self.bump("dmas", 16)
                        s.wait_ge(sem["dmas"], self.cnt["dmas"])
                self.emit_quant(wrow[:, :K], wq[:, :K], K, scr)
                self.xbar_blocks(wqt[:, :K], wq[:, :K])
                with self.blk() as b:
                    @b.gpsimd
                    def _(g):
                        g = self.ser(g)
                        g.dma_start(out=dq[r * 128:(r + 1) * 128, :], in_=wqt[:, :K]).then_inc(sem["dmag"], 16)
                        self.bump("dmag", 16)
                        g.wait_ge(sem["dmag"], self.cnt["dmag"])
        A.pop()
        phq.close()

    def layer1(self):
        from contextlib import ExitStack as _ES
        A, sem, nc, din, psum = self.A, self.sem, self.nc, self.din, self.psum
        ph = _ES()

        self.emit_bcast(1, [(1, 0, "plus1"), (0, 2, None)])
        h3bt = self.tmp_sbuf(ph, [128, NT * D], BF16, "h3bt")
        with _ES() as ph2:
            h3tm = self.tmp_sbuf(ph2, [128, NT * D], BF16, "h3tm", side="right")
            self.emit_ln_mod(h3tm, 0, 2, tap=self.taps.get("t_h3"))
            self.xbar_blocks(h3bt, h3tm)
        with self.blk() as b:
            @b.sync
            def _(s):
                s.dma_start(out=self.spill.rearrange("(t p) d -> p t d", p=128),
                            in_=self.x_sb[:].rearrange("p (t d) -> p t d", d=D)).then_inc(sem["dmas"], 16)
                self.bump("dmas", 16)
                s.wait_ge(sem["dmas"], self.cnt["dmas"])
        xcv = self.x_sb[:].bitcast(BF16)

        A.push()
        wbuf = A.bf16(DC * D, "ipw")
        zstg = [A.bf16(512, "zst0"), A.bf16(512, "zst1")]
        for eg in range(4):
            with self.blk() as b:
                @b.gpsimd
                def _(g):
                    g = self.ser(g)
                    g.dma_start(out=wbuf.rearrange("p (c e) -> p c e", c=DC),
                                in_=din["inpT"][:, eg * 1024:(eg + 1) * 1024]
                                .rearrange("(c p) e -> p c e", p=128)).then_inc(sem["dmag"], 16)
                    self.bump("dmag", 16)
                    g.wait_ge(sem["dmag"], self.cnt["dmag"])
            t0 = self.cnt["td"]
            v0 = self.cnt["vd"]
            zp0 = [self.cnt["dma0"], self.cnt["dma1"]]
            with self.blk() as b:
                @b.tensor
                def _(t):
                    i = 0
                    for ec in range(8):
                        for ts in range(4):
                            if i >= 4:
                                t.wait_ge(sem["vd"], v0 + i - 3)
                            pb = psum[i % 4]
                            r = None
                            for dc in range(DC):
                                rhs = h3bt.rearrange("p (t c f) -> p t c f", c=DC, f=128)[
                                    :, ts * 4:(ts + 1) * 4, dc:dc + 1, :].squeeze(2)
                                r = t.matmul(pb[:, :], wbuf[:, dc * 1024 + ec * 128: dc * 1024 + ec * 128 + 128],
                                             rhs, start=(dc == 0), stop=(dc == DC - 1))
                            r.then_inc(sem["td"], 1)
                            i += 1

                @b.vector
                def _(v):
                    v = self.ser(v)
                    i = 0
                    for ec in range(8):
                        e_abs = eg * 8 + ec
                        for ts in range(4):
                            v.wait_ge(sem["td"], t0 + i + 1)
                            pb = psum[i % 4]
                            if e_abs < 16:
                                bsel = ts // 2
                                toff = (ts % 2) * 512
                                dst = xcv[:, (bsel * 16 + e_abs) * 1024 + toff:
                                          (bsel * 16 + e_abs) * 1024 + toff + 512]
                                v.tensor_copy(dst, pb[:, :])
                            else:
                                if i >= 2:
                                    v.wait_ge(sem["dma%d" % (i % 2)],
                                              zp0[i % 2] + 16 * ((i - 2) // 2 + 1))
                                v.tensor_copy(zstg[i % 2], pb[:, :])
                            self.chain(v, "vd", 1)
                            i += 1

                @b.gpsimd
                def _(g):
                    if eg >= 2:
                        i = 0
                        for ec in range(8):
                            e_abs = eg * 8 + ec
                            zc = e_abs - 16
                            for ts in range(4):
                                g.wait_ge(sem["vd"], v0 + i + 1)
                                bsel = ts // 2
                                toff = (ts % 2) * 512
                                g.dma_start(out=self.zspill[bsel * 16 + zc, :, toff:toff + 512],
                                            in_=zstg[i % 2]).then_inc(sem["dma%d" % (i % 2)], 16)
                                self.bump("dma%d" % (i % 2), 16)
                                i += 1
                        g.wait_ge(sem["dma0"], self.cnt["dma0"])
                        g.wait_ge(sem["dma1"], self.cnt["dma1"])
            self.cnt["td"] = t0 + 32
        A.pop()
        ph.close()

        A.push()
        halo_sb = A.f32(B * 16 * 3, "halo_sb")
        sel_bf = A.bf16(32, "sel_bf")
        gall_bf = A.bf16(384, "gall_bf")
        d0 = self.cnt["dmag"]
        with self.blk() as b:
            @b.gpsimd
            def _(g):
                g = self.ser(g)
                for bb in range(B):
                    g.dma_start(out=self.cc_halo_in[bb * 16:(bb + 1) * 16, :]
                                .rearrange("c (p t) -> p c t", p=128),
                                in_=xcv.rearrange("p (b c t) -> p b c t", b=B, c=16)
                                [:, bb:bb + 1, :, SPAN - 3:].squeeze(1),
                                allow_slow_non_contiguous=True).then_inc(sem["dmag"], 16)
                    self.bump("dmag", 16)
                g.wait_ge(sem["dmag"], self.cnt["dmag"])
                g.collective_compute(
                    "AllGather", ALU.bypass, replica_groups=[list(range(NCORE))],
                    ins=[self.cc_halo_in.opt()], outs=[self.cc_halo_out.opt()],
                ).then_inc(sem["cc"], 1)
                self.bump("cc", 1)
        hstage = A.f32(384, "hstage")
        t0 = self.cnt["td"]
        v0 = self.cnt["vd"]
        s0g = self.cnt["dmag"]
        with self.blk() as b:
            @b.gpsimd
            def _(g):
                g = self.ser(g)
                g.wait_ge(sem["cc"], self.cnt["cc"])
                for hh in range(2):
                    g.dma_start(out=gall_bf[:], in_=self.cc_halo_out[hh * 128:(hh + 1) * 128, :]
                                ).then_inc(sem["dmag"], 16)
                    g.dma_start(out=sel_bf[:], in_=self.din["sel48"][hh * 128:(hh + 1) * 128, :]
                                ).then_inc(sem["dmag"], 16)
                    self.bump("dmag", 32)
                    if hh == 1:
                        g.wait_ge(sem["td"], t0 + 1)
                g.wait_ge(sem["dmag"], self.cnt["dmag"])

            @b.tensor
            def _(t):
                for hh in range(2):
                    t.wait_ge(sem["dmag"], s0g + 32 * (hh + 1))
                    t.matmul(psum[6][0:32, 0:384], sel_bf[:], gall_bf[:],
                             start=(hh == 0), stop=(hh == 1)).then_inc(sem["td"], 1)
            self.bump("td", 2)

            @b.vector
            def _(v):
                v = self.ser(v)
                v.wait_ge(sem["td"], t0 + 2)
                v.tensor_copy(hstage[0:32, :], psum[6][0:32, 0:384])
                self.chain(v, "vd", 1)
        with self.blk() as b:
            @b.sync
            def _(s):
                s.dma_start(out=self.halo_d, in_=hstage[0:32, :]).then_inc(sem["dmas"], 16)
                self.bump("dmas", 16)
                s.wait_ge(sem["dmas"], self.cnt["dmas"])
        with self.blk() as b:
            @b.sync
            def _(s):
                for bb in range(B):
                    s.dma_start(out=halo_sb.rearrange("p (b c t) -> p b c t", b=B, c=16)
                                [:, bb:bb + 1].squeeze(1),
                                in_=self.halo_d[bb * 16:(bb + 1) * 16, :]
                                .rearrange("c (p t) -> p c t", p=128),
                                allow_slow_non_contiguous=True).then_inc(sem["dmas"], 16)
                    self.bump("dmas", 16)
                s.wait_ge(sem["dmas"], self.cnt["dmas"])

        cwcol = A.f32(4 * 16, "cwcol")
        cbcol = A.f32(16, "cbcol")
        with self.blk() as b:
            @b.sync
            def _(s):
                s.dma_start(out=cwcol, in_=din["convw"]).then_inc(sem["dmas"], 16)
                s.dma_start(out=cbcol, in_=din["convb"]).then_inc(sem["dmas"], 16)
                self.bump("dmas", 32)
                s.wait_ge(sem["dmas"], self.cnt["dmas"])
        acc = [A.f32(SPAN, "cacc0"), A.f32(SPAN, "cacc1")]
        for bb in range(B):
            for dic in range(16):
                xoff = (bb * 16 + dic) * 1024
                xcb = xcv[:, xoff:xoff + 1024]
                ab = acc[(bb * 16 + dic) % 2]
                v0 = self.cnt["vd"]
                s0 = self.cnt["sd"]
                with self.blk() as b:
                    @b.vector
                    def _(v):
                        v = self.ser(v)
                        if dic >= 2 or bb > 0:
                            v.wait_ge(sem["sd"], s0)
                        v.tensor_scalar(ab, xcb, cwcol[:, 3 * 16 + dic:3 * 16 + dic + 1], None, ALU.mult)
                        for j in range(3):
                            a = 3 - j
                            v.scalar_tensor_tensor(ab[:, a:], xcb[:, 0:1024 - a],
                                                   cwcol[:, j * 16 + dic:j * 16 + dic + 1],
                                                   ab[:, a:], ALU.mult, ALU.add)
                            for t in range(a):
                                tau = t + j
                                v.scalar_tensor_tensor(ab[:, t:t + 1],
                                                       halo_sb[:, (bb * 16 + dic) * 3 + tau:
                                                               (bb * 16 + dic) * 3 + tau + 1],
                                                       cwcol[:, j * 16 + dic:j * 16 + dic + 1],
                                                       ab[:, t:t + 1], ALU.mult, ALU.add)
                        self.chain(v, "vd", 1)

                    @b.scalar
                    def _(s):
                        s = self.ser(s)
                        s.wait_ge(sem["vd"], v0 + 1)
                        s.activation(xcb, ab, AF.Silu, bias=cbcol[:, dic:dic + 1])
                        self.chain(s, "sd", 1)
        A.pop()
        self.l1_scan()
        self.l1_ffn()

    def l1_scan(self):
        from contextlib import ExitStack as _ES
        A, sem, nc, din, psum = self.A, self.sem, self.nc, self.din, self.psum
        xcv = self.x_sb[:].bitcast(BF16)
        ph = _ES()
        dlt = self.tmp_sbuf(ph, [128, B * 16 * 1024], BF16, "dlt")
        y0s = self.tmp_sbuf(ph, [128, 4 * 1024], BF16, "y0s")
        A.push()
        acsb = A.f32(16 * 16, "acsb")
        dcol = A.f32(16, "dcol")
        dtbc = A.f32(16, "dtbc")
        dsp = A.f32(32, "dsp")
        P_sb = A.f32(512, "P_sb")
        Q_sb = A.f32(512, "Q_sb")
        hin = A.f32(512, "hin")
        hacc = [A.f32(512, "ha0"), A.f32(512, "ha1")]

        s0 = self.cnt["sd"]
        with self.blk() as b:
            @b.sync
            def _(s):
                s.dma_start(out=acsb.rearrange("p (c n) -> p c n", c=16),
                            in_=din["alog"].rearrange("(c p) n -> p c n", p=128)).then_inc(sem["dmas"], 16)
                s.dma_start(out=dcol, in_=din["dparam"]).then_inc(sem["dmas"], 16)
                s.dma_start(out=dtbc, in_=din["dtb"]).then_inc(sem["dmas"], 16)
                self.bump("dmas", 48)
                s.wait_ge(sem["dmas"], self.cnt["dmas"])

            @b.scalar
            def _(s):
                s = self.ser(s)
                s.wait_ge(sem["dmas"], self.cnt["dmas"])
                s.activation(acsb, acsb, AF.Exp)
                self.chain(s, "sd", 1)

            @b.vector
            def _(v):
                v = self.ser(v)
                v.wait_ge(sem["sd"], s0 + 1)
                v.tensor_scalar_mul(acsb, acsb, -1.0)

        A.push()
        dtl_bf = A.bf16(2048, "dtl_bf")
        A.push()
        BC_sb = A.f32(2048, "BC_sb")
        xpw = A.bf16(16 * 96, "xpw")
        with self.blk() as b:
            @b.gpsimd
            def _(g):
                g = self.ser(g)
                g.dma_start(out=xpw.rearrange("p (c e) -> p c e", c=16),
                            in_=din["xprojT"].rearrange("(c p) e -> p c e", p=128)).then_inc(sem["dmag"], 16)
                self.bump("dmag", 16)
                g.wait_ge(sem["dmag"], self.cnt["dmag"])
        t0 = self.cnt["td"]
        v0 = self.cnt["vd"]
        with self.blk() as b:
            @b.tensor
            def _(t):
                i = 0
                for bb in range(B):
                    for ts in range(2):
                        if i >= 1:
                            t.wait_ge(sem["vd"], v0 + 2 * i)
                        r1 = r2 = None
                        for dic in range(16):
                            rhs = xcv[:, (bb * 16 + dic) * 1024 + ts * 512:
                                      (bb * 16 + dic) * 1024 + ts * 512 + 512]
                            r1 = t.matmul(psum[6][0:64, :], xpw[:, dic * 96:dic * 96 + 64], rhs,
                                          start=(dic == 0), stop=(dic == 15))
                        r1.then_inc(sem["td"], 1)
                        for dic in range(16):
                            rhs = xcv[:, (bb * 16 + dic) * 1024 + ts * 512:
                                      (bb * 16 + dic) * 1024 + ts * 512 + 512]
                            r2 = t.matmul(psum[7][0:32, :], xpw[:, dic * 96 + 64:dic * 96 + 96], rhs,
                                          start=(dic == 0), stop=(dic == 15))
                        r2.then_inc(sem["td"], 1)
                        i += 1

            @b.vector
            def _(v):
                v = self.ser(v)
                for i in range(4):
                    bb, ts = i // 2, i % 2
                    co = bb * 1024 + ts * 512
                    v.wait_ge(sem["td"], t0 + 2 * i + 1)
                    v.tensor_copy(dtl_bf[0:64, co:co + 512], psum[6][0:64, :])
                    v.wait_ge(sem["td"], t0 + 2 * i + 2)
                    v.tensor_copy(BC_sb[0:32, co:co + 512], psum[7][0:32, :])
                    self.chain(v, "vd", 2)
        self.cnt["td"] = t0 + 8
        with self.blk() as b:
            @b.sync
            def _(s):
                s.dma_start(out=self.bc_d, in_=BC_sb[0:32, :]).then_inc(sem["dmas"], 16)
                self.bump("dmas", 16)
                s.wait_ge(sem["dmas"], self.cnt["dmas"])
        A.pop()
        A.push()
        sp_a = A.f32(512, "sp_a")
        sp_l = [A.f32(512, "sp_l0"), A.f32(512, "sp_l1")]
        sp_t = A.f32(512, "sp_t")
        phw = _ES()
        dtw = self.tmp_sbuf(phw, [64, DI], BF16, "dtw")
        with self.blk() as b:
            @b.gpsimd
            def _(g):
                g = self.ser(g)
                g.dma_start(out=dtw[0:64, :], in_=din["dtprojT"]).then_inc(sem["dmag"], 16)
                self.bump("dmag", 16)
                g.wait_ge(sem["dmag"], self.cnt["dmag"])
        t0 = self.cnt["td"]
        s0 = self.cnt["sd"]
        v0 = self.cnt["vd"]
        with self.blk() as b:
            @b.tensor
            def _(t):
                i = 0
                for bb in range(B):
                    for dic in range(16):
                        for ts in range(2):
                            if i >= 4:
                                t.wait_ge(sem["vd"], v0 + i - 3)
                            t.matmul(psum[i % 4][:, :], dtw[0:64, dic * 128:(dic + 1) * 128],
                                     dtl_bf[0:64, bb * 1024 + ts * 512: bb * 1024 + ts * 512 + 512],
                                     start=True, stop=True).then_inc(sem["td"], 1)
                            i += 1

            @b.scalar
            def _(s):
                s = self.ser(s)
                i = 0
                for bb in range(B):
                    for dic in range(16):
                        for ts in range(2):
                            s.wait_ge(sem["td"], t0 + i + 1)
                            s.activation(sp_a, psum[i % 4][:, :], AF.Abs,
                                         bias=dtbc[:, dic:dic + 1])
                            s.activation(sp_a, sp_a, AF.Exp, scale=self.neg1_col[:])

                            if i >= 2:
                                s.wait_ge(sem["vd"], v0 + i - 1)
                            s.activation(sp_l[i % 2], sp_a, AF.Ln, bias=self.onescol_f[:])
                            self.chain(s, "sd", 1)
                            i += 1

            @b.vector
            def _(v):
                v = self.ser(v)
                i = 0
                for bb in range(B):
                    for dic in range(16):
                        for ts in range(2):
                            v.wait_ge(sem["sd"], s0 + i + 1)
                            v.tensor_scalar(sp_t, psum[i % 4][:, :],
                                            dtbc[:, dic:dic + 1], 0.0, ALU.add, ALU.max)
                            v.tensor_tensor(dlt[:, (bb * 16 + dic) * 1024 + ts * 512:
                                            (bb * 16 + dic) * 1024 + ts * 512 + 512],
                                            sp_t, sp_l[i % 2], ALU.add)
                            self.chain(v, "vd", 1)
                            i += 1
            self.cnt["td"] = t0 + 64
        A.pop()
        A.pop()
        phw.close()

        s0 = self.cnt["sd"]
        v0 = self.cnt["vd"]
        with self.blk() as b:
            @b.vector
            def _(v):
                v = self.ser(v)
                for bb in range(B):
                    for dic in range(16):
                        v.reduce_sum(dsp[:, bb * 16 + dic: bb * 16 + dic + 1],
                                     dlt[:, (bb * 16 + dic) * 1024:(bb * 16 + dic + 1) * 1024], AX.X)
                self.chain(v, "vd", 1)

            @b.scalar
            def _(s):
                s = self.ser(s)
                s.wait_ge(sem["vd"], v0 + 1)
                for bb in range(B):
                    for dic in range(16):
                        s.activation(P_sb[:, (bb * 16 + dic) * 16:(bb * 16 + dic + 1) * 16],
                                     acsb[:, dic * 16:(dic + 1) * 16], AF.Exp,
                                     scale=dsp[:, bb * 16 + dic: bb * 16 + dic + 1])

        dab = [A.f32(1024, "dab0"), A.f32(1024, "dab1")]
        dbu = A.f32(1024, "dbu")
        h0b = A.f32(1024, "h0b")
        dub = self.tmp_sbuf(ph, [128, 4 * 1024], F32, "dub")
        hcb = A.bf16(1024, "hcb")
        bbc = self.tmp_sbuf(ph, [128, 1024], BF16, "bbc")
        cbc = self.tmp_sbuf(ph, [128, 1024], BF16, "cbc")
        for passno in range(2):
            for bb in range(B):
                for dg in range(4):
                    dics = [dg * 4 + k for k in range(4)]
                    if passno == 0:
                        with self.blk() as b:
                            @b.vector
                            def _(v):
                                v = self.ser(v)
                                for k, dic in enumerate(dics):
                                    o = (bb * 16 + dic) * 1024
                                    v.tensor_tensor(dub[:, k * 1024:(k + 1) * 1024],
                                                    dlt[:, o:o + 1024], xcv[:, o:o + 1024], ALU.mult)
                    else:
                        with self.blk() as b:
                            @b.vector
                            def _(v):
                                v = self.ser(v)
                                for k, dic in enumerate(dics):
                                    o = (bb * 16 + dic) * 1024
                                    v.tensor_tensor_scan(dub[:, k * 1024:(k + 1) * 1024],
                                                         dlt[:, o:o + 1024], dlt[:, o:o + 1024],
                                                         0.0, ALU.add, ALU.bypass)

                            @b.gpsimd
                            def _(g):
                                g = self.ser(g)
                                for k, dic in enumerate(dics):
                                    g.dma_start(out=y0s[:, k * 1024:(k + 1) * 1024],
                                                in_=self.y0_d[(bb * 16 + dic) * 128:(bb * 16 + dic + 1) * 128, :]
                                                ).then_inc(sem["dmag"], 16)
                                    self.bump("dmag", 16)
                                g.wait_ge(sem["dmag"], self.cnt["dmag"])
                    for n in range(DS):
                        t0 = self.cnt["td"]
                        v0 = self.cnt["vd"]
                        s0 = self.cnt["sd"]
                        g0 = self.cnt["dmag"]
                        with self.blk() as b:
                            @b.gpsimd
                            def _(g):
                                g = self.ser(g)
                                if passno == 0:
                                    g.dma_start(out=bbc[:],
                                                in_=self.bc_d[n:n + 1, bb * 1024:(bb + 1) * 1024]
                                                .partition_broadcast(128).squeeze(1)
                                                ).then_inc(sem["dmag"], 16)
                                    self.bump("dmag", 16)
                                g.dma_start(out=cbc[:],
                                            in_=self.bc_d[16 + n:17 + n, bb * 1024:(bb + 1) * 1024]
                                            .partition_broadcast(128).squeeze(1)
                                            ).then_inc(sem["dmag"], 16)
                                self.bump("dmag", 16)
                                g.wait_ge(sem["dmag"], self.cnt["dmag"])

                            @b.scalar
                            def _(s):
                                s = self.ser(s)
                                for k, dic in enumerate(dics):
                                    o = (bb * 16 + dic) * 1024
                                    if k >= 2:
                                        s.wait_ge(sem["vd"], v0 + 4 * (k - 1))
                                    src = dlt[:, o:o + 1024] if passno == 0 else dub[:, k * 1024:(k + 1) * 1024]
                                    s.activation(dab[k % 2][:], src, AF.Exp,
                                                 scale=acsb[:, dic * 16 + n: dic * 16 + n + 1])
                                    self.chain(s, "sd", 1)

                            @b.vector
                            def _(v):
                                v = self.ser(v)
                                v.wait_ge(sem["dmag"], self.cnt["dmag"])
                                for k, dic in enumerate(dics):
                                    col = (bb * 16 + dic) * 16 + n
                                    v.wait_ge(sem["sd"], s0 + k + 1)
                                    if k >= 1:
                                        v.wait_ge(sem["td"], t0 + k)
                                    if passno == 0:
                                        v.tensor_tensor(dbu[:], dub[:, k * 1024:(k + 1) * 1024],
                                                        bbc[:], ALU.mult)
                                        v.tensor_tensor_scan(h0b[:], dab[k % 2][:], dbu[:],
                                                             0.0, ALU.mult, ALU.add)
                                        v.tensor_copy(Q_sb[:, col:col + 1], h0b[:, 1023:1024])
                                        v.tensor_tensor(hcb[:], h0b[:], cbc[:], ALU.mult)
                                    else:
                                        v.scalar_tensor_tensor(hcb[:], dab[k % 2][:],
                                                               hin[:, col:col + 1], cbc[:],
                                                               ALU.mult, ALU.mult)
                                    self.chain(v, "vd", 4)

                            @b.tensor
                            def _(t):
                                for k, dic in enumerate(dics):
                                    t.wait_ge(sem["vd"], v0 + (k + 1) * 4)
                                    pb0 = psum[2 * k]
                                    pb1 = psum[2 * k + 1]
                                    st = (n == 0 and passno == 0)
                                    if passno == 1 and n == 0:
                                        t.matmul(pb0[:, :], self.ident[:], y0s[:, k * 1024:k * 1024 + 512],
                                                 start=True, stop=False)
                                        t.matmul(pb1[:, :], self.ident[:], y0s[:, k * 1024 + 512:(k + 1) * 1024],
                                                 start=True, stop=False)
                                    r = t.matmul(pb0[:, :], self.ident[:], hcb[:, 0:512],
                                                 start=st, stop=False)
                                    r = t.matmul(pb1[:, :], self.ident[:], hcb[:, 512:1024],
                                                 start=st, stop=False)
                                    r.then_inc(sem["td"], 1)
                            self.bump("td", 4)
                    v0 = self.cnt["vd"]
                    g0 = self.cnt["dmag"]
                    s0 = self.cnt["sd"]
                    if passno == 0:
                        with self.blk() as b:
                            @b.vector
                            def _(v):
                                v = self.ser(v)
                                for k, dic in enumerate(dics):
                                    v.tensor_copy(y0s[:, k * 1024:k * 1024 + 512], psum[2 * k][:, :])
                                    v.tensor_copy(y0s[:, k * 1024 + 512:(k + 1) * 1024], psum[2 * k + 1][:, :])
                                    self.chain(v, "vd", 2)

                            @b.gpsimd
                            def _(g):
                                g = self.ser(g)
                                for k, dic in enumerate(dics):
                                    g.wait_ge(sem["vd"], v0 + 2 * (k + 1))
                                    g.dma_start(out=self.y0_d[(bb * 16 + dic) * 128:(bb * 16 + dic + 1) * 128, :],
                                                in_=y0s[:, k * 1024:(k + 1) * 1024]
                                                ).then_inc(sem["dmag"], 16)
                                    self.bump("dmag", 16)
                                g.wait_ge(sem["dmag"], self.cnt["dmag"])
                    if passno == 1:
                        for k, dic in enumerate(dics):
                            o = (bb * 16 + dic) * 1024
                            v0 = self.cnt["vd"]
                            s0 = self.cnt["sd"]
                            g0 = self.cnt["dmag"]
                            with self.blk() as b:
                                @b.gpsimd
                                def _(g):
                                    g = self.ser(g)
                                    g.dma_start(out=cbc[:], in_=self.zspill[bb * 16 + dic, :, :]
                                                ).then_inc(sem["dmag"], 16)
                                    self.bump("dmag", 16)
                                    g.wait_ge(sem["dmag"], self.cnt["dmag"])

                                @b.scalar
                                def _(s):
                                    s = self.ser(s)
                                    s.wait_ge(sem["dmag"], self.cnt["dmag"])
                                    s.activation(bbc[:], cbc[:], AF.Silu)
                                    self.chain(s, "sd", 1)

                                @b.vector
                                def _(v):
                                    v = self.ser(v)
                                    v.wait_ge(sem["sd"], s0 + 1)
                                    for hh in range(2):
                                        v.scalar_tensor_tensor(h0b[:, hh * 512:(hh + 1) * 512],
                                                               xcv[:, o + hh * 512:o + (hh + 1) * 512],
                                                               dcol[:, dic:dic + 1],
                                                               psum[2 * k + hh][:, :],
                                                               ALU.mult, ALU.add)
                                    v.tensor_tensor(dlt[:, o:o + 1024], h0b[:], bbc[:], ALU.mult)
                                    self.chain(v, "vd", 1)
            if passno == 0:
                d0 = self.cnt["dmas"]
                with self.blk() as b:
                    @b.sync
                    def _(s):
                        s.dma_start(out=self.cc_pq_in[0:128, :], in_=P_sb).then_inc(sem["dmas"], 16)
                        s.dma_start(out=self.cc_pq_in[128:256, :], in_=Q_sb).then_inc(sem["dmas"], 16)
                        self.bump("dmas", 32)

                    @b.gpsimd
                    def _(g):
                        g.wait_ge(sem["dmas"], self.cnt["dmas"])
                        g.collective_compute(
                            "AllGather", ALU.bypass, replica_groups=[list(range(NCORE))],
                            ins=[self.cc_pq_in.opt()], outs=[self.cc_pq_out.opt()],
                        ).then_inc(sem["cc"], 1)
                        self.bump("cc", 1)
                d0 = self.cnt["dmas"]
                with self.blk() as b:
                    @b.sync
                    def _(s):
                        s.wait_ge(sem["cc"], self.cnt["cc"])
                        s.dma_start(out=hin, in_=self.cc_pq_out[0:128, :]).then_inc(sem["dmas"], 16)
                        self.bump("dmas", 16)
                        s.wait_ge(sem["dmas"], self.cnt["dmas"])
                with self.blk() as b:
                    @b.vector
                    def _(v):
                        v = self.ser(v)
                        v.memset(hin, 0.0)
                for j in range(NCORE):
                    with self.blk() as b:
                        @b.sync
                        def _(s):
                            s.dma_start(out=hacc[0], in_=self.cc_pq_out[j * 256:j * 256 + 128, :]
                                        ).then_inc(sem["dmas"], 16)
                            s.dma_start(out=hacc[1], in_=self.cc_pq_out[j * 256 + 128:j * 256 + 256, :]
                                        ).then_inc(sem["dmas"], 16)
                            self.bump("dmas", 32)
                            s.wait_ge(sem["dmas"], self.cnt["dmas"])
                    with self.blk() as b:
                        @b.vector
                        def _(v):
                            v = self.ser(v)
                            v.tensor_tensor(hacc[0], hacc[0], hin, ALU.mult)
                            v.tensor_tensor(hacc[0], hacc[0], hacc[1], ALU.add)
                            v.tensor_tensor(hacc[0], hacc[0], hin, ALU.subtract)
                            v.scalar_tensor_tensor(hin, hacc[0], self.mlt[:, j:j + 1],
                                                   hin, ALU.mult, ALU.add)
        A.pop()
        self.emit_bcast(1, [(2, 0, "signmax")])
        with self.blk() as b:
            @b.sync
            def _(s):
                s.dma_start(out=self.x_sb[:].rearrange("p (t d) -> p t d", d=D),
                            in_=self.spill.rearrange("(t p) d -> p t d", p=128)).then_inc(sem["dmas"], 16)
                self.bump("dmas", 16)
                s.wait_ge(sem["dmas"], self.cnt["dmas"])
        A.push()
        ow = A.bf16(16 * 512, "ow")
        epi2 = A.f32(512, "epi2")
        for eh in range(2):
            with self.blk() as b:
                @b.gpsimd
                def _(g):
                    g = self.ser(g)
                    g.dma_start(out=ow.rearrange("p (c e) -> p c e", c=16),
                                in_=din["outpT"][:, eh * 512:(eh + 1) * 512]
                                .rearrange("(c p) e -> p c e", p=128)).then_inc(sem["dmag"], 16)
                    self.bump("dmag", 16)
                    g.wait_ge(sem["dmag"], self.cnt["dmag"])
            groups = []
            for tt in range(NT):
                bb = 0 if tt < 8 else 1
                ttb = tt % 8

                def mme(t, pb, bb=bb, ttb=ttb):
                    r = None
                    for dic in range(16):
                        r = t.matmul(pb[:, :],
                                     dlt[:, (bb * 16 + dic) * 1024 + ttb * 128:
                                         (bb * 16 + dic) * 1024 + ttb * 128 + 128],
                                     ow[:, dic * 512:(dic + 1) * 512],
                                     start=(dic == 0), stop=(dic == 15))
                    return r

                def eve(v, pb, tt=tt, bb=bb, eh=eh):
                    ga = self.bc[:, bb * D + eh * 512: bb * D + (eh + 1) * 512]
                    xt = self.x_sb[:, tt * D + eh * 512: tt * D + (eh + 1) * 512]
                    v.tensor_tensor(epi2, pb[:, :], ga, ALU.mult)
                    v.tensor_tensor(xt, xt, epi2, ALU.add)
                groups.append((mme, eve))
            self.mm_pipeline(groups, psum[:4])
        A.pop()
        ph.close()
        if "t_x1a" in self.taps:
            self.tap_f32(self.taps["t_x1a"].rearrange("(t p) d -> p t d", p=128),
                         self.x_sb[:].rearrange("p (t d) -> p t d", d=D))

    def l1_ffn(self):
        from contextlib import ExitStack as _ES
        A, sem, nc, din, psum = self.A, self.sem, self.nc, self.din, self.psum
        ph = _ES()
        xqg = self.tmp_sbuf(ph, [128, 4 * D], BF16, "xqg")
        self.emit_bcast(1, [(4, 0, "plus1"), (3, 2, None)])
        with _ES() as ph2:
            h4tm = self.tmp_sbuf(ph2, [128, NT * D], BF16, "h4tm", side="right")
            self.emit_ln_mod(h4tm, 0, 2, tap=self.taps.get("t_h4"))
            A.push()
            qscr = {"a": A.f32(D, "qa2"), "b": A.f32(D, "qb2"), "c": A.f32(D, "qc2"),
                    "bm": A.f32(D // 16, "qbm2"), "rs": A.f32(D // 16, "qrs2")}
            h4f = A.f32(D, "h4f")
            for t in range(NT):
                with self.blk() as b:
                    @b.vector
                    def _(v):
                        v = self.ser(v)
                        v.tensor_copy(h4f, h4tm[:, t * D:(t + 1) * D])
                self.emit_quant(h4f, h4tm[:, t * D:(t + 1) * D], D, qscr)
            for tgx in range(4):
                self.xbar_blocks(xqg, h4tm[:, tgx * 4 * D:(tgx + 1) * 4 * D])
                with self.blk() as b:
                    @b.sync
                    def _(s):
                        s.dma_start(out=self.xq_d[:, tgx * 4 * D:(tgx + 1) * 4 * D],
                                    in_=xqg).then_inc(sem["dmas"], 16)
                        self.bump("dmas", 16)
                        s.wait_ge(sem["dmas"], self.cnt["dmas"])
        self.emit_bcast(1, [(5, 0, "signmax")])
        with self.blk() as b:
            @b.sync
            def _(s):
                s.dma_start(out=self.spill.rearrange("(t p) d -> p t d", p=128),
                            in_=self.x_sb[:].rearrange("p (t d) -> p t d", d=D)).then_inc(sem["dmas"], 16)
                self.bump("dmas", 16)
                s.wait_ge(sem["dmas"], self.cnt["dmas"])
        A.pop()

        wqbt = self.x_sb[:].bitcast(BF16)[:, 0:22 * 1024]
        gsil = self.tmp_sbuf(ph, [128, 4 * I1], BF16, "gsil")
        hq = self.tmp_sbuf(ph, [128, 4 * I1], BF16, "hq")
        hqbt = gsil
        A.push()
        qscr2 = {"a": A.f32(1024, "qa3"), "b": A.f32(1024, "qb3"), "c": A.f32(1024, "qc3"),
                 "bm": A.f32(64, "qbm3"), "rs": A.f32(64, "qrs3")}
        hrow = A.f32(1024, "hrow")
        eslb = [A.f32(512, "eslb0"), A.f32(512, "eslb1")]
        NSL = [512] * 5 + [256]

        def gu_mm(t, pb, tg, tt, ns, n0, nw):
            r = None
            for kc in range(8):
                rhs_blocks = wqbt.rearrange("p (nb kc f) -> p nb kc f", kc=8, f=128)
                lhsT = xqg[:, (tt * 8 + kc) * 128:(tt * 8 + kc + 1) * 128]
                rhs = rhs_blocks[:, n0 // 128:(n0 + nw) // 128, kc:kc + 1, :].squeeze(2)
                r = t.matmul(pb[:, 0:nw], lhsT, rhs, start=(kc == 0), stop=(kc == 7))
            return r

        for tg in range(4):
            with self.blk() as b:
                @b.sync
                def _(s):
                    s.dma_start(out=xqg, in_=self.xq_d[:, tg * 4 * D:(tg + 1) * 4 * D]
                                ).then_inc(sem["dmas"], 16)
                    self.bump("dmas", 16)
                    s.wait_ge(sem["dmas"], self.cnt["dmas"])
            for wi, wname in enumerate(("gate", "up")):
                with self.blk() as b:
                    @b.gpsimd
                    def _(g):
                        g = self.ser(g)
                        g.dma_start(out=wqbt.rearrange("p (nb k) -> p nb k", nb=22),
                                    in_=self.wq_d[wname].rearrange("(nb p) k -> p nb k", p=128)
                                    ).then_inc(sem["dmag"], 16)
                        self.bump("dmag", 16)
                        g.wait_ge(sem["dmag"], self.cnt["dmag"])
                t0 = self.cnt["td"]
                v0 = self.cnt["vd"]
                s0 = self.cnt["sd"]
                with self.blk() as b:
                    @b.tensor
                    def _(t):
                        i = 0
                        n0 = 0
                        for ns, nw in enumerate(NSL):
                            for tt in range(4):
                                if i >= 4:
                                    if wi == 0:
                                        t.wait_ge(sem["sd"], s0 + i - 3)
                                    else:
                                        t.wait_ge(sem["vd"], v0 + i - 3)
                                gu_mm(t, psum[i % 4], tg, tt, ns, n0, nw).then_inc(sem["td"], 1)
                                i += 1
                            n0 += nw

                    if wi == 0:
                        @b.scalar
                        def _(s):
                            s = self.ser(s)
                            i = 0
                            n0 = 0
                            for ns, nw in enumerate(NSL):
                                for tt in range(4):
                                    s.wait_ge(sem["td"], t0 + i + 1)
                                    s.activation(gsil[:, tt * I1 + n0: tt * I1 + n0 + nw],
                                                 psum[i % 4][:, 0:nw], AF.Silu)
                                    self.chain(s, "sd", 1)
                                    i += 1
                                n0 += nw
                    else:
                        @b.vector
                        def _(v):
                            v = self.ser(v)
                            i = 0
                            n0 = 0
                            for ns, nw in enumerate(NSL):
                                for tt in range(4):
                                    v.wait_ge(sem["td"], t0 + i + 1)
                                    v.tensor_tensor(hq[:, tt * I1 + n0: tt * I1 + n0 + nw],
                                                    psum[i % 4][:, 0:nw],
                                                    gsil[:, tt * I1 + n0: tt * I1 + n0 + nw],
                                                    ALU.mult)
                                    self.chain(v, "vd", 1)
                                    i += 1
                                n0 += nw
                self.cnt["td"] = t0 + 24
            for tt in range(4):
                for (qo, cw) in ((0, 1024), (1024, 1024), (2048, 768)):
                    with self.blk() as b:
                        @b.vector
                        def _(v):
                            v = self.ser(v)
                            v.tensor_copy(hrow[:, :cw], hq[:, tt * I1 + qo:tt * I1 + qo + cw])
                    self.emit_quant(hrow[:, :cw], hq[:, tt * I1 + qo:tt * I1 + qo + cw], cw, qscr2)
            self.xbar_blocks(hqbt, hq)
            with self.blk() as b:
                @b.gpsimd
                def _(g):
                    g = self.ser(g)
                    g.dma_start(out=wqbt.rearrange("p (db k) -> p db k", db=8),
                                in_=self.wq_d["down"].rearrange("(db p) k -> p db k", p=128)
                                ).then_inc(sem["dmag"], 16)
                    self.bump("dmag", 16)
                    g.wait_ge(sem["dmag"], self.cnt["dmag"])
            t0 = self.cnt["td"]
            v0 = self.cnt["vd"]
            dp0 = [self.cnt["dma0"], self.cnt["dma1"]]
            with self.blk() as b:
                @b.tensor
                def _(t):
                    i = 0
                    for tt in range(4):
                        for dh in range(2):
                            if i >= 4:
                                t.wait_ge(sem["vd"], v0 + i - 3)
                            r = None
                            for fb in range(22):
                                lhsT = hqbt[:, (tt * 22 + fb) * 128:(tt * 22 + fb + 1) * 128]
                                wv = wqbt.rearrange("p (db fb f) -> p db fb f", db=8, f=128)
                                rhs = wv[:, dh * 4:(dh + 1) * 4, fb:fb + 1, :].squeeze(2)
                                r = t.matmul(psum[i % 4][:, :], lhsT, rhs,
                                             start=(fb == 0), stop=(fb == 21))
                            r.then_inc(sem["td"], 1)
                            i += 1

                @b.vector
                def _(v):
                    v = self.ser(v)
                    i = 0
                    for tt in range(4):
                        bb = 0 if tg < 2 else 1
                        for dh in range(2):
                            v.wait_ge(sem["td"], t0 + i + 1)
                            if i >= 2:
                                v.wait_ge(sem["dma%d" % (i % 2)],
                                          dp0[i % 2] + 16 * ((i - 2) // 2 + 1))
                            gb = self.bc[:, bb * D + dh * 512: bb * D + (dh + 1) * 512]
                            v.tensor_tensor(eslb[i % 2], psum[i % 4][:, :], gb, ALU.mult)
                            self.chain(v, "vd", 1)
                            i += 1

                @b.gpsimd
                def _(g):
                    i = 0
                    for tt in range(4):
                        gt = tg * 4 + tt
                        for dh in range(2):
                            g.wait_ge(sem["vd"], v0 + i + 1)
                            g.dma_start(out=self.spill[gt * 128:(gt + 1) * 128, dh * 512:(dh + 1) * 512],
                                        in_=eslb[i % 2], accum_op=ALU.add
                                        ).then_inc(sem["dma%d" % (i % 2)], 16)
                            self.bump("dma%d" % (i % 2), 16)
                            i += 1
                    g.wait_ge(sem["dma0"], self.cnt["dma0"])
                    g.wait_ge(sem["dma1"], self.cnt["dma1"])
            self.cnt["td"] = t0 + 8
        A.pop()
        ph.close()

    def finish(self):
        sem = self.sem
        with self.blk() as b:
            @b.sync
            def _(s):
                s.dma_start(out=self.x_sb[:].rearrange("p (t d) -> p t d", d=D),
                            in_=self.spill.rearrange("(t p) d -> p t d", p=128)).then_inc(sem["dmas"], 16)
                self.bump("dmas", 16)
                s.wait_ge(sem["dmas"], self.cnt["dmas"])
                s.dma_start(out=self.y_out.rearrange("(t p) d -> p t d", p=128),
                            in_=self.x_sb[:].rearrange("p (t d) -> p t d", d=D)).then_inc(sem["dmas"], 16)
                self.bump("dmas", 16)
                s.wait_ge(sem["dmas"], self.cnt["dmas"])


def build_nc(debug_taps=()):
    return MK(debug_taps).nc


def kernel(**inputs):
    maps = host_stage(inputs)
    mk = MK()
    res = run_bass_kernel_spmd(mk.nc, maps, core_ids=list(range(NCORE)))
    return host_gather(res.results)


# version 02
# speedup vs baseline: 1.2056x; 1.2056x over previous
import numpy as np
from contextlib import ExitStack

import concourse.bass as bass
import concourse.mybir as mybir
from concourse.bass_utils import run_bass_kernel_spmd

F32 = mybir.dt.float32
BF16 = mybir.dt.bfloat16
AF = mybir.ActivationFunctionType
ALU = mybir.AluOpType
AX = mybir.AxisListType

NCORE = 8
B = 2
SPAN = 1024
TOK = B * SPAN
NT = TOK // 128
D = 1024
DC = D // 128
H = 8
HD = 128
CH = 64
NCH = SPAN // CH
E = 8
FF = 2048
DI = 2048
DS = 16
DTR = 64
I1 = 2816
LN_EPS = 1e-5
RMS_EPS = 1e-5


def _f(x):
    return np.ascontiguousarray(np.asarray(x), dtype=np.float32)


def host_stage(inp):
    x = _f(inp["L0_attn_hidden_states"])
    shared = {
        "wqT": _f(inp["L0_attn_Wq"].T), "wkT": _f(inp["L0_attn_Wk"].T),
        "wvT": _f(inp["L0_attn_Wv"].T), "woT": _f(inp["L0_attn_Wo"].T),
        "normw_b": np.ascontiguousarray(np.broadcast_to(_f(inp["L0_attn_norm_w"])[None, :], (128, 128))),
        "routerT": _f(inp["L0_ffn_router_w"].T),
        "router_b_b": np.ascontiguousarray(np.broadcast_to(_f(inp["L0_ffn_router_b"])[None, :], (128, 8))),
        "temp_col": np.ascontiguousarray(np.broadcast_to(_f(inp["L0_ffn_temperature"]).reshape(1, 1), (128, 1))),
        "wg": _f(inp["L0_ffn_wg"]), "wp": _f(inp["L0_ffn_wp"]), "wo_e": _f(inp["L0_ffn_wo"]),
        "c0T": _f(inp["L0_c"].T), "wmod0": _f(inp["L0_wmod"]),
        "c1T": _f(inp["L1_c"].T), "wmod1": _f(inp["L1_wmod"]),
        "inpT": _f(inp["L1_attn_in_proj_w"].T),
        "convw": _f(np.ascontiguousarray(
            inp["L1_attn_conv1d_w"].reshape(16, 128, 4).transpose(1, 2, 0).reshape(128, 64))),
        "convb": _f(np.ascontiguousarray(inp["L1_attn_conv1d_b"].reshape(16, 128).T)),
        "xprojT": _f(inp["L1_attn_x_proj_w"].T),
        "dtprojT": _f(inp["L1_attn_dt_proj_w"].T),
        "dtb": _f(np.ascontiguousarray(inp["L1_attn_dt_proj_b"].reshape(16, 128).T)),
        "alog": _f(inp["L1_attn_A_log"]),
        "dparam": _f(np.ascontiguousarray(inp["L1_attn_D_param"].reshape(16, 128).T)),
        "outpT": _f(inp["L1_attn_out_proj_w"].T),
        "wgate_n": _f(inp["L1_ffn_w_gate"]), "wup_n": _f(inp["L1_ffn_w_up"]),
        "wdown_n": _f(inp["L1_ffn_w_down"]),
    }
    maps = []
    for c in range(NCORE):
        m = dict(shared)
        xs = np.concatenate([x[0, c * SPAN:(c + 1) * SPAN], x[1, c * SPAN:(c + 1) * SPAN]], axis=0)
        m["x"] = np.ascontiguousarray(xs)
        mlt = np.zeros((128, 8), np.float32); mlt[:, :c] = 1.0
        meq = np.zeros((128, 8), np.float32)
        if c > 0:
            meq[:, c - 1] = 1.0
        m["mask_lt"] = mlt
        m["mask_eq"] = meq
        m8 = np.zeros((8, 1), np.float32); m8[:c, 0] = 1.0
        m["mask8"] = m8
        s48 = np.zeros((256, 32), np.float32)
        if c > 0:
            s48[(c - 1) * 32:(c - 1) * 32 + 32, :] = np.eye(32, dtype=np.float32)
        m["sel48"] = s48
        maps.append(m)
    return maps


def host_gather(results):
    out = np.empty((B, NCORE * SPAN, D), np.float32)
    for c, r in enumerate(results):
        y = r["y"]
        out[0, c * SPAN:(c + 1) * SPAN] = y[:SPAN]
        out[1, c * SPAN:(c + 1) * SPAN] = y[SPAN:]
    return out


class Arena:

    def __init__(self, t, nf32):
        self.t = t
        self.n = nf32
        self.off = 0
        self.marks = []
        self.peak = 0

    def push(self):
        self.marks.append(self.off)

    def pop(self):
        self.off = self.marks.pop()

    def f32(self, n, name=""):
        self.off = (self.off + 63) & ~63
        assert self.off + n <= self.n, f"arena overflow: {name} need {n} at {self.off}/{self.n}"
        ap = self.t[:, self.off:self.off + n]
        self.off += n
        self.peak = max(self.peak, self.off)
        return ap

    def bf16(self, n, name=""):
        nw = (n + 1) // 2
        return self.f32(nw, name).bitcast(BF16)


_SER_WAIT_ONLY = {"dma_start", "dma_start_transpose", "collective_compute", "indirect_dma_start"}
_SER_SKIP = {"wait_ge", "wait_op", "nop", "sem_inc", "halt", "drain", "br",
             "then_inc", "snap", "reg_mov", "register"}


class Ser:

    def __init__(self, eng, sem, cnt, key):
        self._e = eng
        self._sem = sem
        self._cnt = cnt
        self._key = key

    def __getattr__(self, name):
        attr = getattr(self._e, name)
        if not callable(attr) or name.startswith("_") or name in _SER_SKIP:
            return attr

        def wrapper(*a, **k):
            n = self._cnt[self._key]
            if n > 0:
                self._e.wait_ge(self._sem, n)
            r = attr(*a, **k)
            if name not in _SER_WAIT_ONLY and hasattr(r, "then_inc"):
                r.then_inc(self._sem, 1)
                self._cnt[self._key] = n + 1
            return r
        return wrapper


class MK:
    def __init__(self, debug_taps=()):
        self.nc = bass.Bass("TRN2", target_bir_lowering=False, debug=False)
        self.es = ExitStack()
        self.debug_taps = dict(debug_taps)
        self._build()

    def blk(self):
        return self.nc.Block()

    def ser(self, eng):
        cls = type(eng).__name__
        if "Vector" in cls:
            key = "selfv"
        elif "Scalar" in cls:
            key = "selfs"
        elif "GpSimd" in cls:
            key = "selfg"
        else:
            return eng
        return Ser(eng, self.sem[key], self.cnt, key)

    def chain(self, proxy, name, k=1):
        ins = proxy._e.wait_ge(proxy._sem, self.cnt[proxy._key])
        ins.then_inc(self.sem[name], k)
        self.cnt[name] += k

    def sbuf(self, name, shape, dtype):
        return self.es.enter_context(self.nc.sbuf_tensor(name, shape, dtype))

    _uid = 0

    def tmp_sbuf(self, stack, shape, dtype, name="tmp", side="left"):
        MK._uid += 1
        return stack.enter_context(self.nc.sbuf_tensor(f"tsb_{name}_{MK._uid}", shape, dtype,
                                                       side=side, align_bytes=256))[:]

    def bump(self, name, k=1):
        self.cnt[name] += k
        return self.cnt[name]

    def tap_bf16(self, tap, sb_flat, n):
        sem = self.sem
        nt = n // D
        with self.blk() as b:
            @b.gpsimd
            def _(g):
                g = self.ser(g)
                g.dma_start(out=tap.rearrange("(t p) d -> p t d", p=128),
                            in_=sb_flat.rearrange("p (t d) -> p t d", d=D)).then_inc(sem["dmag"], 16)
                self.bump("dmag", 16)
                g.wait_ge(sem["dmag"], self.cnt["dmag"])

    def tap_f32(self, tap, sb_ap):
        sem = self.sem
        with self.blk() as b:
            @b.sync
            def _(s):
                s = self.ser(s)
                s.dma_start(out=tap, in_=sb_ap).then_inc(sem["dmas"], 16)
                self.bump("dmas", 16)
                s.wait_ge(sem["dmas"], self.cnt["dmas"])

    def mm_pipeline(self, groups, banks, evac="vector", extra_tensor=None):
        nb = len(banks)
        t0 = self.cnt["td"]
        v0 = self.cnt["vd"]
        sem = self.sem
        with self.blk() as b:
            @b.tensor
            def _(t):
                t = self.ser(t)
                for i, (mme, _) in enumerate(groups):
                    if i >= nb:
                        t.wait_ge(sem["vd"], v0 + i - nb + 1)
                    last = mme(t, banks[i % nb])
                    last.then_inc(sem["td"], 1)
                if extra_tensor is not None:
                    extra_tensor(t)

            emitter = getattr(b, evac)

            @emitter
            def _(v):
                v = self.ser(v)
                for i, (_, eve) in enumerate(groups):
                    v.wait_ge(sem["td"], t0 + i + 1)
                    eve(v, banks[i % nb])
                    self.chain(v, "vd", 1)
        self.cnt["td"] = t0 + len(groups)

    def _build(self):
        nc, es = self.nc, self.es
        self._in_shapes = {}
        for name, shape in [
            ("x", [TOK, D]), ("wqT", [D, D]), ("wkT", [D, D]), ("wvT", [D, D]), ("woT", [D, D]),
            ("normw_b", [128, 128]), ("routerT", [D, 8]), ("router_b_b", [128, 8]), ("temp_col", [128, 1]),
            ("wg", [E, D, FF]), ("wp", [E, D, FF]), ("wo_e", [E, FF, D]),
            ("c0T", [D, 2]), ("wmod0", [D, 6 * D]), ("c1T", [D, 2]), ("wmod1", [D, 6 * D]),
            ("inpT", [D, 2 * DI]), ("convw", [128, 64]), ("convb", [128, 16]),
            ("xprojT", [DI, DTR + 2 * DS]), ("dtprojT", [DTR, DI]), ("dtb", [128, 16]),
            ("alog", [DI, DS]), ("dparam", [128, 16]), ("outpT", [DI, D]),
            ("wgate_n", [I1, D]), ("wup_n", [I1, D]), ("wdown_n", [D, I1]),
            ("mask_lt", [128, 8]), ("mask_eq", [128, 8]), ("mask8", [8, 1]), ("sel48", [256, 32]),
        ]:
            self._in_shapes[name] = shape

        class LazyIn:
            def __init__(s2):
                s2.created = {}

            def __getitem__(s2, name):
                if name not in s2.created:
                    s2.created[name] = nc.dram_tensor(name, self._in_shapes[name], F32,
                                                      kind="ExternalInput").ap()
                return s2.created[name]

        self.din = LazyIn()
        din = self.din

        self.y_out = nc.dram_tensor("y", [TOK, D], F32, kind="ExternalOutput").ap()
        self.taps = {}
        for tname, tshape in self.debug_taps.items():
            self.taps[tname] = nc.dram_tensor(tname, list(tshape), F32, kind="ExternalOutput").ap()

        self.spill = nc.dram_tensor("spill", [TOK, D], F32).ap()
        self.mod_d = nc.dram_tensor("mod_d", [2, 2 * 6 * D], F32).ap()
        self.acc_d = nc.dram_tensor("acc_d", [TOK, D], F32).ap()
        self.zspill = nc.dram_tensor("zspill", [B * (DI // 128), 128, SPAN], BF16).ap()
        self.bc_d = nc.dram_tensor("bc_d", [2 * DS, B * SPAN], F32).ap()
        self.wq_d = {w: nc.dram_tensor(f"w{w}q_d", [I1, D] if w != "down" else [D, I1], BF16).ap()
                     for w in ("gate", "up", "down")}
        self.cc_at_in = [nc.dram_tensor(f"cc_at_in{b}", [H * 128 + H, 128], F32).ap() for b in range(B)]
        self.cc_at_out = [nc.dram_tensor(f"cc_at_out{b}", [NCORE * (H * 128 + H), 128], F32).ap() for b in range(B)]
        self.ar_in = nc.dram_tensor("ar_in", [B, 8], F32).ap()
        self.ar_out = nc.dram_tensor("ar_out", [B, 8], F32).ap()
        self.cc_pq_in = nc.dram_tensor("cc_pq_in", [256, B * 16 * DS], F32).ap()
        self.cc_pq_out = nc.dram_tensor("cc_pq_out", [NCORE * 256, B * 16 * DS], F32).ap()
        self.halo_d = nc.dram_tensor("halo_d", [B * 16, 384], F32).ap()
        self.cc_halo_in = nc.dram_tensor("cc_halo_in", [B * 16, 384], F32).ap()
        self.cc_halo_out = nc.dram_tensor("cc_halo_out", [NCORE * B * 16, 384], F32).ap()
        self.y0_d = nc.dram_tensor("y0_d", [B * 16 * 128, 1024], BF16).ap()
        self.xq_d = nc.dram_tensor("xq_d", [128, NT * D], BF16).ap()

        self.x_sb = self.sbuf("x_sb", [128, NT * D], F32)
        arena_t = self.sbuf("arena", [128, 8192], F32)
        self.A = Arena(arena_t[:], 8192)
        self.ones1 = self.sbuf("ones1", [1, 128], BF16)
        self.ones1f = self.sbuf("ones1f", [1, 128], F32)
        self.ident = self.sbuf("ident", [128, 128], BF16)
        self.triu = self.sbuf("triu", [128, 128], BF16)
        self.tri64 = self.sbuf("tri64", [64, 64], F32)
        self.normw = self.sbuf("normw", [128, 128], F32)
        self.rb_b = self.sbuf("rb_b", [128, 8], F32)
        self.tcol = self.sbuf("tcol", [128, 1], F32)
        self.mlt = self.sbuf("mlt", [128, 8], F32)
        self.meq = self.sbuf("meq", [128, 8], F32)
        self.csil = self.sbuf("csil", [128, 2 * DC * 2], BF16)
        self.bc = self.sbuf("bcmod", [128, 4 * D], F32)
        self.eps_col = self.sbuf("eps_col", [128, 1], F32)
        self.epsr_col = self.sbuf("epsr_col", [128, 1], F32)
        self.onescol_bf = self.sbuf("onescol_bf", [128, 1], BF16)
        self.onescol_f = self.sbuf("onescol_f", [128, 1], F32)
        self.neg1_col = self.sbuf("neg1_col", [128, 1], F32)
        self.mask8_bf = self.sbuf("mask8_bf", [8, 1], BF16)

        self.sem = {n: es.enter_context(nc.semaphore(n))
                    for n in ["dmas", "dmag", "dmax", "td", "vd", "sd", "gd", "cc", "dma0", "dma1", "dmar", "selfv", "selfs", "selfg"]}
        self.cnt = {n: 0 for n in self.sem}

        self.psum = [es.enter_context(nc.psum_tensor(f"pb{i}", [128, 512], F32)) for i in range(8)]

        self.ph0_consts()
        self.ph0_mod()
        self.ph0_quant()
        self.layer0()
        self.layer1()
        self.finish()

        nc._es = es

    def ph0_consts(self):
        nc, A, sem, din = self.nc, self.A, self.sem, self.din
        with self.blk() as b:
            @b.sync
            def _(s):
                s = self.ser(s)
                for dst, src in [(self.x_sb[:].rearrange("p (t d) -> p t d", d=D),
                                  din["x"].rearrange("(t p) d -> p t d", p=128)),
                                 (self.normw[:], din["normw_b"]),
                                 (self.rb_b[:], din["router_b_b"]),
                                 (self.tcol[:], din["temp_col"]),
                                 (self.mlt[:], din["mask_lt"]),
                                 (self.meq[:], din["mask_eq"])]:
                    s.dma_start(out=dst, in_=src).then_inc(sem["dmas"], 16)
                    self.bump("dmas", 16)
                s.wait_ge(sem["dmas"], self.cnt["dmas"])

            @b.gpsimd
            def _(g):
                g = self.ser(g)
                g.memset(self.ones1[:], 1.0)
                g.memset(self.ones1f[:], 1.0)
                g.memset(self.eps_col[:], LN_EPS)
                g.memset(self.epsr_col[:], RMS_EPS)
                g.memset(self.onescol_bf[:], 1.0)
                g.memset(self.onescol_f[:], 1.0)
                g.memset(self.neg1_col[:], -1.0)
                g.dma_start(out=self.mask8_bf[:], in_=din["mask8"]).then_inc(sem["dmag"], 16)
                self.bump("dmag", 16)
                g.wait_ge(sem["dmag"], self.cnt["dmag"])

        A.push()
        it_r = A.f32(128, "it_r")
        it_c = A.f32(128, "it_c")
        with self.blk() as b:
            @b.gpsimd
            def _(g):
                g = self.ser(g)
                g.iota(it_r, [[1, 128]], channel_multiplier=0, allow_small_or_imprecise_dtypes=True)
                g.iota(it_c, [[0, 128]], channel_multiplier=1, allow_small_or_imprecise_dtypes=True)
        with self.blk() as b:
            @b.vector
            def _(v):
                v = self.ser(v)
                v.tensor_tensor(self.ident[:], it_c, it_r, ALU.is_equal)
                v.tensor_tensor(self.triu[:], it_c, it_r, ALU.is_le)
                v.tensor_tensor(self.tri64[:], it_c[0:64, 0:64], it_r[0:64, 0:64], ALU.is_le)
        A.pop()

    def ph0_mod(self):
        nc, A, sem, din = self.nc, self.A, self.sem, self.din
        A.push()
        c_f32 = A.f32(2 * DC * 2, "c_f32")
        with self.blk() as b:
            @b.sync
            def _(s):
                s = self.ser(s)
                s.dma_start(out=c_f32[:, 0:16].rearrange("p (c t) -> p c t", t=2),
                            in_=din["c0T"].rearrange("(c p) t -> p c t", p=128)).then_inc(sem["dmas"], 16)
                s.dma_start(out=c_f32[:, 16:32].rearrange("p (c t) -> p c t", t=2),
                            in_=din["c1T"].rearrange("(c p) t -> p c t", p=128)).then_inc(sem["dmas"], 16)
                self.bump("dmas", 32)
                s.wait_ge(sem["dmas"], self.cnt["dmas"])
        with self.blk() as b:
            @b.scalar
            def _(s):
                s = self.ser(s)
                s.activation(c_f32, c_f32, AF.Silu)
        with self.blk() as b:
            @b.vector
            def _(v):
                v = self.ser(v)
                v.tensor_copy(self.csil[:], c_f32)
        A.pop()

        from contextlib import ExitStack as _ES
        phms = _ES()
        self.mod_stage = self.tmp_sbuf(phms, [2, 1024], F32, "mod_stage")
        for l in range(2):
            wname = "wmod0" if l == 0 else "wmod1"
            for eh in range(2):
                A.push()
                phm = _ES()
                wmod_sb = self.tmp_sbuf(phm, [128, DC * 3 * D], BF16, "wmod")
                with self.blk() as b:
                    @b.gpsimd
                    def _(g):
                        g = self.ser(g)
                        g.dma_start(out=wmod_sb.rearrange("p (c e) -> p c e", c=DC),
                                    in_=din[wname][:, eh * 3072:(eh + 1) * 3072]
                                    .rearrange("(c p) e -> p c e", p=128)).then_inc(sem["dmag"], 16)
                        self.bump("dmag", 16)
                        g.wait_ge(sem["dmag"], self.cnt["dmag"])
                t0 = self.cnt["td"]
                v0 = self.cnt["vd"]
                d0 = self.cnt["dmas"]
                with self.blk() as b:
                    @b.tensor
                    def _(t):
                        for et in range(6):
                            if et >= 2:
                                t.wait_ge(sem["vd"], v0 + et - 1)
                            pb = self.psum[et % 2]
                            for dc in range(DC):
                                ins = t.matmul(pb[0:2, :],
                                               self.csil[:, (l * DC + dc) * 2:(l * DC + dc) * 2 + 2],
                                               wmod_sb[:, dc * 3072 + et * 512: dc * 3072 + (et + 1) * 512],
                                               start=(dc == 0), stop=(dc == DC - 1))
                            ins.then_inc(sem["td"], 1)

                    p0 = [self.cnt["dma0"], self.cnt["dma1"]]

                    @b.vector
                    def _(v):
                        v = self.ser(v)
                        for et in range(6):
                            v.wait_ge(sem["td"], t0 + et + 1)
                            if et >= 2:
                                v.wait_ge(sem["dma%d" % (et % 2)], p0[et % 2] + 16 * ((et - 2) // 2 + 1))
                            v.tensor_copy(self.mod_stage[0:2, (et % 2) * 512:(et % 2) * 512 + 512],
                                          self.psum[et % 2][0:2, :])
                            self.chain(v, "vd", 1)

                    @b.sync
                    def _(s):
                        for et in range(6):
                            s.wait_ge(sem["vd"], v0 + et + 1)
                            off = l * 6144 + eh * 3072 + et * 512
                            s.dma_start(out=self.mod_d[0:2, off: off + 512],
                                        in_=self.mod_stage[0:2, (et % 2) * 512:(et % 2) * 512 + 512]
                                        ).then_inc(sem["dma%d" % (et % 2)], 16)
                        s.wait_ge(sem["dma0"], p0[0] + 3 * 16)
                        s.wait_ge(sem["dma1"], p0[1] + 3 * 16)
                self.cnt["td"] = t0 + 6
                self.cnt["dma0"] += 3 * 16
                self.cnt["dma1"] += 3 * 16
                phm.close()
                A.pop()
        phms.close()

        if "t_mod" in self.taps:
            with self.blk() as b:
                @b.sync
                def _(s):
                    s = self.ser(s)
                    s.dma_start(out=self.taps["t_mod"], in_=self.mod_d).then_inc(sem["dmas"], 16)
                    self.bump("dmas", 16)
                    s.wait_ge(sem["dmas"], self.cnt["dmas"])

    def emit_bcast(self, l, specs):
        sem, A = self.sem, self.A
        A.push()
        sg = A.f32(512, "sg")
        sg2 = A.f32(512, "sg2")
        mod_row = A.f32(1024, "mrow")
        d0 = self.cnt["dmar"]
        t0 = self.cnt["td"]
        v0 = self.cnt["vd"]
        pieces = [(vec, slot, tf, bi) for vec, slot, tf in specs for bi in range(B)]
        with self.blk() as b:
            @b.sync
            def _(s):
                s = self.ser(s)
                for i, (vec, slot, tf, bi) in enumerate(pieces):
                    if i >= 1:
                        s.wait_ge(sem["td"], t0 + i)
                    s.dma_start(out=mod_row[0:1, :],
                                in_=self.mod_d[bi:bi + 1, l * 6144 + vec * D: l * 6144 + (vec + 1) * D]
                                ).then_inc(sem["dmar"], 16)

            @b.tensor
            def _(t):
                t = self.ser(t)
                for i, (vec, slot, tf, bi) in enumerate(pieces):
                    t.wait_ge(sem["dmar"], d0 + 16 * (i + 1))
                    if i >= 2:
                        t.wait_ge(sem["vd"], v0 + i - 1)
                    for half in range(2):
                        pb = self.psum[(2 * i + half) % 4]
                        ins = t.matmul(pb[:, :], self.ones1f[:],
                                 mod_row[0:1, half * 512:(half + 1) * 512], start=True, stop=True)
                    ins.then_inc(sem["td"], 1)

            @b.vector
            def _(v):
                v = self.ser(v)
                for i, (vec, slot, tf, bi) in enumerate(pieces):
                    v.wait_ge(sem["td"], t0 + i + 1)
                    dst = self.bc[:, (slot + bi) * D:(slot + bi + 1) * D]
                    for half in range(2):
                        pb = self.psum[(2 * i + half) % 4]
                        dh = dst[:, half * 512:(half + 1) * 512]
                        if tf == "plus1":
                            v.tensor_scalar_add(dh, pb[:, :], 1.0)
                        elif tf == "signmax":
                            v.tensor_scalar(sg2, pb[:, :], -1.0, None, ALU.mult)
                            v.tensor_tensor(dh, pb[:, :], sg2, ALU.max)
                            v.tensor_scalar_max(dh, dh, 0.25)
                            v.tensor_scalar(sg, pb[:, :], 0.0, 2.0, ALU.is_ge, ALU.mult)
                            v.tensor_scalar_add(sg, sg, -1.0)
                            v.tensor_tensor(dh, dh, sg, ALU.mult)
                        else:
                            v.tensor_copy(dh, pb[:, :])
                    self.chain(v, "vd", 1)
        self.cnt["dmar"] = d0 + 16 * len(pieces)
        self.cnt["td"] = t0 + len(pieces)
        A.pop()

    def emit_ln_mod(self, h_bf_flat, slot_scale, slot_shift, tiles=None, tap=None):
        nc, A, sem = self.nc, self.A, self.sem
        if tiles is None:
            tiles = list(range(NT))
        ntl = len(tiles)
        A.push()
        sum_c = A.f32(ntl, "sum_c")
        sq_c = A.f32(ntl, "sq_c")
        rstd_c = A.f32(ntl, "rstd_c")
        sqs = [A.f32(D, "sq_scratch0"), A.f32(D, "sq_scratch1")]
        xn = [A.f32(D, "xn0"), A.f32(D, "xn1")]
        with self.blk() as b:
            @b.vector
            def _(v):
                v = self.ser(v)
                for i, t in enumerate(tiles):
                    v.reduce_sum(sum_c[:, i:i + 1], self.x_sb[:, t * D:(t + 1) * D], AX.X)

            @b.scalar
            def _(s):
                s = self.ser(s)
                for i, t in enumerate(tiles):
                    s.activation(sqs[i % 2], self.x_sb[:, t * D:(t + 1) * D], AF.Square,
                                 accum_out=sq_c[:, i:i + 1])
        with self.blk() as b:
            @b.vector
            def _(v):
                v = self.ser(v)
                v.tensor_scalar_mul(sum_c, sum_c, 1.0 / D)
                v.tensor_scalar_mul(sq_c, sq_c, 1.0 / D)
                v.tensor_tensor(rstd_c, sum_c, sum_c, ALU.mult)
                v.tensor_tensor(sq_c, sq_c, rstd_c, ALU.subtract)
        with self.blk() as b:
            @b.scalar
            def _(s):
                s = self.ser(s)
                s.activation(sq_c, sq_c, AF.Sqrt, bias=self.eps_col[:])
        with self.blk() as b:
            @b.vector
            def _(v):
                v = self.ser(v)
                v.reciprocal(rstd_c, sq_c)
                for i, t in enumerate(tiles):
                    bi = 0 if t < NT // 2 else 1
                    sc = self.bc[:, (slot_scale + bi) * D:(slot_scale + bi + 1) * D]
                    sh = self.bc[:, (slot_shift + bi) * D:(slot_shift + bi + 1) * D]
                    v.tensor_scalar(xn[i % 2], self.x_sb[:, t * D:(t + 1) * D],
                                    sum_c[:, i:i + 1], rstd_c[:, i:i + 1], ALU.subtract, ALU.mult)
                    v.tensor_tensor(xn[i % 2], xn[i % 2], sc, ALU.mult)
                    v.tensor_tensor(h_bf_flat[:, i * D:(i + 1) * D], xn[i % 2], sh, ALU.add)
        if tap is not None:
            self.tap_bf16(tap, h_bf_flat, ntl * D)
        A.pop()

    def xbar_blocks(self, dst_bf, src_bf):
        sem = self.sem
        with self.blk() as b:
            @b.scalar
            def _(s):
                s = self.ser(s)
                s.dma_start(out=dst_bf.rearrange("p (n f) -> p n f", f=128),
                            in_=src_bf.rearrange("p (n f) -> p n f", f=128),
                            transpose=True).then_inc(sem["dmax"], 16)
                self.bump("dmax", 16)
                s.wait_ge(sem["dmax"], self.cnt["dmax"])

    def load_w_bf16(self, dst_bf, src_ap, nchunk, ncol):
        sem = self.sem
        with self.blk() as b:
            @b.gpsimd
            def _(g):
                g = self.ser(g)
                g.dma_start(out=dst_bf.rearrange("p (c e) -> p c e", c=nchunk),
                            in_=src_ap.rearrange("(c p) e -> p c e", p=128)
                            ).then_inc(sem["dmag"], 16)
                self.bump("dmag", 16)
                g.wait_ge(sem["dmag"], self.cnt["dmag"])

    def proj_tm(self, h_bt, w_bf, dst_tm, btiles, nc_blk=DC):
        groups = []
        for ti, doff in btiles:
            for eh in range(2):
                def mme(t, pb, ti=ti, eh=eh):
                    r = None
                    for dc in range(DC):
                        r = t.matmul(pb[:, :],
                                     h_bt[:, (ti * nc_blk + dc) * 128:(ti * nc_blk + dc + 1) * 128],
                                     w_bf[:, dc * 1024 + eh * 512: dc * 1024 + (eh + 1) * 512],
                                     start=(dc == 0), stop=(dc == DC - 1))
                    return r

                def eve(v, pb, doff=doff, eh=eh):
                    v.tensor_copy(dst_tm[:, doff + eh * 512: doff + (eh + 1) * 512], pb[:, :])
                groups.append((mme, eve))
        self.mm_pipeline(groups, self.psum[:4])

    def layer0(self):
        A, sem, nc = self.A, self.sem, self.nc
        if "t_h1" in self.taps:
            from contextlib import ExitStack as _ES
            self.emit_bcast(0, [(1, 0, "plus1"), (0, 2, None)])
            with _ES() as pht:
                h1 = self.tmp_sbuf(pht, [128, NT * D], BF16, "h1tap")
                self.emit_ln_mod(h1, 0, 2, tap=self.taps.get("t_h1"))
        for bi in range(B):
            self.l0_attn_batch(bi)
        if "t_xa" in self.taps:
            self.tap_f32(self.taps["t_xa"].rearrange("(t p) d -> p t d", p=128),
                         self.x_sb[:].rearrange("p (t d) -> p t d", d=D))
        self.l0_ffn()
        if "t_x0" in self.taps:
            self.tap_f32(self.taps["t_x0"].rearrange("(t p) d -> p t d", p=128),
                         self.x_sb[:].rearrange("p (t d) -> p t d", d=D))

    def l0_attn_batch(self, bi):
        A, sem, nc = self.A, self.sem, self.nc
        din, psum = self.din, self.psum
        from contextlib import ExitStack as _ES
        gtiles = [bi * 8 + tt for tt in range(8)]
        ph = _ES()
        A.push()
        self.emit_bcast(0, [(1, 0, "plus1"), (0, 2, None)])
        ktm = self.tmp_sbuf(ph, [128, 8 * D], BF16, "ktm")
        vtm = self.tmp_sbuf(ph, [128, 8 * D], BF16, "vtm")

        with _ES() as ph2:
            h1fb = self.tmp_sbuf(ph2, [128, 8 * D], BF16, "h1fb", side="right")
            with _ES() as ph3:
                h1b = self.tmp_sbuf(ph3, [128, 8 * D], BF16, "h1b", side="right")
                self.emit_ln_mod(h1b, 0, 2, tiles=gtiles)
                if bi == 0 and "t_hb" in self.taps:
                    self.tap_bf16(self.taps["t_hb"], h1b, 8 * D)
                self.xbar_blocks(h1fb, h1b)
            if bi == 0 and "t_hf" in self.taps:
                self.tap_bf16(self.taps["t_hf"], h1fb, 8 * D)
            qtm = self.tmp_sbuf(ph, [128, 8 * D], BF16, "qtm")
            A.push()
            wbuf = A.bf16(DC * D, "wbuf")
            btiles = [(tt, tt * 1024) for tt in range(8)]
            self.load_w_bf16(wbuf, din["wkT"], DC, D)
            self.proj_tm(h1fb, wbuf, ktm, btiles)
            self.load_w_bf16(wbuf, din["wvT"], DC, D)
            self.proj_tm(h1fb, wbuf, vtm, btiles)
            self.load_w_bf16(wbuf, din["wqT"], DC, D)
            self.proj_tm(h1fb, wbuf, qtm, btiles)
            A.pop()
        if bi == 0:
            if "t_k0" in self.taps:
                self.tap_bf16(self.taps["t_k0"], ktm, 8 * D)
            if "t_q0" in self.taps:
                self.tap_bf16(self.taps["t_q0"], qtm, 8 * D)

        kbt = self.tmp_sbuf(ph, [128, 8 * D], BF16, "kbt")
        qbt = self.tmp_sbuf(ph, [128, 8 * D], BF16, "qbt")
        A.push()
        updb = A.bf16(8 * 128, "updb")
        preb = A.bf16(8 * 128, "preb")
        sgath = A.f32(8 * 128, "sgath")
        accp = [A.f32(128, "acc0"), A.f32(128, "acc1")]
        zg = A.f32(128, "zg")
        zg_bf = A.bf16(128, "zg_bf")
        colsum = A.f32(8 * 128, "colsum")
        zpre = A.f32(8 * 128, "zpre")
        zpre_bf = A.bf16(8 * 128, "zpre_bf")
        o_f = A.f32(8 * 128, "o_f")
        qzp = [A.f32(128, "qzp0"), A.f32(128, "qzp1")]
        qz = A.f32(8, "qz")
        rden = A.f32(8, "rden")
        ss = A.f32(8, "ss")
        rr = A.f32(8, "rr")
        stg = A.f32(128, "stg")
        zst = A.f32(128, "zst")
        sct = [A.bf16(128, "sct0"), A.bf16(128, "sct1")]
        self.xbar_blocks(kbt, ktm)
        self.xbar_blocks(qbt, qtm)

        cc_in = self.cc_at_in[bi]
        cc_out = self.cc_at_out[bi]

        for h in range(H):
            hc = h * 128
            groups = []
            for c in range(8):
                def mme(t, pb, c=c, hc=hc):
                    return t.matmul(pb[:, 0:128],
                                    ktm[:, c * 1024 + hc: c * 1024 + hc + 128],
                                    vtm[:, c * 1024 + hc: c * 1024 + hc + 128],
                                    start=True, stop=True)

                def eve(v, pb, c=c):
                    v.tensor_copy(updb[:, c * 128:(c + 1) * 128], pb[:, 0:128])
                groups.append((mme, eve))
            self.mm_pipeline(groups, psum[:4])
            t0 = self.cnt["td"]
            v0 = self.cnt["vd"]
            with self.blk() as b:
                @b.tensor
                def _(t):
                    r = None
                    for c in range(8):
                        r = t.matmul(psum[4][:, 0:128], self.ident[:], updb[:, c * 128:(c + 1) * 128],
                                     start=(c == 0), stop=(c == 7))
                    r.then_inc(sem["td"], 1)
                    for tt in range(8):
                        r = t.matmul(psum[5][0:1, 0:128], self.onescol_bf[:],
                                     ktm[:, tt * 1024 + hc: tt * 1024 + hc + 128],
                                     start=(tt == 0), stop=(tt == 7))
                    r.then_inc(sem["td"], 1)
                self.bump("td", 2)

                @b.vector
                def _(v):
                    v = self.ser(v)
                    v.wait_ge(sem["td"], t0 + 1)
                    if h > 0:
                        v.wait_ge(sem["dmas"], self.cnt["dmas"])
                    v.tensor_copy(stg, psum[4][:, 0:128])
                    v.wait_ge(sem["td"], t0 + 2)
                    v.tensor_copy(zst[0:1, :], psum[5][0:1, 0:128])
                    self.chain(v, "vd", 1)

                @b.sync
                def _(s):
                    s.wait_ge(sem["vd"], self.cnt["vd"])
                    s.dma_start(out=cc_in[h * 128:(h + 1) * 128, :], in_=stg).then_inc(sem["dmas"], 16)
                    s.dma_start(out=cc_in[H * 128 + h:H * 128 + h + 1, :], in_=zst[0:1, :]).then_inc(sem["dmas"], 16)
                    self.bump("dmas", 32)
        with self.blk() as b:
            @b.gpsimd
            def _(g):
                g.wait_ge(sem["dmas"], self.cnt["dmas"])
                g.collective_compute(
                    "AllGather", ALU.bypass, replica_groups=[list(range(NCORE))],
                    ins=[cc_in.opt()], outs=[cc_out.opt()],
                ).then_inc(sem["cc"], 1)
                self.bump("cc", 1)
        with self.blk() as b:
            @b.gpsimd
            def _(g):
                g.wait_ge(sem["cc"], self.cnt["cc"])

        for h in range(H):
            self._l0_attn_head(bi, h, ktm, vtm, qtm, kbt, qbt, updb, preb, sgath,
                               accp, zg, zg_bf, colsum, zpre, zpre_bf, o_f, qzp, qz,
                               rden, ss, rr, sct, cc_out)
        if bi == 0 and "t_o0" in self.taps:
            self.tap_bf16(self.taps["t_o0"], qtm, 8 * D)
        A.pop()

        self.emit_bcast(0, [(2, 0, "signmax")])
        obt = kbt
        A.push()
        wbuf = A.bf16(DC * D, "wbuf2")
        epi = A.f32(512, "epi")
        self.xbar_blocks(obt, qtm)
        self.load_w_bf16(wbuf, din["woT"], DC, D)
        groups2 = []
        for tt in range(8):
            for dh in range(2):
                def mme(t, pb, tt=tt, dh=dh):
                    r = None
                    for ec in range(8):
                        r = t.matmul(pb[:, :],
                                     obt[:, (tt * 8 + ec) * 128:(tt * 8 + ec + 1) * 128],
                                     wbuf[:, ec * 1024 + dh * 512: ec * 1024 + (dh + 1) * 512],
                                     start=(ec == 0), stop=(ec == 7))
                    return r

                def eve(v, pb, tt=tt, dh=dh, epi=epi):
                    gt = bi * 8 + tt
                    ga = self.bc[:, bi * D + dh * 512: bi * D + (dh + 1) * 512]
                    xt = self.x_sb[:, gt * D + dh * 512: gt * D + (dh + 1) * 512]
                    v.tensor_tensor(epi, pb[:, :], ga, ALU.mult)
                    v.tensor_tensor(xt, xt, epi, ALU.add)
                groups2.append((mme, eve))
        self.mm_pipeline(groups2, psum[:4])
        A.pop()
        A.pop()
        ph.close()

    def _l0_attn_head(self, bi, h, ktm, vtm, qtm, kbt, qbt, updb, preb, sgath,
                      accp, zg, zg_bf, colsum, zpre, zpre_bf, o_f, qzp, qz,
                      rden, ss, rr, sct, cc_out):
        A, sem, nc, psum = self.A, self.sem, self.nc, self.psum
        hc = h * 128
        ROWS = H * 128 + H

        def kblk(c):
            return kbt[:, (c * H + h) * 128:(c * H + h + 1) * 128]

        def qblk(c):
            return qbt[:, (c * H + h) * 128:(c * H + h + 1) * 128]

        groups = []
        for c in range(8):
            def mme(t, pb, c=c, hc=hc):
                return t.matmul(pb[:, 0:128],
                                ktm[:, c * 1024 + hc: c * 1024 + hc + 128],
                                vtm[:, c * 1024 + hc: c * 1024 + hc + 128],
                                start=True, stop=True)

            def eve(v, pb, c=c):
                v.tensor_copy(updb[:, c * 128:(c + 1) * 128], pb[:, 0:128])
            groups.append((mme, eve))
        self.mm_pipeline(groups, psum[:4])

        gsrc = cc_out.rearrange("(j r) e -> r j e", j=NCORE)
        with self.blk() as b:
            @b.sync
            def _(s):
                s.dma_start(out=sgath.rearrange("p (j e) -> p j e", j=NCORE),
                            in_=gsrc[h * 128:(h + 1) * 128]).then_inc(sem["dmas"], 16)
                s.dma_start(out=zg[0:8, :],
                            in_=cc_out.rearrange("(j r) e -> j r e", j=NCORE)
                            [:, H * 128 + h:H * 128 + h + 1, :].squeeze(1)).then_inc(sem["dmas"], 16)
                self.bump("dmas", 32)
                s.wait_ge(sem["dmas"], self.cnt["dmas"])

        with self.blk() as b:
            @b.vector
            def _(v):
                v = self.ser(v)
                v.memset(accp[0], 0.0)
                for j in range(NCORE):
                    v.scalar_tensor_tensor(accp[(j + 1) % 2], sgath[:, j * 128:(j + 1) * 128],
                                           self.mlt[:, j:j + 1], accp[j % 2], ALU.mult, ALU.add)
                v.tensor_copy(preb[:, 0:128], accp[0])
                v.tensor_copy(zg_bf[0:8, :], zg[0:8, :])
                self.chain(v, "vd", 1)
            t0c = self.cnt["td"]
            v0c = self.cnt["vd"]

            @b.tensor
            def _(t):
                t.wait_ge(sem["vd"], v0c)
                t.matmul(psum[6][0:1, 0:128], self.mask8_bf[:], zg_bf[0:8, :],
                         start=True, stop=True).then_inc(sem["td"], 1)
                r = None
                for tt in range(4):
                    r = t.matmul(psum[7][0:1, tt * 128:(tt + 1) * 128],
                                 self.onescol_bf[:],
                                 ktm[:, tt * 1024 + hc: tt * 1024 + hc + 128],
                                 start=True, stop=True)
                r.then_inc(sem["td"], 1)
                t.wait_ge(sem["vd"], v0c + 1)
                for tt in range(4, 8):
                    r = t.matmul(psum[6][0:1, (tt - 4) * 128:(tt - 3) * 128],
                                 self.onescol_bf[:],
                                 ktm[:, tt * 1024 + hc: tt * 1024 + hc + 128],
                                 start=True, stop=True)
                r.then_inc(sem["td"], 1)
                self.bump("td", 3)

            @b.vector
            def _(v):
                v = self.ser(v)
                v.wait_ge(sem["td"], t0c + 2)
                v.tensor_copy(zpre[0:1, 0:128], psum[6][0:1, 0:128])
                v.tensor_copy(colsum[0:1, 0:512], psum[7][0:1, 0:512])
                self.chain(v, "vd", 1)
                v.wait_ge(sem["td"], t0c + 3)
                v.tensor_copy(colsum[0:1, 512:1024], psum[6][0:1, 0:512])
                self.chain(v, "vd", 1)
        with self.blk() as b:
            @b.vector
            def _(v):
                v = self.ser(v)
                for tt in range(1, 8):
                    v.tensor_tensor(zpre[0:1, tt * 128:(tt + 1) * 128],
                                    zpre[0:1, (tt - 1) * 128: tt * 128],
                                    colsum[0:1, (tt - 1) * 128: tt * 128], ALU.add)
                v.tensor_copy(zpre_bf[0:1, :], zpre[0:1, :])
                for c in range(1, 8):
                    v.tensor_tensor(accp[0], accp[0], updb[:, (c - 1) * 128: c * 128], ALU.add)
                    v.tensor_copy(preb[:, c * 128:(c + 1) * 128], accp[0])
        t0 = self.cnt["td"]
        v0 = self.cnt["vd"]
        with self.blk() as b:
            @b.tensor
            def _(t):
                for tt in range(8):
                    if tt >= 2:
                        t.wait_ge(sem["vd"], v0 + tt - 1)
                    pb = psum[tt % 2]
                    t.matmul(pb[:, 0:128], self.triu[:],
                             ktm[:, tt * 1024 + hc: tt * 1024 + hc + 128], start=True, stop=False)
                    t.matmul(pb[:, 0:128], self.ones1[:],
                             zpre_bf[0:1, tt * 128:(tt + 1) * 128],
                             start=False, stop=True).then_inc(sem["td"], 1)

            @b.vector
            def _(v):
                v = self.ser(v)
                for tt in range(8):
                    v.wait_ge(sem["td"], t0 + tt + 1)
                    pb = psum[tt % 2]
                    v.tensor_tensor(qzp[tt % 2], pb[:, 0:128],
                                    qtm[:, tt * 1024 + hc: tt * 1024 + hc + 128], ALU.mult)
                    v.reduce_sum(qz[:, tt:tt + 1], qzp[tt % 2], AX.X)
                    self.chain(v, "vd", 1)
        self.cnt["td"] = t0 + 8
        with self.blk() as b:
            @b.vector
            def _(v):
                v = self.ser(v)
                v.tensor_scalar(rr, qz, -1.0, None, ALU.mult)
                v.tensor_tensor(rden, qz, rr, ALU.max)
                v.tensor_scalar_max(rden, rden, 1.0)
                v.reciprocal(rden, rden)
        t0 = self.cnt["td"]
        v0 = self.cnt["vd"]
        with self.blk() as b:
            @b.tensor
            def _(t):
                for c in range(8):
                    if c >= 2:
                        t.wait_ge(sem["vd"], v0 + 2 * (c - 2) + 1)
                    t.matmul(psum[c % 2][:, 0:128], kblk(c), qblk(c),
                             start=True, stop=True).then_inc(sem["td"], 1)
                    t.wait_ge(sem["vd"], v0 + 2 * c + 1)
                    if c >= 2:
                        t.wait_ge(sem["vd"], v0 + 2 * (c - 2) + 2)
                    t.matmul(psum[2 + c % 2][:, 0:128], sct[c % 2][:],
                             vtm[:, c * 1024 + hc: c * 1024 + hc + 128],
                             start=True, stop=False)
                    t.matmul(psum[2 + c % 2][:, 0:128], qblk(c), preb[:, c * 128:(c + 1) * 128],
                             start=False, stop=True).then_inc(sem["td"], 1)

            @b.vector
            def _(v):
                v = self.ser(v)
                for c in range(8):
                    v.wait_ge(sem["td"], t0 + 2 * c + 1)
                    v.tensor_tensor(sct[c % 2][:], psum[c % 2][:, 0:128],
                                    self.triu[:], ALU.mult)
                    self.chain(v, "vd", 1)
                    v.wait_ge(sem["td"], t0 + 2 * c + 2)
                    v.tensor_scalar(o_f[:, c * 128:(c + 1) * 128],
                                    psum[2 + c % 2][:, 0:128],
                                    rden[:, c:c + 1], None, ALU.mult)
                    self.chain(v, "vd", 1)
        self.cnt["td"] = t0 + 16
        with self.blk() as b:
            @b.scalar
            def _(s):
                s = self.ser(s)
                for tt in range(8):
                    s.activation(qzp[tt % 2], o_f[:, tt * 128:(tt + 1) * 128], AF.Square,
                                 accum_out=ss[:, tt:tt + 1])
        with self.blk() as b:
            @b.vector
            def _(v):
                v = self.ser(v)
                v.tensor_scalar_mul(ss, ss, 1.0 / 128.0)
                self.chain(v, "vd", 1)

            @b.scalar
            def _(s):
                s = self.ser(s)
                s.wait_ge(sem["vd"], self.cnt["vd"])
                s.activation(ss, ss, AF.Sqrt, bias=self.epsr_col[:])
                self.chain(s, "sd", 1)
        with self.blk() as b:
            @b.vector
            def _(v):
                v = self.ser(v)
                v.reciprocal(rr, ss)
                for tt in range(8):
                    v.scalar_tensor_tensor(qtm[:, tt * 1024 + hc: tt * 1024 + hc + 128],
                                           o_f[:, tt * 128:(tt + 1) * 128],
                                           rr[:, tt:tt + 1], self.normw[:],
                                           ALU.mult, ALU.mult)

    def l0_ffn(self):
        from contextlib import ExitStack as _ES
        A, sem, nc = self.A, self.sem, self.nc
        din, psum = self.din, self.psum
        ph = _ES()
        A.push()

        self.emit_bcast(0, [(4, 0, "plus1"), (3, 2, None)])
        h2bt = self.tmp_sbuf(ph, [128, NT * D], BF16, "h2bt")
        with _ES() as ph2:
            h2tm = self.tmp_sbuf(ph2, [128, NT * D], BF16, "h2tm", side="right")
            self.emit_ln_mod(h2tm, 0, 2, tap=self.taps.get("t_h2"))
            self.xbar_blocks(h2bt, h2tm)
            self.emit_bcast(0, [(5, 0, "signmax")])
            with self.blk() as b:
                @b.vector
                def _(v):
                    v = self.ser(v)
                    tmp = A.f32(D, "basetmp")
                    for t in range(NT):
                        bi = 0 if t < 8 else 1
                        gb = self.bc[:, bi * D:(bi + 1) * D]
                        v.tensor_tensor(tmp, h2tm[:, t * D:(t + 1) * D], gb, ALU.mult)
                        v.tensor_tensor(self.x_sb[:, t * D:(t + 1) * D],
                                        self.x_sb[:, t * D:(t + 1) * D], tmp, ALU.add)
        with self.blk() as b:
            @b.sync
            def _(s):
                s.dma_start(out=self.spill.rearrange("(t p) d -> p t d", p=128),
                            in_=self.x_sb[:].rearrange("p (t d) -> p t d", d=D)).then_inc(sem["dmas"], 16)
                self.bump("dmas", 16)
                s.wait_ge(sem["dmas"], self.cnt["dmas"])

        rt_bf = A.bf16(DC * 8, "rt_bf")
        rtemp = A.f32(1, "rtemp")
        rbt = A.f32(8, "rbt")
        scores = A.f32(NT * 8, "scores")
        with self.blk() as b:
            @b.gpsimd
            def _(g):
                g = self.ser(g)
                g.dma_start(out=rt_bf.rearrange("p (c e) -> p c e", c=DC),
                            in_=din["routerT"].rearrange("(c p) e -> p c e", p=128)
                            ).then_inc(sem["dmag"], 16)
                self.bump("dmag", 16)
                g.wait_ge(sem["dmag"], self.cnt["dmag"])

            @b.vector
            def _(v):
                v = self.ser(v)
                v.tensor_scalar_max(rtemp, self.tcol[:], 0.1)
                v.reciprocal(rtemp, rtemp)
                v.tensor_scalar(rbt, self.rb_b[:], rtemp, None, ALU.mult)
        t0 = self.cnt["td"]
        v0 = self.cnt["vd"]
        with self.blk() as b:
            @b.tensor
            def _(t):
                for tt in range(NT):
                    if tt >= 4:
                        t.wait_ge(sem["vd"], v0 + tt - 3)
                    pb = psum[tt % 4]
                    r = None
                    for dc in range(DC):
                        r = t.matmul(pb[:, 0:8],
                                     h2bt[:, (tt * DC + dc) * 128:(tt * DC + dc + 1) * 128],
                                     rt_bf[:, dc * 8:(dc + 1) * 8],
                                     start=(dc == 0), stop=(dc == DC - 1))
                    r.then_inc(sem["td"], 1)

            @b.vector
            def _(v):
                v = self.ser(v)
                for tt in range(NT):
                    v.wait_ge(sem["td"], t0 + tt + 1)
                    v.scalar_tensor_tensor(scores[:, tt * 8:(tt + 1) * 8], psum[tt % 4][:, 0:8],
                                           rtemp, rbt, ALU.mult, ALU.add)
                    self.chain(v, "vd", 1)
        self.cnt["td"] = t0 + NT

        if "t_sc" in self.taps:
            self.tap_f32(self.taps["t_sc"], scores)
        self.l0_sinkhorn(scores)
        ws = scores
        if "t_ws" in self.taps:
            self.tap_f32(self.taps["t_ws"], ws)

        part = self.x_sb[:, 0:8192].bitcast(BF16)
        wg_sb = self.tmp_sbuf(ph, [128, DC * 1024], BF16, "wgh")
        wp_sb = self.tmp_sbuf(ph, [128, DC * 1024], BF16, "wph")
        wo_sb = self.tmp_sbuf(ph, [128, 8 * 1024], BF16, "woh")
        wg2 = self.x_sb[:, 8192:12288].bitcast(BF16)
        wp2 = self.x_sb[:, 12288:16384].bitcast(BF16)
        hloc = A.bf16(8 * 512, "hloc")
        pcop = [A.bf16(512, "pcop0"), A.bf16(512, "pcop1")]
        e1 = A.f32(512, "e1")
        esl = A.f32(4 * 512, "esl")
        pairs = [(e, fh) for e in range(E) for fh in range(2)]

        def issue_gp(g, i):
            e, fh = pairs[i]
            wgd = wg_sb if i % 2 == 0 else wg2
            wpd = wp_sb if i % 2 == 0 else wp2
            g.dma_start(out=wgd.rearrange("p (c f) -> p c f", c=DC),
                        in_=din["wg"][e, :, fh * 1024:(fh + 1) * 1024]
                        .rearrange("(c p) f -> p c f", p=128)).then_inc(sem["dmar"], 16)
            g.dma_start(out=wpd.rearrange("p (c f) -> p c f", c=DC),
                        in_=din["wp"][e, :, fh * 1024:(fh + 1) * 1024]
                        .rearrange("(c p) f -> p c f", p=128)).then_inc(sem["dmar"], 16)
            self.bump("dmar", 32)

        def issue_wo(g, i):
            e, fh = pairs[i]
            g.dma_start(out=wo_sb.rearrange("p (c f) -> p c f", c=8),
                        in_=din["wo_e"][e, fh * 1024:(fh + 1) * 1024, :]
                        .rearrange("(c p) f -> p c f", p=128)).then_inc(sem["dmar"], 16)
            self.bump("dmar", 16)

        gp_ready = {}
        wo_ready = {}
        c0 = self.cnt["dmar"]
        with self.blk() as b:
            @b.gpsimd
            def _(g):
                g = self.ser(g)
                issue_gp(g, 0)
                issue_wo(g, 0)
        gp_ready[0] = c0 + 32
        wo_ready[0] = c0 + 48
        for i, (e, fh) in enumerate(pairs):
            wgd = wg_sb if i % 2 == 0 else wg2
            wpd = wp_sb if i % 2 == 0 else wp2
            if i >= 1:
                wo_ready[i] = self.cnt["dmar"] + 16
            if i + 1 < len(pairs):
                gp_ready[i + 1] = self.cnt["dmar"] + (16 if i >= 1 else 0) + 32
            for g4 in range(4):
                pf = None
                if g4 == 0:
                    def pf(g, i=i):
                        if i >= 1:
                            issue_wo(g, i)
                        if i + 1 < len(pairs):
                            issue_gp(g, i + 1)
                self.l0_moe_group(e, fh, g4, h2bt, wgd, wpd, wo_sb, hloc,
                                  pcop, part, e1, esl, ws,
                                  wait_gp=(gp_ready[i] if g4 == 0 else None),
                                  wait_wo=(wo_ready[i] if g4 == 0 else None),
                                  prefetch=pf)
        ph.close()
        A.pop()
        A.push()
        accb = A.f32(D, "accb")
        tmp2 = A.f32(D, "tmp2")
        d0 = self.cnt["dmas"]
        with self.blk() as b:
            @b.sync
            def _(s):
                s.dma_start(out=self.x_sb[:].rearrange("p (t d) -> p t d", d=D),
                            in_=self.spill.rearrange("(t p) d -> p t d", p=128)).then_inc(sem["dmas"], 16)
                self.bump("dmas", 16)
                s.wait_ge(sem["dmas"], self.cnt["dmas"])
        p0 = [self.cnt["dma0"], self.cnt["dma1"]]
        v0 = self.cnt["vd"]
        with self.blk() as b:
            @b.sync
            def _(s):
                for t in range(NT):
                    if t >= 2:
                        s.wait_ge(sem["vd"], v0 + t - 1)
                    s.dma_start(out=accb if t % 2 == 0 else tmp2,
                                in_=self.acc_d[t * 128:(t + 1) * 128, :]
                                ).then_inc(sem["dma%d" % (t % 2)], 16)

            @b.vector
            def _(v):
                v = self.ser(v)
                gbscr = A.f32(D, "gbscr")
                for t in range(NT):
                    bi = 0 if t < 8 else 1
                    gb = self.bc[:, bi * D:(bi + 1) * D]
                    v.wait_ge(sem["dma%d" % (t % 2)], p0[t % 2] + 16 * (t // 2 + 1))
                    v.tensor_tensor(gbscr, accb if t % 2 == 0 else tmp2, gb, ALU.mult)
                    v.tensor_tensor(self.x_sb[:, t * D:(t + 1) * D],
                                    self.x_sb[:, t * D:(t + 1) * D], gbscr, ALU.add)
                    self.chain(v, "vd", 1)
        self.cnt["dma0"] = p0[0] + 16 * 8
        self.cnt["dma1"] = p0[1] + 16 * 8
        A.pop()

    def l0_moe_group(self, e, fh, g4, h2bt, wg_sb, wp_sb, wo_sb, hloc, pcop,
                     part, e1, esl, ws, wait_gp=None, wait_wo=None, prefetch=None):
        sem, psum = self.sem, self.psum
        t0 = self.cnt["td"]
        v0 = self.cnt["vd"]
        s0 = self.cnt["sd"]
        g0 = self.cnt["dmag"]
        with self.blk() as b:
            @b.tensor
            def _(t):
                if wait_gp is not None:
                    t.wait_ge(sem["dmar"], wait_gp)
                for fc in range(8):
                    if fc >= 1:
                        t.wait_ge(sem["sd"], s0 + fc)
                        t.wait_ge(sem["vd"], v0 + fc)
                    r = None
                    for dc in range(DC):
                        rhs = h2bt.rearrange("p (t c f) -> p t c f", c=DC, f=128)[
                            :, g4 * 4:(g4 + 1) * 4, dc:dc + 1, :].squeeze(2)
                        r = t.matmul(psum[4][:, :], wg_sb[:, dc * 1024 + fc * 128: dc * 1024 + fc * 128 + 128],
                                     rhs, start=(dc == 0), stop=(dc == DC - 1))
                    for dc in range(DC):
                        rhs = h2bt.rearrange("p (t c f) -> p t c f", c=DC, f=128)[
                            :, g4 * 4:(g4 + 1) * 4, dc:dc + 1, :].squeeze(2)
                        r = t.matmul(psum[5][:, :], wp_sb[:, dc * 1024 + fc * 128: dc * 1024 + fc * 128 + 128],
                                     rhs, start=(dc == 0), stop=(dc == DC - 1))
                    r.then_inc(sem["td"], 1)
                if wait_wo is not None:
                    t.wait_ge(sem["dmar"], wait_wo)
                t.wait_ge(sem["vd"], v0 + 8)
                for dh in range(2):
                    if dh == 1:
                        t.wait_ge(sem["vd"], v0 + 8 + 4)
                    r = None
                    for tt in range(4):
                        for fc in range(8):
                            r = t.matmul(psum[tt][:, :],
                                         hloc[:, fc * 512 + tt * 128: fc * 512 + tt * 128 + 128],
                                         wo_sb[:, fc * 1024 + dh * 512: fc * 1024 + (dh + 1) * 512],
                                         start=(fc == 0), stop=(fc == 7))
                        r.then_inc(sem["td"], 1)

            @b.scalar
            def _(s):
                s = self.ser(s)
                for fc in range(8):
                    s.wait_ge(sem["td"], t0 + fc + 1)
                    s.copy(pcop[fc % 2], psum[5][:, :])
                    self.chain(s, "sd", 1)

            @b.vector
            def _(v):
                v = self.ser(v)
                for fc in range(8):
                    v.wait_ge(sem["sd"], s0 + fc + 1)
                    v.scalar_tensor_tensor(hloc[:, fc * 512:(fc + 1) * 512], psum[4][:, :],
                                           0.0, pcop[fc % 2], ALU.max, ALU.mult)
                    self.chain(v, "vd", 1)
                for dh in range(2):
                    if fh == 1 and dh == 1:
                        v.wait_ge(sem["dmag"], g0 + 16)
                    for tt in range(4):
                        v.wait_ge(sem["td"], t0 + 8 + dh * 4 + tt + 1)
                        gt = g4 * 4 + tt
                        if fh == 0:
                            v.tensor_copy(part[:, gt * 1024 + dh * 512: gt * 1024 + (dh + 1) * 512],
                                          psum[tt][:, :])
                        else:
                            v.tensor_tensor(e1, psum[tt][:, :],
                                            part[:, gt * 1024 + dh * 512: gt * 1024 + (dh + 1) * 512],
                                            ALU.add)
                            v.tensor_scalar(esl[:, tt * 512:(tt + 1) * 512], e1,
                                            ws[:, gt * 8 + e: gt * 8 + e + 1], None, ALU.mult)
                        self.chain(v, "vd", 1)

            @b.gpsimd
            def _(g):
                g = self.ser(g)
                if prefetch is not None:
                    prefetch(g)
                if fh == 1:
                    for dh in range(2):
                        g.wait_ge(sem["vd"], v0 + 8 + dh * 4 + 4)
                        g.dma_start(out=self.acc_d[g4 * 512:(g4 + 1) * 512, dh * 512:(dh + 1) * 512]
                                    .rearrange("(t p) d -> p t d", p=128),
                                    in_=esl.rearrange("p (t d) -> p t d", d=512),
                                    accum_op=(ALU.bypass if e == 0 else ALU.add)
                                    ).then_inc(sem["dmag"], 16)
                        self.bump("dmag", 16)
                    g.wait_ge(sem["dmag"], self.cnt["dmag"])
        self.cnt["td"] = t0 + 16

    def l0_sinkhorn(self, scores):
        A, sem, nc, psum = self.A, self.sem, self.nc, self.psum
        A.push()
        m16 = A.f32(16, "m16")
        n16 = A.f32(16, "n16")
        se16 = A.f32(16, "se16")
        esc = A.f32(8, "esc")
        es_all = A.f32(128, "es_all")
        lsb = A.f32(8, "lsb")
        gs_sb = A.f32(16, "gs_sb")
        bc8 = A.f32(16, "bc8")
        sc3 = scores.rearrange("p (t e) -> p t e", e=8)

        for it in range(3):
            v0 = self.cnt["vd"]
            s0 = self.cnt["sd"]
            with self.blk() as b:
                @b.vector
                def _(v):
                    v = self.ser(v)
                    v.tensor_reduce(m16.unsqueeze(2), sc3, AX.X, ALU.max)
                    v.tensor_scalar_mul(n16, m16, -1.0)
                    self.chain(v, "vd", 1)

                @b.scalar
                def _(s):
                    s = self.ser(s)
                    s.wait_ge(sem["vd"], v0 + 1)
                    for tt in range(NT):
                        s.activation(esc, scores[:, tt * 8:(tt + 1) * 8], AF.Exp,
                                     bias=n16[:, tt:tt + 1], accum_out=se16[:, tt:tt + 1])
                    s.activation(se16, se16, AF.Ln)
                    self.chain(s, "sd", 1)

                @b.vector
                def _(v):
                    v = self.ser(v)
                    v.wait_ge(sem["sd"], s0 + 1)
                    v.tensor_tensor(m16, m16, se16, ALU.add)
                    v.tensor_tensor(sc3, sc3, m16.unsqueeze(2).broadcast_to([128, 16, 8]),
                                    ALU.subtract)
            v0 = self.cnt["vd"]
            s0 = self.cnt["sd"]
            t0 = self.cnt["td"]
            d0 = self.cnt["dmas"]
            c0 = self.cnt["cc"]
            with self.blk() as b:
                @b.scalar
                def _(s):
                    s = self.ser(s)
                    s.activation(es_all, scores, AF.Exp)
                    self.chain(s, "sd", 1)

                @b.tensor
                def _(t):
                    t.wait_ge(sem["sd"], s0 + 1)
                    t.matmul(psum[6][0:1, 0:128], self.onescol_f[:], es_all,
                             start=True, stop=True).then_inc(sem["td"], 1)
                self.bump("td", 1)

                @b.vector
                def _(v):
                    v = self.ser(v)
                    v.wait_ge(sem["td"], t0 + 1)
                    for bb in range(B):
                        if bb > 0:
                            v.wait_ge(sem["dmas"], d0 + 16 * bb)
                        v.tensor_reduce(lsb[0:1, :].unsqueeze(2),
                                        psum[6][0:1, bb * 64:(bb + 1) * 64]
                                        .rearrange("p (t e) -> p e t", e=8), AX.X, ALU.add)
                        self.chain(v, "vd", 1)

                @b.sync
                def _(s):
                    for bb in range(B):
                        s.wait_ge(sem["vd"], v0 + bb + 1)
                        s.dma_start(out=self.ar_in[bb:bb + 1, :], in_=lsb[0:1, :]).then_inc(sem["dmas"], 16)
                self.bump("dmas", 32)

                @b.gpsimd
                def _(g):
                    g.wait_ge(sem["dmas"], d0 + 32)
                    g.collective_compute(
                        "AllReduce", ALU.add, replica_groups=[list(range(NCORE))],
                        ins=[self.ar_in.opt()], outs=[self.ar_out.opt()],
                    ).then_inc(sem["cc"], 1)
                self.bump("cc", 1)
            v0 = self.cnt["vd"]
            s0 = self.cnt["sd"]
            t0 = self.cnt["td"]
            d0 = self.cnt["dmas"]
            with self.blk() as b:
                @b.sync
                def _(s):
                    s.wait_ge(sem["cc"], self.cnt["cc"])
                    s.dma_start(out=gs_sb[0:1, 0:8], in_=self.ar_out[0:1, :]).then_inc(sem["dmas"], 16)
                    s.dma_start(out=gs_sb[0:1, 8:16], in_=self.ar_out[1:2, :]).then_inc(sem["dmas"], 16)
                    self.bump("dmas", 32)
                    s.wait_ge(sem["dmas"], self.cnt["dmas"])
            with self.blk() as b:
                @b.scalar
                def _(s):
                    s = self.ser(s)
                    s.activation(gs_sb[0:1, :], gs_sb[0:1, :], AF.Ln)
                    self.chain(s, "sd", 1)

                @b.tensor
                def _(t):
                    t.wait_ge(sem["sd"], s0 + 1)
                    r = None
                    for bb in range(B):
                        r = t.matmul(psum[7][:, bb * 8:(bb + 1) * 8], self.ones1f[:],
                                     gs_sb[0:1, bb * 8:(bb + 1) * 8], start=True, stop=True)
                    r.then_inc(sem["td"], 1)
                self.bump("td", 1)

                @b.vector
                def _(v):
                    v = self.ser(v)
                    v.wait_ge(sem["td"], t0 + 1)
                    v.tensor_copy(bc8, psum[7][:, 0:16])
                    for bb in range(B):
                        h = sc3[:, bb * 8:(bb + 1) * 8]
                        v.tensor_tensor(h, h,
                                        bc8[:, bb * 8:(bb + 1) * 8].unsqueeze(1).broadcast_to([128, 8, 8]), ALU.subtract)
        v0 = self.cnt["vd"]
        s0 = self.cnt["sd"]
        with self.blk() as b:
            @b.scalar
            def _(s):
                s = self.ser(s)
                s.activation(scores, scores, AF.Exp)
                self.chain(s, "sd", 1)

            @b.vector
            def _(v):
                v = self.ser(v)
                v.wait_ge(sem["sd"], s0 + 1)
                v.tensor_reduce(se16.unsqueeze(2), sc3, AX.X, ALU.add)
                v.tensor_scalar_add(se16, se16, 1e-8)
                v.reciprocal(se16, se16)
                v.tensor_tensor(sc3, sc3, se16.unsqueeze(2).broadcast_to([128, 16, 8]),
                                ALU.mult)
                v.tensor_reduce(m16.unsqueeze(2), sc3, AX.X, ALU.max)
                es3 = es_all.rearrange("p (t e) -> p t e", e=8)
                v.tensor_tensor(es3, sc3,
                                m16.unsqueeze(2).broadcast_to([128, 16, 8]),
                                ALU.is_equal)
                v.scalar_tensor_tensor(es_all, es_all, -1e30, scores, ALU.mult, ALU.add)
                v.tensor_reduce(n16.unsqueeze(2), es3, AX.X, ALU.max)
                v.tensor_tensor(se16, m16, n16, ALU.add)
                v.tensor_scalar_add(se16, se16, 1e-8)
                v.reciprocal(se16, se16)
                v.tensor_tensor(es3, sc3,
                                n16.unsqueeze(2).broadcast_to([128, 16, 8]),
                                ALU.is_ge)
                v.tensor_tensor(scores, scores, es_all, ALU.mult)
                v.tensor_tensor(sc3, sc3, se16.unsqueeze(2).broadcast_to([128, 16, 8]),
                                ALU.mult)
        A.pop()

    def _quant_ops(self, v, w_f32, out_bf, K, scr):
        nb = K // 16
        a, bsc, csc = scr["a"][:, :K], scr["b"][:, :K], scr["c"][:, :K]
        bm, rs = scr["bm"][:, :nb], scr["rs"][:, :nb]
        w3 = w_f32.rearrange("p (g k) -> p g k", k=16)
        v.tensor_reduce(bm.unsqueeze(2), w3, AX.X, ALU.max, apply_absolute_value=True)
        v.tensor_scalar(bm, bm, 1.0 / 6.0, 1e-12, ALU.mult, ALU.max)
        v.reciprocal(rs, bm)
        rsb = rs.unsqueeze(2).broadcast_to([128, nb, 16])
        a3 = a.rearrange("p (g k) -> p g k", k=16)
        v.tensor_scalar(a, w_f32, -1.0, None, ALU.mult)
        v.tensor_tensor(a, w_f32, a, ALU.max)
        v.tensor_tensor(a3, a3, rsb, ALU.mult)
        v.tensor_single_scalar(bsc, a, 0.25, ALU.is_gt)
        v.scalar_tensor_tensor(bsc, a, 0.75, bsc, ALU.is_gt, ALU.add)
        v.scalar_tensor_tensor(bsc, a, 1.25, bsc, ALU.is_gt, ALU.add)
        v.scalar_tensor_tensor(bsc, a, 1.75, bsc, ALU.is_gt, ALU.add)
        v.tensor_scalar_mul(bsc, bsc, 0.5)
        v.scalar_tensor_tensor(bsc, a, 2.5, bsc, ALU.is_gt, ALU.add)
        v.scalar_tensor_tensor(bsc, a, 3.5, bsc, ALU.is_gt, ALU.add)
        v.tensor_single_scalar(csc, a, 5.0, ALU.is_gt)
        v.scalar_tensor_tensor(bsc, csc, 2.0, bsc, ALU.mult, ALU.add)
        v.tensor_single_scalar(csc, w_f32, 0.0, ALU.is_ge)
        v.tensor_scalar(csc, csc, 2.0, -1.0, ALU.mult, ALU.add)
        c3 = csc.rearrange("p (g k) -> p g k", k=16)
        v.tensor_tensor(c3, c3, bm.unsqueeze(2).broadcast_to([128, nb, 16]), ALU.mult)
        v.tensor_tensor(out_bf, bsc, csc, ALU.mult)

    def emit_quant(self, w_f32, out_bf, K, scr):
        with self.blk() as b:
            @b.vector
            def _(v):
                v = self.ser(v)
                self._quant_ops(v, w_f32, out_bf, K, scr)

    def ph0_quant(self):
        from contextlib import ExitStack as _ES
        A, sem, nc, din = self.A, self.sem, self.nc, self.din
        phq = _ES()
        scr = {"a": self.tmp_sbuf(phq, [128, I1], F32, "qa"),
               "b": self.tmp_sbuf(phq, [128, I1], F32, "qb"),
               "c": self.tmp_sbuf(phq, [128, I1], F32, "qc"),
               "bm": self.tmp_sbuf(phq, [128, I1 // 16], F32, "qbm"),
               "rs": self.tmp_sbuf(phq, [128, I1 // 16], F32, "qrs")}
        wrow = [self.tmp_sbuf(phq, [128, I1], F32, "wrow") for _ in range(2)]
        wq = [self.tmp_sbuf(phq, [128, I1], BF16, "wqrow") for _ in range(2)]
        wqt = [self.tmp_sbuf(phq, [128, I1], BF16, "wqtrow") for _ in range(2)]
        rows = []
        for (wname, dq, nrows, K) in [("wgate_n", self.wq_d["gate"], 22, 1024),
                                      ("wup_n", self.wq_d["up"], 22, 1024),
                                      ("wdown_n", self.wq_d["down"], 8, I1)]:
            for r in range(nrows):
                rows.append((din[wname][r * 128:(r + 1) * 128, :],
                             dq[r * 128:(r + 1) * 128, :], K))
        d0 = self.cnt["dmas"]
        x0 = self.cnt["dmax"]
        g0 = self.cnt["dmag"]
        v0 = self.cnt["vd"]
        with self.blk() as b:
            @b.sync
            def _(s):
                s.dma_start(out=wrow[0][:, :rows[0][2]], in_=rows[0][0]).then_inc(sem["dmas"], 16)
                self.bump("dmas", 16)
        for i, (srcap, dstap, K) in enumerate(rows):
            with self.blk() as b:
                if i + 1 < len(rows):
                    @b.sync
                    def _(s):
                        s.dma_start(out=wrow[(i + 1) % 2][:, :rows[i + 1][2]],
                                    in_=rows[i + 1][0]).then_inc(sem["dmas"], 16)
                        self.bump("dmas", 16)

                @b.vector
                def _(v):
                    v = self.ser(v)
                    v.wait_ge(sem["dmas"], d0 + 16 * (i + 1))
                    if i >= 2:
                        v.wait_ge(sem["dmax"], x0 + 16 * (i - 1))
                    self._quant_ops(v, wrow[i % 2][:, :K], wq[i % 2][:, :K], K, scr)
                    self.chain(v, "vd", 1)

                @b.scalar
                def _(s):
                    s = self.ser(s)
                    s.wait_ge(sem["vd"], v0 + i + 1)
                    if i >= 2:
                        s.wait_ge(sem["dmag"], g0 + 16 * (i - 1))
                    s.dma_start(out=wqt[i % 2][:, :K].rearrange("p (n f) -> p n f", f=128),
                                in_=wq[i % 2][:, :K].rearrange("p (n f) -> p n f", f=128),
                                transpose=True).then_inc(sem["dmax"], 16)
                    self.bump("dmax", 16)

                @b.gpsimd
                def _(g):
                    g = self.ser(g)
                    g.wait_ge(sem["dmax"], x0 + 16 * (i + 1))
                    g.dma_start(out=dstap, in_=wqt[i % 2][:, :K]).then_inc(sem["dmag"], 16)
                    self.bump("dmag", 16)
        with self.blk() as b:
            @b.gpsimd
            def _(g):
                g = self.ser(g)
                g.wait_ge(sem["dmag"], self.cnt["dmag"])
        phq.close()

    def layer1(self):
        from contextlib import ExitStack as _ES
        A, sem, nc, din, psum = self.A, self.sem, self.nc, self.din, self.psum
        ph = _ES()

        self.emit_bcast(1, [(1, 0, "plus1"), (0, 2, None)])
        h3bt = self.tmp_sbuf(ph, [128, NT * D], BF16, "h3bt")
        with _ES() as ph2:
            h3tm = self.tmp_sbuf(ph2, [128, NT * D], BF16, "h3tm", side="right")
            self.emit_ln_mod(h3tm, 0, 2, tap=self.taps.get("t_h3"))
            self.xbar_blocks(h3bt, h3tm)
        with self.blk() as b:
            @b.sync
            def _(s):
                s.dma_start(out=self.spill.rearrange("(t p) d -> p t d", p=128),
                            in_=self.x_sb[:].rearrange("p (t d) -> p t d", d=D)).then_inc(sem["dmas"], 16)
                self.bump("dmas", 16)
                s.wait_ge(sem["dmas"], self.cnt["dmas"])
        xcv = self.x_sb[:].bitcast(BF16)

        A.push()
        wbuf = A.bf16(DC * D, "ipw")
        zstg = [A.bf16(512, "zst0"), A.bf16(512, "zst1")]
        for eg in range(4):
            with self.blk() as b:
                @b.gpsimd
                def _(g):
                    g = self.ser(g)
                    g.dma_start(out=wbuf.rearrange("p (c e) -> p c e", c=DC),
                                in_=din["inpT"][:, eg * 1024:(eg + 1) * 1024]
                                .rearrange("(c p) e -> p c e", p=128)).then_inc(sem["dmag"], 16)
                    self.bump("dmag", 16)
                    g.wait_ge(sem["dmag"], self.cnt["dmag"])
            t0 = self.cnt["td"]
            v0 = self.cnt["vd"]
            zp0 = [self.cnt["dma0"], self.cnt["dma1"]]
            with self.blk() as b:
                @b.tensor
                def _(t):
                    i = 0
                    for ec in range(8):
                        for ts in range(4):
                            if i >= 4:
                                t.wait_ge(sem["vd"], v0 + i - 3)
                            pb = psum[i % 4]
                            r = None
                            for dc in range(DC):
                                rhs = h3bt.rearrange("p (t c f) -> p t c f", c=DC, f=128)[
                                    :, ts * 4:(ts + 1) * 4, dc:dc + 1, :].squeeze(2)
                                r = t.matmul(pb[:, :], wbuf[:, dc * 1024 + ec * 128: dc * 1024 + ec * 128 + 128],
                                             rhs, start=(dc == 0), stop=(dc == DC - 1))
                            r.then_inc(sem["td"], 1)
                            i += 1

                @b.vector
                def _(v):
                    v = self.ser(v)
                    i = 0
                    for ec in range(8):
                        e_abs = eg * 8 + ec
                        for ts in range(4):
                            v.wait_ge(sem["td"], t0 + i + 1)
                            pb = psum[i % 4]
                            if e_abs < 16:
                                bsel = ts // 2
                                toff = (ts % 2) * 512
                                dst = xcv[:, (bsel * 16 + e_abs) * 1024 + toff:
                                          (bsel * 16 + e_abs) * 1024 + toff + 512]
                                v.tensor_copy(dst, pb[:, :])
                            else:
                                if i >= 2:
                                    v.wait_ge(sem["dma%d" % (i % 2)],
                                              zp0[i % 2] + 16 * ((i - 2) // 2 + 1))
                                v.tensor_copy(zstg[i % 2], pb[:, :])
                            self.chain(v, "vd", 1)
                            i += 1

                @b.gpsimd
                def _(g):
                    if eg >= 2:
                        i = 0
                        for ec in range(8):
                            e_abs = eg * 8 + ec
                            zc = e_abs - 16
                            for ts in range(4):
                                g.wait_ge(sem["vd"], v0 + i + 1)
                                bsel = ts // 2
                                toff = (ts % 2) * 512
                                g.dma_start(out=self.zspill[bsel * 16 + zc, :, toff:toff + 512],
                                            in_=zstg[i % 2]).then_inc(sem["dma%d" % (i % 2)], 16)
                                self.bump("dma%d" % (i % 2), 16)
                                i += 1
                        g.wait_ge(sem["dma0"], self.cnt["dma0"])
                        g.wait_ge(sem["dma1"], self.cnt["dma1"])
            self.cnt["td"] = t0 + 32
        A.pop()
        ph.close()

        A.push()
        halo_sb = A.f32(B * 16 * 3, "halo_sb")
        sel_bf = A.bf16(32, "sel_bf")
        gall_bf = A.bf16(384, "gall_bf")
        d0 = self.cnt["dmag"]
        with self.blk() as b:
            @b.gpsimd
            def _(g):
                g = self.ser(g)
                for bb in range(B):
                    g.dma_start(out=self.cc_halo_in[bb * 16:(bb + 1) * 16, :]
                                .rearrange("c (p t) -> p c t", p=128),
                                in_=xcv.rearrange("p (b c t) -> p b c t", b=B, c=16)
                                [:, bb:bb + 1, :, SPAN - 3:].squeeze(1),
                                allow_slow_non_contiguous=True).then_inc(sem["dmag"], 16)
                    self.bump("dmag", 16)
                g.wait_ge(sem["dmag"], self.cnt["dmag"])
                g.collective_compute(
                    "AllGather", ALU.bypass, replica_groups=[list(range(NCORE))],
                    ins=[self.cc_halo_in.opt()], outs=[self.cc_halo_out.opt()],
                ).then_inc(sem["cc"], 1)
                self.bump("cc", 1)
        hstage = A.f32(384, "hstage")
        t0 = self.cnt["td"]
        v0 = self.cnt["vd"]
        s0g = self.cnt["dmag"]
        with self.blk() as b:
            @b.gpsimd
            def _(g):
                g = self.ser(g)
                g.wait_ge(sem["cc"], self.cnt["cc"])
                for hh in range(2):
                    g.dma_start(out=gall_bf[:], in_=self.cc_halo_out[hh * 128:(hh + 1) * 128, :]
                                ).then_inc(sem["dmag"], 16)
                    g.dma_start(out=sel_bf[:], in_=self.din["sel48"][hh * 128:(hh + 1) * 128, :]
                                ).then_inc(sem["dmag"], 16)
                    self.bump("dmag", 32)
                    if hh == 1:
                        g.wait_ge(sem["td"], t0 + 1)
                g.wait_ge(sem["dmag"], self.cnt["dmag"])

            @b.tensor
            def _(t):
                for hh in range(2):
                    t.wait_ge(sem["dmag"], s0g + 32 * (hh + 1))
                    t.matmul(psum[6][0:32, 0:384], sel_bf[:], gall_bf[:],
                             start=(hh == 0), stop=(hh == 1)).then_inc(sem["td"], 1)
            self.bump("td", 2)

            @b.vector
            def _(v):
                v = self.ser(v)
                v.wait_ge(sem["td"], t0 + 2)
                v.tensor_copy(hstage[0:32, :], psum[6][0:32, 0:384])
                self.chain(v, "vd", 1)
        with self.blk() as b:
            @b.sync
            def _(s):
                s.dma_start(out=self.halo_d, in_=hstage[0:32, :]).then_inc(sem["dmas"], 16)
                self.bump("dmas", 16)
                s.wait_ge(sem["dmas"], self.cnt["dmas"])
        with self.blk() as b:
            @b.sync
            def _(s):
                for bb in range(B):
                    s.dma_start(out=halo_sb.rearrange("p (b c t) -> p b c t", b=B, c=16)
                                [:, bb:bb + 1].squeeze(1),
                                in_=self.halo_d[bb * 16:(bb + 1) * 16, :]
                                .rearrange("c (p t) -> p c t", p=128),
                                allow_slow_non_contiguous=True).then_inc(sem["dmas"], 16)
                    self.bump("dmas", 16)
                s.wait_ge(sem["dmas"], self.cnt["dmas"])

        cwcol = A.f32(4 * 16, "cwcol")
        cbcol = A.f32(16, "cbcol")
        with self.blk() as b:
            @b.sync
            def _(s):
                s.dma_start(out=cwcol, in_=din["convw"]).then_inc(sem["dmas"], 16)
                s.dma_start(out=cbcol, in_=din["convb"]).then_inc(sem["dmas"], 16)
                self.bump("dmas", 32)
                s.wait_ge(sem["dmas"], self.cnt["dmas"])
        acc = [A.f32(SPAN, "cacc0"), A.f32(SPAN, "cacc1")]
        for bb in range(B):
            for dic in range(16):
                xoff = (bb * 16 + dic) * 1024
                xcb = xcv[:, xoff:xoff + 1024]
                ab = acc[(bb * 16 + dic) % 2]
                v0 = self.cnt["vd"]
                s0 = self.cnt["sd"]
                with self.blk() as b:
                    @b.vector
                    def _(v):
                        v = self.ser(v)
                        if dic >= 2 or bb > 0:
                            v.wait_ge(sem["sd"], s0)
                        v.tensor_scalar(ab, xcb, cwcol[:, 3 * 16 + dic:3 * 16 + dic + 1], None, ALU.mult)
                        for j in range(3):
                            a = 3 - j
                            v.scalar_tensor_tensor(ab[:, a:], xcb[:, 0:1024 - a],
                                                   cwcol[:, j * 16 + dic:j * 16 + dic + 1],
                                                   ab[:, a:], ALU.mult, ALU.add)
                            for t in range(a):
                                tau = t + j
                                v.scalar_tensor_tensor(ab[:, t:t + 1],
                                                       halo_sb[:, (bb * 16 + dic) * 3 + tau:
                                                               (bb * 16 + dic) * 3 + tau + 1],
                                                       cwcol[:, j * 16 + dic:j * 16 + dic + 1],
                                                       ab[:, t:t + 1], ALU.mult, ALU.add)
                        self.chain(v, "vd", 1)

                    @b.scalar
                    def _(s):
                        s = self.ser(s)
                        s.wait_ge(sem["vd"], v0 + 1)
                        s.activation(xcb, ab, AF.Silu, bias=cbcol[:, dic:dic + 1])
                        self.chain(s, "sd", 1)
        A.pop()
        self.l1_scan()
        self.l1_ffn()

    def l1_scan(self):
        from contextlib import ExitStack as _ES
        A, sem, nc, din, psum = self.A, self.sem, self.nc, self.din, self.psum
        xcv = self.x_sb[:].bitcast(BF16)
        ph = _ES()
        dlt = self.tmp_sbuf(ph, [128, B * 16 * 1024], BF16, "dlt")
        y0s = self.tmp_sbuf(ph, [128, 4 * 1024], BF16, "y0s")
        A.push()
        acsb = A.f32(16 * 16, "acsb")
        dcol = A.f32(16, "dcol")
        dtbc = A.f32(16, "dtbc")
        dsp = A.f32(32, "dsp")
        P_sb = A.f32(512, "P_sb")
        Q_sb = A.f32(512, "Q_sb")
        hin = A.f32(512, "hin")
        hacc = [A.f32(512, "ha0"), A.f32(512, "ha1")]

        s0 = self.cnt["sd"]
        with self.blk() as b:
            @b.sync
            def _(s):
                s.dma_start(out=acsb.rearrange("p (c n) -> p c n", c=16),
                            in_=din["alog"].rearrange("(c p) n -> p c n", p=128)).then_inc(sem["dmas"], 16)
                s.dma_start(out=dcol, in_=din["dparam"]).then_inc(sem["dmas"], 16)
                s.dma_start(out=dtbc, in_=din["dtb"]).then_inc(sem["dmas"], 16)
                self.bump("dmas", 48)
                s.wait_ge(sem["dmas"], self.cnt["dmas"])

            @b.scalar
            def _(s):
                s = self.ser(s)
                s.wait_ge(sem["dmas"], self.cnt["dmas"])
                s.activation(acsb, acsb, AF.Exp)
                self.chain(s, "sd", 1)

            @b.vector
            def _(v):
                v = self.ser(v)
                v.wait_ge(sem["sd"], s0 + 1)
                v.tensor_scalar_mul(acsb, acsb, -1.0)

        A.push()
        dtl_bf = A.bf16(2048, "dtl_bf")
        A.push()
        BC_sb = A.f32(2048, "BC_sb")
        xpw = A.bf16(16 * 96, "xpw")
        with self.blk() as b:
            @b.gpsimd
            def _(g):
                g = self.ser(g)
                g.dma_start(out=xpw.rearrange("p (c e) -> p c e", c=16),
                            in_=din["xprojT"].rearrange("(c p) e -> p c e", p=128)).then_inc(sem["dmag"], 16)
                self.bump("dmag", 16)
                g.wait_ge(sem["dmag"], self.cnt["dmag"])
        t0 = self.cnt["td"]
        v0 = self.cnt["vd"]
        with self.blk() as b:
            @b.tensor
            def _(t):
                i = 0
                for bb in range(B):
                    for ts in range(2):
                        if i >= 1:
                            t.wait_ge(sem["vd"], v0 + 2 * i)
                        r1 = r2 = None
                        for dic in range(16):
                            rhs = xcv[:, (bb * 16 + dic) * 1024 + ts * 512:
                                      (bb * 16 + dic) * 1024 + ts * 512 + 512]
                            r1 = t.matmul(psum[6][0:64, :], xpw[:, dic * 96:dic * 96 + 64], rhs,
                                          start=(dic == 0), stop=(dic == 15))
                        r1.then_inc(sem["td"], 1)
                        for dic in range(16):
                            rhs = xcv[:, (bb * 16 + dic) * 1024 + ts * 512:
                                      (bb * 16 + dic) * 1024 + ts * 512 + 512]
                            r2 = t.matmul(psum[7][0:32, :], xpw[:, dic * 96 + 64:dic * 96 + 96], rhs,
                                          start=(dic == 0), stop=(dic == 15))
                        r2.then_inc(sem["td"], 1)
                        i += 1

            @b.vector
            def _(v):
                v = self.ser(v)
                for i in range(4):
                    bb, ts = i // 2, i % 2
                    co = bb * 1024 + ts * 512
                    v.wait_ge(sem["td"], t0 + 2 * i + 1)
                    v.tensor_copy(dtl_bf[0:64, co:co + 512], psum[6][0:64, :])
                    v.wait_ge(sem["td"], t0 + 2 * i + 2)
                    v.tensor_copy(BC_sb[0:32, co:co + 512], psum[7][0:32, :])
                    self.chain(v, "vd", 2)
        self.cnt["td"] = t0 + 8
        with self.blk() as b:
            @b.sync
            def _(s):
                s.dma_start(out=self.bc_d, in_=BC_sb[0:32, :]).then_inc(sem["dmas"], 16)
                self.bump("dmas", 16)
                s.wait_ge(sem["dmas"], self.cnt["dmas"])
        A.pop()
        A.push()
        sp_a = A.f32(512, "sp_a")
        sp_l = [A.f32(512, "sp_l0"), A.f32(512, "sp_l1")]
        sp_t = A.f32(512, "sp_t")
        phw = _ES()
        dtw = self.tmp_sbuf(phw, [64, DI], BF16, "dtw")
        with self.blk() as b:
            @b.gpsimd
            def _(g):
                g = self.ser(g)
                g.dma_start(out=dtw[0:64, :], in_=din["dtprojT"]).then_inc(sem["dmag"], 16)
                self.bump("dmag", 16)
                g.wait_ge(sem["dmag"], self.cnt["dmag"])
        t0 = self.cnt["td"]
        s0 = self.cnt["sd"]
        v0 = self.cnt["vd"]
        with self.blk() as b:
            @b.tensor
            def _(t):
                i = 0
                for bb in range(B):
                    for dic in range(16):
                        for ts in range(2):
                            if i >= 4:
                                t.wait_ge(sem["vd"], v0 + i - 3)
                            t.matmul(psum[i % 4][:, :], dtw[0:64, dic * 128:(dic + 1) * 128],
                                     dtl_bf[0:64, bb * 1024 + ts * 512: bb * 1024 + ts * 512 + 512],
                                     start=True, stop=True).then_inc(sem["td"], 1)
                            i += 1

            @b.scalar
            def _(s):
                s = self.ser(s)
                i = 0
                for bb in range(B):
                    for dic in range(16):
                        for ts in range(2):
                            s.wait_ge(sem["td"], t0 + i + 1)
                            s.activation(sp_a, psum[i % 4][:, :], AF.Abs,
                                         bias=dtbc[:, dic:dic + 1])
                            s.activation(sp_a, sp_a, AF.Exp, scale=self.neg1_col[:])

                            if i >= 2:
                                s.wait_ge(sem["vd"], v0 + i - 1)
                            s.activation(sp_l[i % 2], sp_a, AF.Ln, bias=self.onescol_f[:])
                            self.chain(s, "sd", 1)
                            i += 1

            @b.vector
            def _(v):
                v = self.ser(v)
                i = 0
                for bb in range(B):
                    for dic in range(16):
                        for ts in range(2):
                            v.wait_ge(sem["sd"], s0 + i + 1)
                            v.tensor_scalar(sp_t, psum[i % 4][:, :],
                                            dtbc[:, dic:dic + 1], 0.0, ALU.add, ALU.max)
                            v.tensor_tensor(dlt[:, (bb * 16 + dic) * 1024 + ts * 512:
                                            (bb * 16 + dic) * 1024 + ts * 512 + 512],
                                            sp_t, sp_l[i % 2], ALU.add)
                            self.chain(v, "vd", 1)
                            i += 1
            self.cnt["td"] = t0 + 64
        A.pop()
        A.pop()
        phw.close()

        s0 = self.cnt["sd"]
        v0 = self.cnt["vd"]
        with self.blk() as b:
            @b.vector
            def _(v):
                v = self.ser(v)
                for bb in range(B):
                    for dic in range(16):
                        v.reduce_sum(dsp[:, bb * 16 + dic: bb * 16 + dic + 1],
                                     dlt[:, (bb * 16 + dic) * 1024:(bb * 16 + dic + 1) * 1024], AX.X)
                self.chain(v, "vd", 1)

            @b.scalar
            def _(s):
                s = self.ser(s)
                s.wait_ge(sem["vd"], v0 + 1)
                for bb in range(B):
                    for dic in range(16):
                        s.activation(P_sb[:, (bb * 16 + dic) * 16:(bb * 16 + dic + 1) * 16],
                                     acsb[:, dic * 16:(dic + 1) * 16], AF.Exp,
                                     scale=dsp[:, bb * 16 + dic: bb * 16 + dic + 1])

        dab = [A.f32(1024, "dab0"), A.f32(1024, "dab1")]
        dbu = A.f32(1024, "dbu")
        h0b = A.f32(1024, "h0b")
        dub = self.tmp_sbuf(ph, [128, 4 * 1024], F32, "dub")
        hcb = A.bf16(1024, "hcb")
        bbc = self.tmp_sbuf(ph, [128, 1024], BF16, "bbc")
        cbc = self.tmp_sbuf(ph, [128, 1024], BF16, "cbc")
        for passno in range(2):
            for bb in range(B):
                for dg in range(4):
                    dics = [dg * 4 + k for k in range(4)]
                    if passno == 0:
                        with self.blk() as b:
                            @b.vector
                            def _(v):
                                v = self.ser(v)
                                for k, dic in enumerate(dics):
                                    o = (bb * 16 + dic) * 1024
                                    v.tensor_tensor(dub[:, k * 1024:(k + 1) * 1024],
                                                    dlt[:, o:o + 1024], xcv[:, o:o + 1024], ALU.mult)
                    else:
                        with self.blk() as b:
                            @b.vector
                            def _(v):
                                v = self.ser(v)
                                for k, dic in enumerate(dics):
                                    o = (bb * 16 + dic) * 1024
                                    v.tensor_tensor_scan(dub[:, k * 1024:(k + 1) * 1024],
                                                         dlt[:, o:o + 1024], dlt[:, o:o + 1024],
                                                         0.0, ALU.add, ALU.bypass)

                            @b.gpsimd
                            def _(g):
                                g = self.ser(g)
                                for k, dic in enumerate(dics):
                                    g.dma_start(out=y0s[:, k * 1024:(k + 1) * 1024],
                                                in_=self.y0_d[(bb * 16 + dic) * 128:(bb * 16 + dic + 1) * 128, :]
                                                ).then_inc(sem["dmag"], 16)
                                    self.bump("dmag", 16)
                                g.wait_ge(sem["dmag"], self.cnt["dmag"])
                    for n in range(DS):
                        t0 = self.cnt["td"]
                        v0 = self.cnt["vd"]
                        s0 = self.cnt["sd"]
                        g0 = self.cnt["dmag"]
                        with self.blk() as b:
                            @b.gpsimd
                            def _(g):
                                g = self.ser(g)
                                if passno == 0:
                                    g.dma_start(out=bbc[:],
                                                in_=self.bc_d[n:n + 1, bb * 1024:(bb + 1) * 1024]
                                                .partition_broadcast(128).squeeze(1)
                                                ).then_inc(sem["dmag"], 16)
                                    self.bump("dmag", 16)
                                g.dma_start(out=cbc[:],
                                            in_=self.bc_d[16 + n:17 + n, bb * 1024:(bb + 1) * 1024]
                                            .partition_broadcast(128).squeeze(1)
                                            ).then_inc(sem["dmag"], 16)
                                self.bump("dmag", 16)
                                g.wait_ge(sem["dmag"], self.cnt["dmag"])

                            @b.scalar
                            def _(s):
                                s = self.ser(s)
                                for k, dic in enumerate(dics):
                                    o = (bb * 16 + dic) * 1024
                                    if k >= 2:
                                        s.wait_ge(sem["vd"], v0 + 4 * (k - 1))
                                    src = dlt[:, o:o + 1024] if passno == 0 else dub[:, k * 1024:(k + 1) * 1024]
                                    s.activation(dab[k % 2][:], src, AF.Exp,
                                                 scale=acsb[:, dic * 16 + n: dic * 16 + n + 1])
                                    self.chain(s, "sd", 1)

                            @b.vector
                            def _(v):
                                v = self.ser(v)
                                v.wait_ge(sem["dmag"], self.cnt["dmag"])
                                for k, dic in enumerate(dics):
                                    col = (bb * 16 + dic) * 16 + n
                                    v.wait_ge(sem["sd"], s0 + k + 1)
                                    if k >= 1:
                                        v.wait_ge(sem["td"], t0 + k)
                                    if passno == 0:
                                        v.tensor_tensor(dbu[:], dub[:, k * 1024:(k + 1) * 1024],
                                                        bbc[:], ALU.mult)
                                        v.tensor_tensor_scan(h0b[:], dab[k % 2][:], dbu[:],
                                                             0.0, ALU.mult, ALU.add)
                                        v.tensor_copy(Q_sb[:, col:col + 1], h0b[:, 1023:1024])
                                        v.tensor_tensor(hcb[:], h0b[:], cbc[:], ALU.mult)
                                    else:
                                        v.scalar_tensor_tensor(hcb[:], dab[k % 2][:],
                                                               hin[:, col:col + 1], cbc[:],
                                                               ALU.mult, ALU.mult)
                                    self.chain(v, "vd", 4)

                            @b.tensor
                            def _(t):
                                for k, dic in enumerate(dics):
                                    t.wait_ge(sem["vd"], v0 + (k + 1) * 4)
                                    pb0 = psum[2 * k]
                                    pb1 = psum[2 * k + 1]
                                    st = (n == 0 and passno == 0)
                                    if passno == 1 and n == 0:
                                        t.matmul(pb0[:, :], self.ident[:], y0s[:, k * 1024:k * 1024 + 512],
                                                 start=True, stop=False)
                                        t.matmul(pb1[:, :], self.ident[:], y0s[:, k * 1024 + 512:(k + 1) * 1024],
                                                 start=True, stop=False)
                                    r = t.matmul(pb0[:, :], self.ident[:], hcb[:, 0:512],
                                                 start=st, stop=False)
                                    r = t.matmul(pb1[:, :], self.ident[:], hcb[:, 512:1024],
                                                 start=st, stop=False)
                                    r.then_inc(sem["td"], 1)
                            self.bump("td", 4)
                    v0 = self.cnt["vd"]
                    g0 = self.cnt["dmag"]
                    s0 = self.cnt["sd"]
                    if passno == 0:
                        with self.blk() as b:
                            @b.vector
                            def _(v):
                                v = self.ser(v)
                                for k, dic in enumerate(dics):
                                    v.tensor_copy(y0s[:, k * 1024:k * 1024 + 512], psum[2 * k][:, :])
                                    v.tensor_copy(y0s[:, k * 1024 + 512:(k + 1) * 1024], psum[2 * k + 1][:, :])
                                    self.chain(v, "vd", 2)

                            @b.gpsimd
                            def _(g):
                                g = self.ser(g)
                                for k, dic in enumerate(dics):
                                    g.wait_ge(sem["vd"], v0 + 2 * (k + 1))
                                    g.dma_start(out=self.y0_d[(bb * 16 + dic) * 128:(bb * 16 + dic + 1) * 128, :],
                                                in_=y0s[:, k * 1024:(k + 1) * 1024]
                                                ).then_inc(sem["dmag"], 16)
                                    self.bump("dmag", 16)
                                g.wait_ge(sem["dmag"], self.cnt["dmag"])
                    if passno == 1:
                        for k, dic in enumerate(dics):
                            o = (bb * 16 + dic) * 1024
                            v0 = self.cnt["vd"]
                            s0 = self.cnt["sd"]
                            g0 = self.cnt["dmag"]
                            with self.blk() as b:
                                @b.gpsimd
                                def _(g):
                                    g = self.ser(g)
                                    g.dma_start(out=cbc[:], in_=self.zspill[bb * 16 + dic, :, :]
                                                ).then_inc(sem["dmag"], 16)
                                    self.bump("dmag", 16)
                                    g.wait_ge(sem["dmag"], self.cnt["dmag"])

                                @b.scalar
                                def _(s):
                                    s = self.ser(s)
                                    s.wait_ge(sem["dmag"], self.cnt["dmag"])
                                    s.activation(bbc[:], cbc[:], AF.Silu)
                                    self.chain(s, "sd", 1)

                                @b.vector
                                def _(v):
                                    v = self.ser(v)
                                    v.wait_ge(sem["sd"], s0 + 1)
                                    for hh in range(2):
                                        v.scalar_tensor_tensor(h0b[:, hh * 512:(hh + 1) * 512],
                                                               xcv[:, o + hh * 512:o + (hh + 1) * 512],
                                                               dcol[:, dic:dic + 1],
                                                               psum[2 * k + hh][:, :],
                                                               ALU.mult, ALU.add)
                                    v.tensor_tensor(dlt[:, o:o + 1024], h0b[:], bbc[:], ALU.mult)
                                    self.chain(v, "vd", 1)
            if passno == 0:
                d0 = self.cnt["dmas"]
                with self.blk() as b:
                    @b.sync
                    def _(s):
                        s.dma_start(out=self.cc_pq_in[0:128, :], in_=P_sb).then_inc(sem["dmas"], 16)
                        s.dma_start(out=self.cc_pq_in[128:256, :], in_=Q_sb).then_inc(sem["dmas"], 16)
                        self.bump("dmas", 32)

                    @b.gpsimd
                    def _(g):
                        g.wait_ge(sem["dmas"], self.cnt["dmas"])
                        g.collective_compute(
                            "AllGather", ALU.bypass, replica_groups=[list(range(NCORE))],
                            ins=[self.cc_pq_in.opt()], outs=[self.cc_pq_out.opt()],
                        ).then_inc(sem["cc"], 1)
                        self.bump("cc", 1)
                d0 = self.cnt["dmas"]
                with self.blk() as b:
                    @b.sync
                    def _(s):
                        s.wait_ge(sem["cc"], self.cnt["cc"])
                        s.dma_start(out=hin, in_=self.cc_pq_out[0:128, :]).then_inc(sem["dmas"], 16)
                        self.bump("dmas", 16)
                        s.wait_ge(sem["dmas"], self.cnt["dmas"])
                with self.blk() as b:
                    @b.vector
                    def _(v):
                        v = self.ser(v)
                        v.memset(hin, 0.0)
                for j in range(NCORE):
                    with self.blk() as b:
                        @b.sync
                        def _(s):
                            s.dma_start(out=hacc[0], in_=self.cc_pq_out[j * 256:j * 256 + 128, :]
                                        ).then_inc(sem["dmas"], 16)
                            s.dma_start(out=hacc[1], in_=self.cc_pq_out[j * 256 + 128:j * 256 + 256, :]
                                        ).then_inc(sem["dmas"], 16)
                            self.bump("dmas", 32)
                            s.wait_ge(sem["dmas"], self.cnt["dmas"])
                    with self.blk() as b:
                        @b.vector
                        def _(v):
                            v = self.ser(v)
                            v.tensor_tensor(hacc[0], hacc[0], hin, ALU.mult)
                            v.tensor_tensor(hacc[0], hacc[0], hacc[1], ALU.add)
                            v.tensor_tensor(hacc[0], hacc[0], hin, ALU.subtract)
                            v.scalar_tensor_tensor(hin, hacc[0], self.mlt[:, j:j + 1],
                                                   hin, ALU.mult, ALU.add)
        A.pop()
        self.emit_bcast(1, [(2, 0, "signmax")])
        with self.blk() as b:
            @b.sync
            def _(s):
                s.dma_start(out=self.x_sb[:].rearrange("p (t d) -> p t d", d=D),
                            in_=self.spill.rearrange("(t p) d -> p t d", p=128)).then_inc(sem["dmas"], 16)
                self.bump("dmas", 16)
                s.wait_ge(sem["dmas"], self.cnt["dmas"])
        A.push()
        ow = A.bf16(16 * 512, "ow")
        epi2 = A.f32(512, "epi2")
        for eh in range(2):
            with self.blk() as b:
                @b.gpsimd
                def _(g):
                    g = self.ser(g)
                    g.dma_start(out=ow.rearrange("p (c e) -> p c e", c=16),
                                in_=din["outpT"][:, eh * 512:(eh + 1) * 512]
                                .rearrange("(c p) e -> p c e", p=128)).then_inc(sem["dmag"], 16)
                    self.bump("dmag", 16)
                    g.wait_ge(sem["dmag"], self.cnt["dmag"])
            groups = []
            for tt in range(NT):
                bb = 0 if tt < 8 else 1
                ttb = tt % 8

                def mme(t, pb, bb=bb, ttb=ttb):
                    r = None
                    for dic in range(16):
                        r = t.matmul(pb[:, :],
                                     dlt[:, (bb * 16 + dic) * 1024 + ttb * 128:
                                         (bb * 16 + dic) * 1024 + ttb * 128 + 128],
                                     ow[:, dic * 512:(dic + 1) * 512],
                                     start=(dic == 0), stop=(dic == 15))
                    return r

                def eve(v, pb, tt=tt, bb=bb, eh=eh):
                    ga = self.bc[:, bb * D + eh * 512: bb * D + (eh + 1) * 512]
                    xt = self.x_sb[:, tt * D + eh * 512: tt * D + (eh + 1) * 512]
                    v.tensor_tensor(epi2, pb[:, :], ga, ALU.mult)
                    v.tensor_tensor(xt, xt, epi2, ALU.add)
                groups.append((mme, eve))
            self.mm_pipeline(groups, psum[:4])
        A.pop()
        ph.close()
        if "t_x1a" in self.taps:
            self.tap_f32(self.taps["t_x1a"].rearrange("(t p) d -> p t d", p=128),
                         self.x_sb[:].rearrange("p (t d) -> p t d", d=D))

    def l1_ffn(self):
        from contextlib import ExitStack as _ES
        A, sem, nc, din, psum = self.A, self.sem, self.nc, self.din, self.psum
        ph = _ES()
        xqg = self.tmp_sbuf(ph, [128, 4 * D], BF16, "xqg")
        self.emit_bcast(1, [(4, 0, "plus1"), (3, 2, None)])
        with _ES() as ph2:
            h4tm = self.tmp_sbuf(ph2, [128, NT * D], BF16, "h4tm", side="right")
            self.emit_ln_mod(h4tm, 0, 2, tap=self.taps.get("t_h4"))
            A.push()
            qscr = {"a": A.f32(D, "qa2"), "b": A.f32(D, "qb2"), "c": A.f32(D, "qc2"),
                    "bm": A.f32(D // 16, "qbm2"), "rs": A.f32(D // 16, "qrs2")}
            h4f = A.f32(D, "h4f")
            for t in range(NT):
                with self.blk() as b:
                    @b.vector
                    def _(v):
                        v = self.ser(v)
                        v.tensor_copy(h4f, h4tm[:, t * D:(t + 1) * D])
                self.emit_quant(h4f, h4tm[:, t * D:(t + 1) * D], D, qscr)
            for tgx in range(4):
                self.xbar_blocks(xqg, h4tm[:, tgx * 4 * D:(tgx + 1) * 4 * D])
                with self.blk() as b:
                    @b.sync
                    def _(s):
                        s.dma_start(out=self.xq_d[:, tgx * 4 * D:(tgx + 1) * 4 * D],
                                    in_=xqg).then_inc(sem["dmas"], 16)
                        self.bump("dmas", 16)
                        s.wait_ge(sem["dmas"], self.cnt["dmas"])
        self.emit_bcast(1, [(5, 0, "signmax")])
        with self.blk() as b:
            @b.sync
            def _(s):
                s.dma_start(out=self.spill.rearrange("(t p) d -> p t d", p=128),
                            in_=self.x_sb[:].rearrange("p (t d) -> p t d", d=D)).then_inc(sem["dmas"], 16)
                self.bump("dmas", 16)
                s.wait_ge(sem["dmas"], self.cnt["dmas"])
        A.pop()

        wqbt = self.x_sb[:].bitcast(BF16)[:, 0:22 * 1024]
        gsil = self.tmp_sbuf(ph, [128, 4 * I1], BF16, "gsil")
        hq = self.tmp_sbuf(ph, [128, 4 * I1], BF16, "hq")
        hqbt = gsil
        A.push()
        qscr2 = {"a": A.f32(1024, "qa3"), "b": A.f32(1024, "qb3"), "c": A.f32(1024, "qc3"),
                 "bm": A.f32(64, "qbm3"), "rs": A.f32(64, "qrs3")}
        hrow = A.f32(1024, "hrow")
        eslb = [A.f32(512, "eslb0"), A.f32(512, "eslb1")]
        NSL = [512] * 5 + [256]

        def gu_mm(t, pb, tg, tt, ns, n0, nw):
            r = None
            for kc in range(8):
                rhs_blocks = wqbt.rearrange("p (nb kc f) -> p nb kc f", kc=8, f=128)
                lhsT = xqg[:, (tt * 8 + kc) * 128:(tt * 8 + kc + 1) * 128]
                rhs = rhs_blocks[:, n0 // 128:(n0 + nw) // 128, kc:kc + 1, :].squeeze(2)
                r = t.matmul(pb[:, 0:nw], lhsT, rhs, start=(kc == 0), stop=(kc == 7))
            return r

        for tg in range(4):
            with self.blk() as b:
                @b.sync
                def _(s):
                    s.dma_start(out=xqg, in_=self.xq_d[:, tg * 4 * D:(tg + 1) * 4 * D]
                                ).then_inc(sem["dmas"], 16)
                    self.bump("dmas", 16)
                    s.wait_ge(sem["dmas"], self.cnt["dmas"])
            for wi, wname in enumerate(("gate", "up")):
                with self.blk() as b:
                    @b.gpsimd
                    def _(g):
                        g = self.ser(g)
                        g.dma_start(out=wqbt.rearrange("p (nb k) -> p nb k", nb=22),
                                    in_=self.wq_d[wname].rearrange("(nb p) k -> p nb k", p=128)
                                    ).then_inc(sem["dmag"], 16)
                        self.bump("dmag", 16)
                        g.wait_ge(sem["dmag"], self.cnt["dmag"])
                t0 = self.cnt["td"]
                v0 = self.cnt["vd"]
                s0 = self.cnt["sd"]
                with self.blk() as b:
                    @b.tensor
                    def _(t):
                        i = 0
                        n0 = 0
                        for ns, nw in enumerate(NSL):
                            for tt in range(4):
                                if i >= 4:
                                    if wi == 0:
                                        t.wait_ge(sem["sd"], s0 + i - 3)
                                    else:
                                        t.wait_ge(sem["vd"], v0 + i - 3)
                                gu_mm(t, psum[i % 4], tg, tt, ns, n0, nw).then_inc(sem["td"], 1)
                                i += 1
                            n0 += nw

                    if wi == 0:
                        @b.scalar
                        def _(s):
                            s = self.ser(s)
                            i = 0
                            n0 = 0
                            for ns, nw in enumerate(NSL):
                                for tt in range(4):
                                    s.wait_ge(sem["td"], t0 + i + 1)
                                    s.activation(gsil[:, tt * I1 + n0: tt * I1 + n0 + nw],
                                                 psum[i % 4][:, 0:nw], AF.Silu)
                                    self.chain(s, "sd", 1)
                                    i += 1
                                n0 += nw
                    else:
                        @b.vector
                        def _(v):
                            v = self.ser(v)
                            i = 0
                            n0 = 0
                            for ns, nw in enumerate(NSL):
                                for tt in range(4):
                                    v.wait_ge(sem["td"], t0 + i + 1)
                                    v.tensor_tensor(hq[:, tt * I1 + n0: tt * I1 + n0 + nw],
                                                    psum[i % 4][:, 0:nw],
                                                    gsil[:, tt * I1 + n0: tt * I1 + n0 + nw],
                                                    ALU.mult)
                                    self.chain(v, "vd", 1)
                                    i += 1
                                n0 += nw
                self.cnt["td"] = t0 + 24
            for tt in range(4):
                for (qo, cw) in ((0, 1024), (1024, 1024), (2048, 768)):
                    with self.blk() as b:
                        @b.vector
                        def _(v):
                            v = self.ser(v)
                            v.tensor_copy(hrow[:, :cw], hq[:, tt * I1 + qo:tt * I1 + qo + cw])
                    self.emit_quant(hrow[:, :cw], hq[:, tt * I1 + qo:tt * I1 + qo + cw], cw, qscr2)
            self.xbar_blocks(hqbt, hq)
            with self.blk() as b:
                @b.gpsimd
                def _(g):
                    g = self.ser(g)
                    g.dma_start(out=wqbt.rearrange("p (db k) -> p db k", db=8),
                                in_=self.wq_d["down"].rearrange("(db p) k -> p db k", p=128)
                                ).then_inc(sem["dmag"], 16)
                    self.bump("dmag", 16)
                    g.wait_ge(sem["dmag"], self.cnt["dmag"])
            t0 = self.cnt["td"]
            v0 = self.cnt["vd"]
            dp0 = [self.cnt["dma0"], self.cnt["dma1"]]
            with self.blk() as b:
                @b.tensor
                def _(t):
                    i = 0
                    for tt in range(4):
                        for dh in range(2):
                            if i >= 4:
                                t.wait_ge(sem["vd"], v0 + i - 3)
                            r = None
                            for fb in range(22):
                                lhsT = hqbt[:, (tt * 22 + fb) * 128:(tt * 22 + fb + 1) * 128]
                                wv = wqbt.rearrange("p (db fb f) -> p db fb f", db=8, f=128)
                                rhs = wv[:, dh * 4:(dh + 1) * 4, fb:fb + 1, :].squeeze(2)
                                r = t.matmul(psum[i % 4][:, :], lhsT, rhs,
                                             start=(fb == 0), stop=(fb == 21))
                            r.then_inc(sem["td"], 1)
                            i += 1

                @b.vector
                def _(v):
                    v = self.ser(v)
                    i = 0
                    for tt in range(4):
                        bb = 0 if tg < 2 else 1
                        for dh in range(2):
                            v.wait_ge(sem["td"], t0 + i + 1)
                            if i >= 2:
                                v.wait_ge(sem["dma%d" % (i % 2)],
                                          dp0[i % 2] + 16 * ((i - 2) // 2 + 1))
                            gb = self.bc[:, bb * D + dh * 512: bb * D + (dh + 1) * 512]
                            v.tensor_tensor(eslb[i % 2], psum[i % 4][:, :], gb, ALU.mult)
                            self.chain(v, "vd", 1)
                            i += 1

                @b.gpsimd
                def _(g):
                    i = 0
                    for tt in range(4):
                        gt = tg * 4 + tt
                        for dh in range(2):
                            g.wait_ge(sem["vd"], v0 + i + 1)
                            g.dma_start(out=self.spill[gt * 128:(gt + 1) * 128, dh * 512:(dh + 1) * 512],
                                        in_=eslb[i % 2], accum_op=ALU.add
                                        ).then_inc(sem["dma%d" % (i % 2)], 16)
                            self.bump("dma%d" % (i % 2), 16)
                            i += 1
                    g.wait_ge(sem["dma0"], self.cnt["dma0"])
                    g.wait_ge(sem["dma1"], self.cnt["dma1"])
            self.cnt["td"] = t0 + 8
        A.pop()
        ph.close()

    def finish(self):
        sem = self.sem
        with self.blk() as b:
            @b.sync
            def _(s):
                s.dma_start(out=self.x_sb[:].rearrange("p (t d) -> p t d", d=D),
                            in_=self.spill.rearrange("(t p) d -> p t d", p=128)).then_inc(sem["dmas"], 16)
                self.bump("dmas", 16)
                s.wait_ge(sem["dmas"], self.cnt["dmas"])
                s.dma_start(out=self.y_out.rearrange("(t p) d -> p t d", p=128),
                            in_=self.x_sb[:].rearrange("p (t d) -> p t d", d=D)).then_inc(sem["dmas"], 16)
                self.bump("dmas", 16)
                s.wait_ge(sem["dmas"], self.cnt["dmas"])


def build_nc(debug_taps=()):
    return MK(debug_taps).nc


def kernel(**inputs):
    maps = host_stage(inputs)
    mk = MK()
    res = run_bass_kernel_spmd(mk.nc, maps, core_ids=list(range(NCORE)))
    return host_gather(res.results)
